# version 12
# baseline (speedup 1.0000x reference)
import numpy as np
import concourse.bass as bass
import concourse.mybir as mybir
from concourse.bass_utils import run_bass_kernel_spmd

F32 = mybir.dt.float32
BF16 = mybir.dt.bfloat16
AF = mybir.ActivationFunctionType
ALU = mybir.AluOpType
AX = mybir.AxisListType

ENGS = ("sync", "scalar", "vector", "tensor", "gpsimd")

D = 2048
T = 2048
NCHUNK = 512
NT = T // NCHUNK
KT = D // 128


class PB:

    def __init__(self, nc):
        self.nc = nc
        self.items = {e: [] for e in ENGS}
        self.counts = {}
        self._sem_handles = {}
        self._ctxs = []
        self.res = {}
        self.sat = {e: {} for e in ENGS}
        self.n_ops = 0
        self._pool_i = {}

    def sem(self, name):
        if name not in self._sem_handles:
            cm = self.nc.semaphore(name)
            h = cm.__enter__()
            self._ctxs.append(cm)
            self._sem_handles[name] = h
            self.counts[name] = 0
        return self._sem_handles[name]

    def _wait(self, eng, sem_name, val):
        if val <= 0:
            return
        if self.sat[eng].get(sem_name, 0) >= val:
            return
        h = self.sem(sem_name)
        self.items[eng].append(lambda e, h=h, v=val: e.wait_ge(h, v))
        self.sat[eng][sem_name] = val

    def _deps(self, eng, reads, writes):
        qe = "q_" + eng
        for k in reads:
            r = self.res.get(k)
            if r and r["w"]:
                if r.get("wide") and r["w"][0] == qe:
                    continue
                self._wait(eng, *r["w"])
        for k in writes:
            r = self.res.get(k)
            if r:
                if r["w"] and not (r.get("wide") and r["w"][0] == qe):
                    self._wait(eng, *r["w"])
                for s, v in r["r"]:
                    if s == qe:
                        continue
                    self._wait(eng, s, v)

    def _notes(self, reads, writes, sem, val, wide=False):
        for k in reads:
            self.res.setdefault(k, {"w": None, "r": []})["r"].append((sem, val))
        for k in writes:
            self.res[k] = {"w": (sem, val), "r": [], "wide": wide}

    def op(self, eng, fn, reads=(), writes=(), wide=False):
        sem = "q_" + eng
        self._deps(eng, reads, writes)
        h = self.sem(sem)
        self.counts[sem] += 1
        val = self.counts[sem]
        self.items[eng].append(lambda e, h=h: fn(e).then_inc(h, 1))
        self._notes(reads, writes, sem, val, wide=wide)
        self.n_ops += 1
        return sem, val

    def dma(self, eng, out, in_, reads=(), writes=(), transpose=False,
            sem=None, **kw):
        if sem is None:
            i = self._pool_i.get(eng, 0)
            self._pool_i[eng] = i + 1
            sem = f"dp_{eng}_{i % 10}"
        self._deps(eng, reads, writes)
        h = self.sem(sem)
        self._wait(eng, sem, self.counts[sem])
        self.counts[sem] += 16
        val = self.counts[sem]
        if transpose:
            self.items[eng].append(
                lambda e, h=h: e.dma_start_transpose(out, in_).then_inc(h, 16))
        else:
            self.items[eng].append(
                lambda e, h=h: e.dma_start(out=out, in_=in_, **kw)
                .then_inc(h, 16))
        self._notes(reads, writes, sem, val)
        self.n_ops += 1
        return sem, val

    def barrier(self):
        snapshot = dict(self.counts)
        for e in ENGS:
            for s, v in snapshot.items():
                self._wait(e, s, v)
        self.res = {}

    def build(self):
        block_cm = self.nc.Block()
        block = block_cm.__enter__()
        for e in ENGS:
            ops = self.items[e]
            if not ops:
                continue

            def make(ops):
                def run(eng):
                    for f in ops:
                        f(eng)
                return run
            getattr(block, e)(make(ops))
        block_cm.__exit__(None, None, None)
        for cm in reversed(self._ctxs):
            cm.__exit__(None, None, None)


class Ctx:

    def __init__(self, dbg=()):
        self.nc = bass.Bass("TRN2", target_bir_lowering=False, debug=False)
        self.pb = PB(self.nc)
        self.dbg = set(dbg)
        self.dram = {}
        self.sbuf = {}
        self._sb_ctxs = []
        self.wcache = None
        self.plane_keys = {}

    def din(self, name, shape, dtype=F32):
        ap = self.nc.dram_tensor(name, list(shape), dtype,
                                 kind="ExternalInput").ap()
        self.dram[name] = ap
        return ap

    def dout(self, name, shape, dtype=F32):
        ap = self.nc.dram_tensor(name, list(shape), dtype,
                                 kind="ExternalOutput").ap()
        self.dram[name] = ap
        return ap

    def scratch(self, name, shape, dtype):
        kind = "ExternalOutput" if name in self.dbg else "Internal"
        ap = self.nc.dram_tensor(name, list(shape), dtype, kind=kind).ap()
        self.dram[name] = ap
        return ap

    def sb(self, name, shape, dtype):
        if name in self.sbuf:
            return self.sbuf[name]
        cm = self.nc.sbuf_tensor(name, list(shape), dtype)
        h = cm.__enter__()
        self._sb_ctxs.append(cm)
        self.sbuf[name] = h
        return h

    def ps(self, name, shape=(128, NCHUNK), dtype=F32):
        if name in self.sbuf:
            return self.sbuf[name]
        cm = self.nc.psum_tensor(name, list(shape), dtype)
        h = cm.__enter__()
        self._sb_ctxs.append(cm)
        self.sbuf[name] = h
        return h

    def wkey(self, plane, i):
        k = f"{plane}@{i}"
        ks = self.plane_keys.setdefault(plane, [])
        if k not in ks:
            ks.append(k)
        return k

    def rkeys(self, plane):
        return tuple(self.plane_keys.get(plane, ()))

    def finish(self):
        self.pb.build()
        for cm in reversed(self._sb_ctxs):
            cm.__exit__(None, None, None)
        return self.nc


def get_pool(ctx):
    return dict(
        wbuf=ctx.sb("wbuf", [128, KT * D], BF16),
        xbuf=ctx.sb("xbuf", [128, KT * T], BF16),
        fs0=ctx.sb("fs0", [128, T], F32),
        fs1=ctx.sb("fs1", [128, T], F32),
        fs2=ctx.sb("fs2", [128, T], F32),
        bs0=ctx.sb("bs0", [128, T], BF16),
    )


def FK(buf):
    return tuple(f"{buf}_{j}" for j in range(NT))


def emit_ln(ctx, x_tm, xn_tm, delta_tm=None, x_out_tm=None,
            x_name="x", xn_name="xn_tm", delta_name=None, xo_name=None,
            xnf_tm=None, xnf_name=None):
    pb = ctx.pb
    fs = [ctx.sb(f"fs{i}", [128, T], F32) for i in range(5)]
    bs = [ctx.sb(f"bs{i}", [128, T], BF16) for i in range(2)]
    xt = [fs[0], fs[1]]
    dl = [fs[3], fs[4]]
    xnt = [bs[0], bs[1]]
    sq = fs[2]
    mean = ctx.sb("ln_mean", [128, 8], F32)
    ssq = ctx.sb("ln_ssq", [128, 8], F32)
    var = ctx.sb("ln_var", [128, 8], F32)
    rstd = ctx.sb("ln_rstd", [128, 8], F32)
    for t in range(T // 128):
        i = t % 2
        r = slice(t * 128, (t + 1) * 128)
        kx, kn = FK(f"fs{i}"), FK(f"bs{i}")
        kd = FK(f"fs{3 + i}")
        pb.dma("sync", xt[i][:, :], x_tm[r, :], writes=kx,
               reads=ctx.rkeys(x_name))
        if delta_tm is not None:
            pb.dma("scalar", dl[i][:, :], delta_tm[r, :], writes=kd,
                   reads=ctx.rkeys(delta_name))
            pb.op("vector", lambda e, i=i: e.tensor_add(
                xt[i][:, :], xt[i][:, :], dl[i][:, :]),
                reads=kx + kd, writes=kx)
            if x_out_tm is not None:
                pb.dma("sync", x_out_tm[r, :], xt[i][:, :],
                       reads=kx, writes=(ctx.wkey(xo_name, t),))
        pb.op("vector", lambda e, i=i: e.reduce_sum(
            mean[:, 0:1], xt[i][:, :], axis=AX.X), reads=kx, writes=("ln_m",))
        pb.op("vector", lambda e: e.tensor_scalar_mul(
            mean[:, 0:1], mean[:, 0:1], 1.0 / D), reads=("ln_m",),
            writes=("ln_m",))
        pb.op("scalar", lambda e, i=i: e.activation(
            sq[:, :], xt[i][:, :], AF.Square, accum_out=ssq[:, 0:1]),
            reads=kx, writes=FK("fs2") + ("ln_ss",))
        pb.op("vector", lambda e: e.tensor_mul(
            var[:, 0:1], mean[:, 0:1], mean[:, 0:1]), reads=("ln_m",),
            writes=("ln_v",))
        pb.op("vector", lambda e: e.scalar_tensor_tensor(
            var[:, 0:1], ssq[:, 0:1], 1.0 / D, var[:, 0:1],
            ALU.mult, ALU.subtract),
            reads=("ln_ss", "ln_v"), writes=("ln_v",))
        pb.op("vector", lambda e: e.tensor_scalar_add(
            var[:, 0:1], var[:, 0:1], 1e-5), reads=("ln_v",), writes=("ln_v",))
        pb.op("scalar", lambda e: e.activation(
            rstd[:, 0:1], var[:, 0:1], AF.Sqrt),
            reads=("ln_v",), writes=("ln_r",))
        pb.op("vector", lambda e: e.reciprocal(rstd[:, 0:1], rstd[:, 0:1]),
              reads=("ln_r",), writes=("ln_r",))
        pb.op("vector", lambda e, i=i: e.tensor_scalar(
            xnt[i][:, :], xt[i][:, :], mean[:, 0:1], rstd[:, 0:1],
            ALU.subtract, ALU.mult),
            reads=kx + ("ln_m", "ln_r"), writes=kn)
        if xnf_tm is not None:
            pb.op("vector", lambda e, i=i: e.tensor_scalar(
                sq[:, :], xt[i][:, :], mean[:, 0:1], rstd[:, 0:1],
                ALU.subtract, ALU.mult),
                reads=kx + ("ln_m", "ln_r"), writes=FK("fs2"))
            pb.dma("scalar", xnf_tm[r, :], sq[:, :], reads=FK("fs2"),
                   writes=(ctx.wkey(xnf_name, t),))
        pb.dma("sync", xn_tm[r, :], xnt[i][:, :], reads=kn,
               writes=(ctx.wkey(xn_name, t),))


def emit_load_fm(ctx, dst_sb, dst_key, src_tm, src_name, n_dtiles=KT):
    pb = ctx.pb
    reads = ctx.rkeys(src_name)
    for k in range(n_dtiles):
        pb.dma("sync", dst_sb[:, k * T:(k + 1) * T],
               src_tm[:, k * 128:(k + 1) * 128], transpose=True,
               reads=reads, writes=(f"{dst_key}{k}",))


def emit_load_plane(ctx, dst_sb, dst_key, src_fm, src_name, n_dtiles=KT,
                    q="sync", key_fn=None):
    pb = ctx.pb
    for k in range(n_dtiles):
        rk = key_fn(k) if key_fn else (ctx.rkeys(src_name) if src_name else ())
        pb.dma(q, dst_sb[:, k * T:(k + 1) * T],
               src_fm[k * 128:(k + 1) * 128, :],
               reads=rk, writes=(f"{dst_key}{k}",))


def emit_linear_fm(ctx, wname, W_dram, x_sb, x_key, out_bf=None, out_f32=None,
                   Din=D, Dout=D, bias_sb=None, bias_key=None, bias_off=0,
                   act=None, scale=None, out_name=None, outf_name=None,
                   mt_range=None, w_transpose=False):
    pb = ctx.pb
    kt = Din // 128
    mt = Dout // 128
    assert kt * Dout <= KT * D
    wbuf = ctx.sb("wbuf", [128, KT * D], BF16)
    ev = ctx.sb("fs0", [128, T], F32)
    evb = ctx.sb("bs0", [128, T], BF16)
    kev, kevb = FK("fs0"), FK("bs0")
    ps = [ctx.ps(f"ps{i}") for i in range(7)]
    if ctx.wcache != wname:
        for k in range(kt):
            if w_transpose:
                pb.dma("sync", wbuf[:, k * Dout:(k + 1) * Dout],
                       W_dram[:, k * 128:(k + 1) * 128], transpose=True,
                       writes=(f"wbuf{k}",), reads=ctx.rkeys(wname))
            else:
                pb.dma("gpsimd", wbuf[:, k * Dout:(k + 1) * Dout],
                       W_dram[k * 128:(k + 1) * 128, :], writes=(f"wbuf{k}",),
                       reads=ctx.rkeys(wname))
        ctx.wcache = wname
    for m in (range(mt) if mt_range is None else mt_range):
        pgi = list(range(NT))
        for n in range(NT):
            def mmgroup(e, m=m, n=n):
                last = None
                for k in range(kt):
                    last = e.matmul(
                        ps[pgi[n]][:, :],
                        wbuf[:, k * Dout + m * 128:k * Dout + (m + 1) * 128],
                        x_sb[:, k * T + n * NCHUNK:k * T + (n + 1) * NCHUNK],
                        start=(k == 0), stop=(k == kt - 1))
                return last
            pb.op("tensor", mmgroup,
                  reads=tuple(f"wbuf{k}" for k in range(kt)) +
                  tuple(f"{x_key}{k}" for k in range(kt)),
                  writes=(f"ps{pgi[n]}",))
        dst, dst_key = (evb, "bs0") if out_f32 is None else (ev, "fs0")
        for n in range(NT):
            ns = slice(n * NCHUNK, (n + 1) * NCHUNK)
            rd = (f"ps{pgi[n]}",) + ((bias_key,) if bias_key else ())
            wkk = (kevb if dst is evb else kev)[n:n + 1]
            if act is None and bias_sb is not None:
                def ev_fn(e, m=m, n=n, ns=ns):
                    return e.tensor_scalar(
                        dst[:, ns], ps[pgi[n]][:, :],
                        bias_sb[:, bias_off + m:bias_off + m + 1], None,
                        ALU.add)
                pb.op("vector", ev_fn, reads=rd, writes=wkk)
            else:
                def ev_fn(e, n=n, ns=ns):
                    kw = {}
                    if bias_sb is not None:
                        kw["bias"] = bias_sb[:, bias_off + m:bias_off + m + 1]
                    if scale is not None:
                        kw["scale"] = scale
                    return e.activation(dst[:, ns], ps[pgi[n]][:, :],
                                        act if act is not None else AF.Copy,
                                        **kw)
                pb.op("scalar", ev_fn, reads=rd, writes=wkk)
        mr = slice(m * 128, (m + 1) * 128)
        if out_bf is not None and out_f32 is not None:
            pb.op("vector", lambda e: e.tensor_copy(evb[:, :], ev[:, :]),
                  reads=kev, writes=kevb)
        if out_bf is not None:
            pb.dma("sync", out_bf[mr, :], evb[:, :], reads=kevb,
                   writes=(ctx.wkey(out_name, m),) if out_name else ())
        if out_f32 is not None:
            pb.dma("sync", out_f32[mr, :], ev[:, :], reads=kev,
                   writes=(ctx.wkey(outf_name, m),) if outf_name else ())


def emit_load_bias(ctx, sb, sb_key, b_dram, nt, col_off=0):
    pb = ctx.pb
    b2 = b_dram.rearrange("(m p) -> m p", p=128)
    for m in range(nt):
        pb.dma("sync", sb[:, col_off + m:col_off + m + 1],
               b2[m:m + 1, :].transpose([1, 0]), writes=(sb_key,))


NCONST = 16


def const_col(ctx, value):
    cst = ctx.sb("cst", [128, NCONST], F32)
    if not hasattr(ctx, "const_values"):
        ctx.const_values = []
        cin = ctx.din("cst", [128, NCONST])
        ctx.pb.dma("sync", cst[:, :], cin[:, :], writes=("cst",))
    if value not in ctx.const_values:
        assert len(ctx.const_values) < NCONST
        ctx.const_values.append(value)
    return cst[:, ctx.const_values.index(value):
               ctx.const_values.index(value) + 1]


def const_input(ctx):
    v = getattr(ctx, "const_values", [])
    arr = np.zeros((128, NCONST), np.float32)
    for i, x in enumerate(v):
        arr[:, i] = x
    return arr


def emit_stream_elem(ctx, ins, in_names, outs_f32, outs_bf, n_tmp, body,
                     d_tiles=KT, in_keys=None):
    pb = ctx.pb
    nin, nof, nob = len(ins), len(outs_f32), len(outs_bf)
    fs = [ctx.sb(f"fs{i}", [128, T], F32) for i in range(6)]
    bs = [ctx.sb(f"bs{i}", [128, T], BF16) for i in range(2)]
    slots = [(fs[i], i, j) for i in range(6) for j in range(NT)]
    per = nin + nof + n_tmp
    assert 2 * per <= len(slots), (nin, nof, n_tmp)
    bslots = [(bs[i], i, j) for i in range(2) for j in range(NT)]
    assert 2 * nob <= len(bslots)

    def fkey(s):
        return f"fs{s[1]}_{s[2]}"

    def bkey(s):
        return f"bs{s[1]}_{s[2]}"

    def ap(s, w=NCHUNK):
        return s[0][:, s[2] * NCHUNK:s[2] * NCHUNK + w]

    ci = 0
    for d in range(d_tiles):
        rr = slice(d * 128, (d + 1) * 128)
        for c in range(NT):
            cc = slice(c * NCHUNK, (c + 1) * NCHUNK)
            par = ci % 2
            isl = slots[par * per:par * per + nin]
            osl = slots[par * per + nin:par * per + nin + nof]
            tsl = slots[par * per + nin + nof:par * per + per]
            obl = bslots[par * nob:(par + 1) * nob]
            for i, pl in enumerate(ins):
                q = "sync" if i % 2 == 0 else "scalar"
                rk = (in_keys[i](d, c, ci) if in_keys and in_keys[i]
                      else ctx.rkeys(in_names[i]))
                pb.dma(q, ap(isl[i]), pl[rr, cc],
                       reads=rk, writes=(fkey(isl[i]),))
            body(pb, [ap(s) for s in isl], [ap(s) for s in osl],
                 [ap(s) for s in tsl], [ap(s) for s in obl],
                 [fkey(s) for s in isl], [fkey(s) for s in osl],
                 [fkey(s) for s in tsl], [bkey(s) for s in obl],
                 (d, c, ci, "vector"))
            for i, (pl, nm) in enumerate(outs_f32):
                pb.dma("sync", pl[rr, cc], ap(osl[i]), reads=(fkey(osl[i]),),
                       writes=(ctx.wkey(nm, ci),))
            for i, (pl, nm) in enumerate(outs_bf):
                pb.dma("scalar", pl[rr, cc], ap(obl[i]), reads=(bkey(obl[i]),),
                       writes=(ctx.wkey(nm, ci),))
            ci += 1


def emit_l0_elemA(ctx, FQr, FQi, FKr, FKi, FVr, FVi, FXr, FXi, XHr, XHi):
    pb = ctx.pb
    carry = ctx.sb("carryRI", [128, 8], F32)

    def body(pb, ib, ob, tb, bb, ik, ok, tk, bk, info):
        d, c, ci, eng = info
        qr, qi, kr, ki, vr, vi, xr, xi = ib
        xhr, xhi = ob
        t0, t1 = tb
        kqr, kqi, kkr, kki, kvr, kvi, kxr, kxi = ik
        kxhr, kxhi = ok
        kt0, kt1 = tk
        def o(eng, fn, reads=(), writes=(), wide=True):
            pb.op(eng, fn, reads=reads, writes=writes, wide=wide)
        o(eng, lambda e: e.tensor_mul(t0, qr, kr), reads=(kqr, kkr), writes=(kt0,))
        o(eng, lambda e: e.tensor_mul(t1, qi, ki), reads=(kqi, kki), writes=(kt1,), wide=True)
        o(eng, lambda e: e.tensor_add(t0, t0, t1), reads=(kt0, kt1), writes=(kt0,))
        o(eng, lambda e: e.tensor_mul(t1, qr, ki), reads=(kqr, kki), writes=(kt1,), wide=True)
        o(eng, lambda e: e.tensor_mul(xhr, qi, kr), reads=(kqi, kkr), writes=(kxhr,))
        o(eng, lambda e: e.tensor_sub(t1, t1, xhr), reads=(kt1, kxhr), writes=(kt1,), wide=True)
        o(eng, lambda e: e.tensor_mul(xhr, t0, vr), reads=(kt0, kvr), writes=(kxhr,))
        o(eng, lambda e: e.tensor_mul(xhi, t1, vi), reads=(kt1, kvi), writes=(kxhi,), wide=True)
        o(eng, lambda e: e.tensor_sub(xhr, xhr, xhi), reads=(kxhr, kxhi), writes=(kxhr,))
        o(eng, lambda e: e.tensor_mul(t0, t0, vi), reads=(kt0, kvi), writes=(kt0,), wide=True)
        o(eng, lambda e: e.tensor_mul(xhi, t1, vr), reads=(kt1, kvr), writes=(kxhi,))
        o(eng, lambda e: e.tensor_add(t0, t0, xhi), reads=(kt0, kxhi), writes=(kt0,), wide=True)
        if c == 0:
            o("vector", lambda e: e.tensor_tensor_scan(
                xhr, xhr, xhr, 0.0, ALU.add, ALU.bypass), reads=(kxhr,), writes=(kxhr,), wide=False)
            o("vector", lambda e: e.tensor_tensor_scan(
                t0, t0, t0, 0.0, ALU.add, ALU.bypass), reads=(kt0,), writes=(kt0,), wide=False)
        else:
            o("vector", lambda e: e.tensor_tensor_scan(
                xhr, xhr, xhr, carry[:, 0:1], ALU.add, ALU.bypass),
                reads=(kxhr, "carry"), writes=(kxhr,), wide=False)
            o("vector", lambda e: e.tensor_tensor_scan(
                t0, t0, t0, carry[:, 1:2], ALU.add, ALU.bypass),
                reads=(kt0, "carry"), writes=(kt0,), wide=False)
        o("vector", lambda e: e.tensor_copy(carry[:, 0:1], xhr[:, NCHUNK - 1:NCHUNK]),
          reads=(kxhr,), writes=("carry",), wide=False)
        o("vector", lambda e: e.tensor_copy(carry[:, 1:2], t0[:, NCHUNK - 1:NCHUNK]),
          reads=(kt0,), writes=("carry",), wide=False)
        o("vector", lambda e: e.tensor_add(xhr, xr, xhr), reads=(kxr, kxhr), writes=(kxhr,))
        o("vector", lambda e: e.tensor_add(xhi, xi, t0), reads=(kxi, kt0), writes=(kxhi,), wide=True)

    nm = ["FQr", "FQi", "FKr", "FKi", "FVr", "FVi", "FXr", "FXi"]
    emit_stream_elem(ctx, [FQr, FQi, FKr, FKi, FVr, FVi, FXr, FXi], nm,
                     [(XHr, "XHr"), (XHi, "XHi")], [], 2, body, d_tiles=MH,
                     in_keys=[(lambda d, c, ci, n=n: (f"{n}@{d}",))
                              for n in nm])


def emit_l0_elemB(ctx, FSr, FSi, XHr, XHi, G, TAr, TAi):
    pb = ctx.pb

    def body(pb, ib, ob, tb, bb, ik, ok, tk, bk, info):
        eng = info[3]
        sr, si, xhr, xhi = ib
        (g,) = ob
        t0, t1 = tb
        tar, tai = bb
        ksr, ksi, kxhr, kxhi = ik
        (kg,) = ok
        kt0, kt1 = tk
        ktar, ktai = bk
        def o(eng, fn, reads=(), writes=(), wide=True):
            pb.op(eng, fn, reads=reads, writes=writes, wide=wide)
        o(eng, lambda e: e.tensor_mul(t0, sr, xhr), reads=(ksr, kxhr), writes=(kt0,))
        o(eng, lambda e: e.tensor_mul(t1, si, xhi), reads=(ksi, kxhi), writes=(kt1,), wide=True)
        o(eng, lambda e: e.tensor_add(tar, t0, t1), reads=(kt0, kt1), writes=(ktar,), wide=True)
        o(eng, lambda e: e.tensor_mul(t0, sr, xhi), reads=(ksr, kxhi), writes=(kt0,))
        o(eng, lambda e: e.tensor_mul(t1, si, xhr), reads=(ksi, kxhr), writes=(kt1,), wide=True)
        o(eng, lambda e: e.tensor_sub(tai, t0, t1), reads=(kt0, kt1), writes=(ktai,), wide=True)
        o(eng, lambda e: e.tensor_mul(t0, sr, sr), reads=(ksr,), writes=(kt0,))
        o(eng, lambda e: e.tensor_mul(t1, si, si), reads=(ksi,), writes=(kt1,), wide=True)
        o(eng, lambda e: e.tensor_add(g, t0, t1), reads=(kt0, kt1), writes=(kg,), wide=True)

    emit_stream_elem(ctx, [FSr, FSi, XHr, XHi], ["FSr", "FSi", "XHr", "XHi"],
                     [(G, "G")], [(TAr, "TAr"), (TAi, "TAi")], 2, body,
                     d_tiles=MH,
                     in_keys=[lambda d, c, ci: (f"FSr@{d}",),
                              lambda d, c, ci: (f"FSi@{d}",),
                              lambda d, c, ci: (f"XHr@{ci}",),
                              lambda d, c, ci: (f"XHi@{ci}",)])


def dft_constants():
    d = np.arange(D, dtype=np.float64)
    ang = np.outer(d, d) * (2.0 * np.pi / D)
    return np.cos(ang).astype(np.float32), (-np.sin(ang)).astype(np.float32)


MH = 9


def emit_mirror(ctx, plane, name, neg, src_bf=False):
    pb = ctx.pb
    pool = get_pool(ctx)
    xbuf = pool["xbuf"]
    p1 = ctx.sb("p1m", [128, 128], BF16)
    identb = ctx.sb("identb", [128, 128], BF16)
    if getattr(ctx, "_p1_loaded", False) is False:
        pb.dma("gpsimd", p1[:, :], ctx.dram["P1m"][:, :], writes=("p1m",))
        pb.dma("gpsimd", identb[:, :], ctx.dram["ident"][:, :],
               writes=("identb",))
        ctx._p1_loaded = True
    ps = [ctx.ps(f"ps{i}") for i in range(7)]
    if src_bf:
        ev = ctx.sb("bs0", [128, T], BF16)
        kev = FK("bs0")
    else:
        ev = ctx.sb("fs0", [128, T], F32)
        kev = FK("fs0")
    q = "sync" if src_bf else "gpsimd"
    for k in range(8):
        pb.dma(q, xbuf[:, k * T:(k + 1) * T], plane[k * 128:(k + 1) * 128, :],
               reads=ctx.rkeys(name), writes=(f"xbuf{k}",))
    for j in range(7):
        a, b = 6 - j, 7 - j
        for n in range(NT):
            ns = slice(n * NCHUNK, (n + 1) * NCHUNK)
            def mgroup(e, j=j, a=a, b=b, n=n):
                e.matmul(ps[n][:, :], p1[:, :],
                         xbuf[:, a * T + n * NCHUNK:a * T + (n + 1) * NCHUNK],
                         start=True, stop=False)
                return e.matmul(
                    ps[n][:, :], identb[0:1, :],
                    xbuf[0:1, b * T + n * NCHUNK:b * T + (n + 1) * NCHUNK],
                    start=False, stop=True)
            pb.op("tensor", mgroup,
                  reads=(f"xbuf{a}", f"xbuf{b}", "p1m", "identb"),
                  writes=(f"ps{n}",))
            pb.op("vector", lambda e, n=n, ns=ns: e.tensor_scalar(
                ev[:, ns], ps[n][:, :], -1.0 if neg else 1.0, None, ALU.mult),
                reads=(f"ps{n}",), writes=(kev[n],))
        pb.dma("sync", plane[(9 + j) * 128:(10 + j) * 128, :], ev[:, :],
               reads=kev, writes=(ctx.wkey(name, f"M{9 + j}"),))


def emit_fft_pass(ctx, wnm, pl, suffix, srcname):
    pb = ctx.pb
    pool = get_pool(ctx)
    xbuf = pool["xbuf"]
    out = ctx.scratch(pl + suffix, [D, T], F32)
    if srcname is None:
        emit_load_fm(ctx, xbuf, "xbuf", ctx.dram["xn_tm"], "xn_tm")
    else:
        emit_load_plane(ctx, xbuf, "xbuf", ctx.dram[srcname], srcname)
    emit_linear_fm(ctx, wnm, ctx.dram[wnm], xbuf, "xbuf", out_f32=out,
                   outf_name=pl + suffix, mt_range=range(MH))


def emit_l0_front(ctx, x_tm):
    pb = ctx.pb
    pool = get_pool(ctx)
    xbuf = pool["xbuf"]
    xn_tm = ctx.scratch("xn_tm", [T, D], BF16)
    emit_ln(ctx, x_tm, xn_tm)
    planes = {}
    for nm in ("q0", "k0", "v0", "s0", "t0"):
        planes[nm] = ctx.scratch(nm, [D, T], BF16)
    emit_load_fm(ctx, xbuf, "xbuf", xn_tm, "xn_tm")
    for nm, wn in (("q0", "L0_attn_Wq"), ("k0", "L0_attn_Wk"),
                   ("v0", "L0_attn_Wv"), ("s0", "L0_attn_Ws"),
                   ("t0", "L0_attn_Wt")):
        emit_linear_fm(ctx, wn, ctx.dram[wn], xbuf, "xbuf",
                       out_bf=planes[nm], out_name=nm)


def emit_load_plane_w(ctx, dst_sb, dst_key, src_fm, src_name, ktiles,
                      tok0, w, k0=0, dst_slot0=0, q="sync"):
    pb = ctx.pb
    for k in range(ktiles):
        pb.dma(q, dst_sb[:, (dst_slot0 + k) * w:(dst_slot0 + k + 1) * w],
               src_fm[(k0 + k) * 128:(k0 + k + 1) * 128, tok0:tok0 + w],
               reads=ctx.rkeys(src_name), writes=(f"{dst_key}{dst_slot0 + k}",))


def emit_linear_cols(ctx, wname, W_dram, rhs_fn, rhs_keys, Din, Dout,
                     out_specs, tok0=0, tokw=T, bias_sb=None, bias_key=None,
                     bias_off=0, act=None, scale=None):
    pb = ctx.pb
    kt = Din // 128
    mt = Dout // 128
    ntc = tokw // NCHUNK
    wcol = ctx.sb("wcol", [128, 8192], BF16)
    dbuf_w = kt <= 32
    ev = ctx.sb("fs0", [128, T], F32)
    evb = ctx.sb("bs0", [128, T], BF16)
    kev, kevb = FK("fs0"), FK("bs0")
    ps = [ctx.ps(f"ps{i}") for i in range(7)]
    for m in range(mt):
        half = (m % 2) if dbuf_w else 0
        woff = half * kt * 128
        wkeys = []
        for k0 in range(0, kt, 8):
            kn = min(8, kt - k0)
            src = W_dram[k0 * 128:(k0 + kn) * 128, m * 128:(m + 1) * 128]
            src3 = src.rearrange("(j p) m -> p j m", p=128)
            dst = wcol[:, woff + k0 * 128:woff + (k0 + kn) * 128]
            dst3 = dst.rearrange("p (j m) -> p j m", m=128)
            kk = f"wcol{half}_{k0}"
            pb.dma("gpsimd", dst3, src3, reads=ctx.rkeys(wname), writes=(kk,))
            wkeys.append(kk)
        pgi = list(range(ntc))
        for n in range(ntc):
            def mmgroup(e, m=m, n=n, woff=woff):
                last = None
                for k in range(kt):
                    last = e.matmul(
                        ps[pgi[n]][:, :],
                        wcol[:, woff + k * 128:woff + (k + 1) * 128],
                        rhs_fn(k, n),
                        start=(k == 0), stop=(k == kt - 1))
                return last
            pb.op("tensor", mmgroup,
                  reads=tuple(wkeys) + tuple(rhs_keys(n)),
                  writes=(f"ps{pgi[n]}",))
        only_bf = all(s[3] for s in out_specs)
        dst, dst_key = (evb, kevb) if only_bf else (ev, kev)
        for n in range(ntc):
            ns = slice(n * NCHUNK, (n + 1) * NCHUNK)
            rd = (f"ps{pgi[n]}",) + ((bias_key,) if bias_key else ())
            if act is None and bias_sb is not None:
                def ev_fn(e, m=m, n=n, ns=ns):
                    return e.tensor_scalar(
                        dst[:, ns], ps[pgi[n]][:, :],
                        bias_sb[:, bias_off + m:bias_off + m + 1], None,
                        ALU.add)
                pb.op("vector", ev_fn, reads=rd, writes=dst_key[n:n + 1])
            else:
                def ev_fn(e, m=m, n=n, ns=ns):
                    kw = {}
                    if bias_sb is not None:
                        kw["bias"] = bias_sb[:, bias_off + m:bias_off + m + 1]
                    if scale is not None:
                        kw["scale"] = scale
                    return e.activation(dst[:, ns], ps[pgi[n]][:, :],
                                        act if act is not None else AF.Copy,
                                        **kw)
                pb.op("scalar", ev_fn, reads=rd, writes=dst_key[n:n + 1])
        for (plane, pname, is_bf, m_lo, m_hi) in out_specs:
            if not (m_lo <= m < m_hi):
                continue
            mr = slice((m - m_lo) * 128, (m - m_lo + 1) * 128)
            src_b = dst if (is_bf == only_bf or is_bf) else ev
            if is_bf and not only_bf:
                pb.op("vector", lambda e: e.tensor_copy(
                    evb[:, :tokw], ev[:, :tokw]), reads=kev, writes=kevb)
                src_b = evb
            pb.dma("sync", plane[mr, tok0:tok0 + tokw], src_b[:, :tokw],
                   reads=(kevb if is_bf else kev),
                   writes=(ctx.wkey(pname, f"{m}_{tok0}"),))


def emit_l0_mlp(ctx):
    pb = ctx.pb
    pool = get_pool(ctx)
    xbuf, wbuf = pool["xbuf"], pool["wbuf"]
    bias = ctx.sb("bias", [128, 128], F32)
    emit_load_bias(ctx, bias, "bias", ctx.dram["L0_attn_b1"], 64, 0)
    emit_load_bias(ctx, bias, "bias", ctx.dram["L0_attn_b2"], 32, 64)
    h1 = ctx.scratch("h1", [8192, T], BF16)
    Hr = ctx.scratch("Hr", [D, T], F32)
    Hi = ctx.scratch("Hi", [D, T], F32)
    ctx.wcache = None
    emit_load_plane(ctx, xbuf, "xbuf", ctx.dram["TAr"], "TAr")
    emit_load_plane(ctx, wbuf, "wbuf", ctx.dram["TAi"], "TAi")

    def rhs1(k, n):
        buf = xbuf if k < KT else wbuf
        kk = k % KT
        return buf[:, kk * T + n * NCHUNK:kk * T + (n + 1) * NCHUNK]

    def rkeys1(n):
        return tuple(f"xbuf{k}" for k in range(KT)) + \
            tuple(f"wbuf{k}" for k in range(KT))

    emit_linear_cols(ctx, "L0_attn_W1", ctx.dram["L0_attn_W1"], rhs1, rkeys1,
                     2 * D, 4 * D, [(h1, "h1", True, 0, 64)],
                     bias_sb=bias, bias_key="bias", bias_off=0, act=AF.Relu)
    pb.barrier()
    HW = 1024
    for hi in range(2):
        t0 = hi * HW
        emit_load_plane_w(ctx, xbuf, "xbuf", h1, "h1", 32, t0, HW)
        emit_load_plane_w(ctx, wbuf, "wbuf", h1, "h1", 32, t0, HW, k0=32)

        def rhs2(k, n, t0=t0):
            buf = xbuf if k < 32 else wbuf
            kk = k % 32
            return buf[:, kk * HW + n * NCHUNK:kk * HW + (n + 1) * NCHUNK]

        def rkeys2(n):
            return tuple(f"xbuf{k}" for k in range(32)) + \
                tuple(f"wbuf{k}" for k in range(32))

        emit_linear_cols(ctx, "L0_attn_W2", ctx.dram["L0_attn_W2"], rhs2,
                         rkeys2, 4 * D, 2 * D,
                         [(Hr, "Hr", False, 0, 16), (Hi, "Hi", False, 16, 32)],
                         tok0=t0, tokw=HW,
                         bias_sb=bias, bias_key="bias", bias_off=64)


def emit_l0_combine(ctx, XHr, XHi, G, FTr, FTi, Hr, Hi, XHnR, XHnI):
    def body(pb, ib, ob, tb, bb, ik, ok, tk, bk, info):
        eng = info[3]
        xhr, xhi, g, ftr, fti, hr, hi = ib
        t0, t1, t2 = tb
        onr, oni = bb
        kxhr, kxhi, kg, kftr, kfti, khr, khi = ik
        kt0, kt1, kt2 = tk
        konr, koni = bk
        def o(eng, fn, reads=(), writes=(), wide=True):
            pb.op(eng, fn, reads=reads, writes=writes, wide=wide)
        o(eng, lambda e: e.tensor_mul(t0, ftr, hr), reads=(kftr, khr), writes=(kt0,))
        o(eng, lambda e: e.tensor_mul(t1, fti, hi), reads=(kfti, khi), writes=(kt1,), wide=True)
        o(eng, lambda e: e.tensor_sub(t0, t0, t1), reads=(kt0, kt1), writes=(kt0,))
        o(eng, lambda e: e.tensor_mul(t1, g, xhr), reads=(kg, kxhr), writes=(kt1,), wide=True)
        o(eng, lambda e: e.tensor_sub(t2, xhr, t1), reads=(kxhr, kt1), writes=(kt2,))
        o(eng, lambda e: e.tensor_add(onr, t2, t0), reads=(kt2, kt0), writes=(konr,), wide=True)
        o(eng, lambda e: e.tensor_mul(t0, ftr, hi), reads=(kftr, khi), writes=(kt0,))
        o(eng, lambda e: e.tensor_mul(t1, fti, hr), reads=(kfti, khr), writes=(kt1,), wide=True)
        o(eng, lambda e: e.tensor_add(t0, t0, t1), reads=(kt0, kt1), writes=(kt0,))
        o(eng, lambda e: e.tensor_mul(t1, g, xhi), reads=(kg, kxhi), writes=(kt1,), wide=True)
        o(eng, lambda e: e.tensor_sub(t2, xhi, t1), reads=(kxhi, kt1), writes=(kt2,))
        o(eng, lambda e: e.tensor_add(oni, t2, t0), reads=(kt2, kt0), writes=(koni,), wide=True)

    emit_stream_elem(ctx, [XHr, XHi, G, FTr, FTi, Hr, Hi],
                     ["XHr", "XHi", "G", "FTr", "FTi", "Hr", "Hi"],
                     [], [(XHnR, "XHnR"), (XHnI, "XHnI")], 3, body,
                     in_keys=[
                         lambda d, c, ci: (f"XHr@{ci}",) if d < MH
                         else (f"XHr@M{d}",),
                         lambda d, c, ci: (f"XHi@{ci}",) if d < MH
                         else (f"XHi@M{d}",),
                         lambda d, c, ci: (f"G@{ci}",) if d < MH
                         else (f"G@M{d}",),
                         lambda d, c, ci: (f"FTr@{d}",) if d < MH
                         else (f"FTr@M{d}",),
                         lambda d, c, ci: (f"FTi@{d}",) if d < MH
                         else (f"FTi@M{d}",),
                         lambda d, c, ci: (f"Hr@{d}_{0 if c < 2 else 1024}",),
                         lambda d, c, ci: (f"Hi@{d}_{0 if c < 2 else 1024}",)])


def emit_l0_moe(ctx, moe_out):
    pb = ctx.pb
    pool = get_pool(ctx)
    xbuf = pool["xbuf"]
    ps = [ctx.ps(f"ps{i}") for i in range(7)]
    pst = ctx.ps("pst", [128, 512], BF16)
    identb = ctx.sb("identb", [128, 128], BF16)
    pb.dma("gpsimd", identb[:, :], ctx.dram["ident"][:, :], writes=("identb",))
    bs0 = ctx.sb("bs0", [128, T], BF16)
    bs1 = ctx.sb("bs1", [128, T], BF16)
    rnat = bs0
    rT = ctx.sb("rT", [128, 8 * KT], BF16)
    pb.dma("gpsimd", rnat[0:8, :], ctx.dram["L0_ffn_router_w"][:, :],
           writes=FK("bs0"))
    for k in range(KT):
        pb.op("tensor", lambda e, k=k: e.transpose(
            pst[0:128, 0:8], rnat[0:8, k * 128:(k + 1) * 128],
            identb[0:8, 0:8]),
            reads=FK("bs0") + ("identb",), writes=("pst",))
        pb.op("vector", lambda e, k=k: e.tensor_copy(
            rT[:, 8 * k:8 * k + 8], pst[0:128, 0:8]),
            reads=("pst",), writes=("rT",))
    dnat = bs1
    dT = ctx.sb("dT", [128, KT * 128], BF16)
    dn_v = ctx.dram["L0_ffn_down_w"].rearrange("e i d -> (e i) d")
    pb.dma("gpsimd", dnat[:, :], dn_v, writes=FK("bs1"))
    for k in range(KT):
        pb.op("tensor", lambda e, k=k: e.transpose(
            pst[:, 0:128], dnat[:, k * 128:(k + 1) * 128], identb[:, :]),
            reads=FK("bs1") + ("identb",), writes=("pst",))
        pb.op("vector", lambda e, k=k: e.tensor_copy(
            dT[:, k * 128:(k + 1) * 128], pst[:, 0:128]),
            reads=("pst",), writes=("dT",))
    Eb = ctx.sb("Eb", [8, 128], BF16)
    pb.dma("gpsimd", Eb[:, :], ctx.dram["Emoe"][:, :], writes=("Eb",))
    g8 = bs1
    lgt = ctx.sb("lgt", [128, 8], F32)
    lm = ctx.sb("lm", [128, 8], F32)
    top8 = ctx.sb("ltop", [128, 8], F32)
    gsum = ctx.sb("gsum", [128, 8], F32)
    gw = ctx.sb("gw", [128, 8], F32)
    gb = ctx.sb("gb", [128, 8], BF16)
    for tc in range(T // 128):
        def lgroup(e, tc=tc):
            last = None
            for k in range(KT):
                last = e.matmul(ps[6][:, 0:8],
                                xbuf[:, k * T + tc * 128:k * T + (tc + 1) * 128],
                                rT[:, 8 * k:8 * k + 8],
                                start=(k == 0), stop=(k == KT - 1))
            return last
        pb.op("tensor", lgroup,
              reads=tuple(f"xbuf{k}" for k in range(KT)) + ("rT",),
              writes=("ps6",))
        pb.op("vector", lambda e: e.tensor_copy(lgt[:, :], ps[6][:, 0:8]),
              reads=("ps6",), writes=("lgt",))
        pb.op("vector", lambda e: e.reduce_max(lm[:, 0:1], lgt[:, :], axis=AX.X),
              reads=("lgt",), writes=("lm",))
        pb.op("vector", lambda e: e.tensor_scalar_mul(lm[:, 0:1], lm[:, 0:1], -1.0),
              reads=("lm",), writes=("lm",))
        pb.op("scalar", lambda e: e.activation(gw[:, :], lgt[:, :], AF.Exp,
                                               bias=lm[:, 0:1],
                                               accum_out=gsum[:, 0:1]),
              reads=("lgt", "lm"), writes=("gw", "gsum"))
        pb.op("vector", lambda e: e.reciprocal(gsum[:, 1:2], gsum[:, 0:1]),
              reads=("gsum",), writes=("gsum",))
        pb.op("vector", lambda e: e.tensor_scalar(gw[:, :], gw[:, :],
                                                  gsum[:, 1:2], None, ALU.mult),
              reads=("gw", "gsum"), writes=("gw",))
        pb.op("vector", lambda e: e.max(top8[:, :], gw[:, :]),
              reads=("gw",), writes=("ltop",))
        pb.op("vector", lambda e: e.tensor_scalar(lgt[:, :], gw[:, :],
                                                  top8[:, 1:2], None, ALU.is_ge),
              reads=("gw", "ltop"), writes=("lgt",))
        pb.op("vector", lambda e: e.tensor_mul(gb[:, :], gw[:, :], lgt[:, :]),
              reads=("gw", "lgt"), writes=("gb",))
        pb.op("tensor", lambda e, tc=tc: e.transpose(
            pst[0:8, 0:128], gb[:, :], identb[:, :]),
            reads=("gb", "identb"), writes=("pst",))
        pb.op("vector", lambda e, tc=tc: e.tensor_copy(
            g8[0:8, tc * 128:(tc + 1) * 128], pst[0:8, 0:128]),
            reads=("pst",), writes=(FK("bs1")[tc // 4],))
    gate128 = bs0
    kg128 = FK("bs0")
    for n in range(NT):
        ns = slice(n * NCHUNK, (n + 1) * NCHUNK)
        pb.op("tensor", lambda e, ns=ns: e.matmul(
            ps[5][:, :], Eb[:, :], g8[0:8, ns], start=True, stop=True),
            reads=("Eb",) + FK("bs1"), writes=("ps5",))
        pb.op("vector", lambda e, n=n, ns=ns: e.tensor_copy(
            gate128[:, ns], ps[5][:, :]), reads=("ps5",), writes=(kg128[n],))
    av = ctx.sb("fs4", [128, T], F32)
    ag = bs1
    kav, kag = FK("fs4"), FK("bs1")
    for n in range(NT):
        ns = slice(n * NCHUNK, (n + 1) * NCHUNK)
        def dgroup(e, n=n, ns=ns):
            last = None
            for k in range(KT):
                last = e.matmul(ps[n][:, :], dT[:, k * 128:(k + 1) * 128],
                                xbuf[:, k * T + n * NCHUNK:k * T + (n + 1) * NCHUNK],
                                start=(k == 0), stop=(k == KT - 1))
            return last
        pb.op("tensor", dgroup,
              reads=("dT",) + tuple(f"xbuf{k}" for k in range(KT)),
              writes=(f"ps{n}",))
        pb.op("scalar", lambda e, n=n, ns=ns: e.activation(
            av[:, ns], ps[n][:, :], AF.Silu),
            reads=(f"ps{n}",), writes=(kav[n],))
        pb.op("vector", lambda e, n=n, ns=ns: e.tensor_mul(
            ag[:, ns], av[:, ns], gate128[:, ns]),
            reads=(kav[n], kg128[n]), writes=(kag[n],))
    uT = dT
    up_v = ctx.dram["L0_ffn_up_w"].rearrange("e d i -> d e i")
    uv = bs0[:, 0:128]
    for k in range(KT):
        pb.dma("gpsimd", uv.rearrange("p (e i) -> p e i", i=16),
               up_v[k * 128:(k + 1) * 128, :, :], writes=("bs0_0",))
        pb.op("tensor", lambda e, k=k: e.transpose(
            pst[:, 0:128], uv[:, :], identb[:, :]),
            reads=("bs0_0", "identb"), writes=("pst",))
        pb.op("vector", lambda e, k=k: e.tensor_copy(
            uT[:, k * 128:(k + 1) * 128], pst[:, 0:128]),
            reads=("pst",), writes=("dT",))
    ev = ctx.sb("fs5", [128, T], F32)
    kev = FK("fs5")
    for m in range(KT):
        for n in range(NT):
            ns = slice(n * NCHUNK, (n + 1) * NCHUNK)
            pb.op("tensor", lambda e, m=m, n=n, ns=ns: e.matmul(
                ps[4 + (m % 2)][:, :], uT[:, m * 128:(m + 1) * 128],
                ag[:, ns], start=True, stop=True),
                reads=("dT",) + kag, writes=(f"ps{4 + (m % 2)}",))
            pb.op("vector", lambda e, m=m, n=n, ns=ns: e.tensor_copy(
                ev[:, ns], ps[4 + (m % 2)][:, :]),
                reads=(f"ps{4 + (m % 2)}",), writes=(kev[n],))
        pb.dma("sync", moe_out[m * 128:(m + 1) * 128, :], ev[:, :],
               reads=kev, writes=(ctx.wkey("moe_out", m),))


def emit_l0_delta(ctx, A1, A2, MOE, delta):
    def body(pb, ib, ob, tb, bb, ik, ok, tk, bk, info):
        eng = info[3]
        a1, a2, mo = ib
        (dl,) = ob
        ka1, ka2, kmo = ik
        (kdl,) = ok
        def o(eng, fn, reads=(), writes=(), wide=True):
            pb.op(eng, fn, reads=reads, writes=writes, wide=wide)
        o(eng, lambda e: e.tensor_add(dl, a1, a2), reads=(ka1, ka2), writes=(kdl,), wide=True)
        o(eng, lambda e: e.tensor_scalar(dl, dl, 1.0 / D, None, ALU.mult),
          reads=(kdl,), writes=(kdl,))
        o(eng, lambda e: e.tensor_add(dl, dl, mo), reads=(kdl, kmo), writes=(kdl,), wide=True)

    emit_stream_elem(ctx, [A1, A2, MOE], ["A1", "A2", "moe_out"],
                     [(delta, "delta")], [], 0, body,
                     in_keys=[lambda d, c, ci: (f"A1@{d}",),
                              lambda d, c, ci: (f"A2@{d}",),
                              lambda d, c, ci: (f"moe_out@{d}",)])


def emit_transpose_f32(ctx, src_fm, src_name, dst_tm, dst_name):
    pb = ctx.pb
    identf = ctx.sb("identf", [128, 128], F32)
    pb.dma("sync", identf[:, :], ctx.dram["ident"][:, :], writes=("identf",))
    stage = ctx.sb("fs0", [128, T], F32)
    tin = [ctx.sb(f"tt{i}", [128, 128], F32) for i in range(2)]
    kst = FK("fs0")
    ps = ctx.ps("ps0")
    for tt in range(T // 128):
        for dt in range(KT):
            i = dt % 2
            pb.dma("sync", tin[i][:, :],
                   src_fm[dt * 128:(dt + 1) * 128, tt * 128:(tt + 1) * 128],
                   reads=ctx.rkeys(src_name), writes=(f"tt{i}",))
            pb.op("tensor", lambda e, i=i: e.transpose(
                ps[:, 0:128], tin[i][:, :], identf[:, :]),
                reads=(f"tt{i}", "identf"), writes=("ps0",))
            pb.op("vector", lambda e, dt=dt: e.tensor_copy(
                stage[:, dt * 128:(dt + 1) * 128], ps[:, 0:128]),
                reads=("ps0",), writes=(kst[dt // 4],))
        pb.dma("sync", dst_tm[tt * 128:(tt + 1) * 128, :], stage[:, :],
               reads=kst, writes=(ctx.wkey(dst_name, tt),))


def emit_l0(ctx, x_tm, delta_tm):
    pb = ctx.pb
    pool = get_pool(ctx)
    xbuf = pool["xbuf"]
    emit_l0_front(ctx, x_tm)
    moe_out = ctx.scratch("moe_out", [D, T], F32)
    emit_l0_moe(ctx, moe_out)
    for pl, srcn in (("FX", None), ("FQ", "q0"), ("FK", "k0"),
                     ("FV", "v0"), ("FS", "s0"), ("FT", "t0")):
        emit_fft_pass(ctx, "Cdft", pl, "r", srcn)
    for pl, srcn in (("FX", None), ("FQ", "q0"), ("FK", "k0"),
                     ("FV", "v0")):
        emit_fft_pass(ctx, "NSdft", pl, "i", srcn)
    XHr = ctx.scratch("XHr", [D, T], F32)
    XHi = ctx.scratch("XHi", [D, T], F32)
    emit_l0_elemA(ctx, ctx.dram["FQr"], ctx.dram["FQi"], ctx.dram["FKr"],
                  ctx.dram["FKi"], ctx.dram["FVr"], ctx.dram["FVi"],
                  ctx.dram["FXr"], ctx.dram["FXi"], XHr, XHi)
    emit_fft_pass(ctx, "NSdft", "FS", "i", "s0")
    G = ctx.scratch("G", [D, T], F32)
    TAr = ctx.scratch("TAr", [D, T], BF16)
    TAi = ctx.scratch("TAi", [D, T], BF16)
    emit_l0_elemB(ctx, ctx.dram["FSr"], ctx.dram["FSi"], XHr, XHi,
                  G, TAr, TAi)
    emit_fft_pass(ctx, "NSdft", "FT", "i", "t0")
    pb.barrier()
    emit_mirror(ctx, TAr, "TAr", False, src_bf=True)
    emit_mirror(ctx, TAi, "TAi", True, src_bf=True)
    emit_mirror(ctx, XHr, "XHr", False)
    emit_mirror(ctx, XHi, "XHi", True)
    emit_mirror(ctx, G, "G", False)
    emit_mirror(ctx, ctx.dram["FTr"], "FTr", False)
    emit_mirror(ctx, ctx.dram["FTi"], "FTi", True)
    pb.barrier()
    emit_l0_mlp(ctx)
    pb.barrier()
    XHnR = ctx.scratch("XHnR", [D, T], BF16)
    XHnI = ctx.scratch("XHnI", [D, T], BF16)
    emit_l0_combine(ctx, ctx.dram["XHr"], ctx.dram["XHi"], ctx.dram["G"],
                    ctx.dram["FTr"], ctx.dram["FTi"], ctx.dram["Hr"],
                    ctx.dram["Hi"], XHnR, XHnI)
    ctx.wcache = None
    A1 = ctx.scratch("A1", [D, T], F32)
    A2 = ctx.scratch("A2", [D, T], F32)
    emit_load_plane(ctx, xbuf, "xbuf", XHnR, "XHnR",
                    key_fn=lambda k: tuple(f"XHnR@{4 * k + j}" for j in range(4)))
    emit_linear_fm(ctx, "Cdft", ctx.dram["Cdft"], xbuf, "xbuf", out_f32=A1,
                   outf_name="A1")
    emit_load_plane(ctx, xbuf, "xbuf", XHnI, "XHnI",
                    key_fn=lambda k: tuple(f"XHnI@{4 * k + j}" for j in range(4)))
    emit_linear_fm(ctx, "NSdft", ctx.dram["NSdft"], xbuf, "xbuf", out_f32=A2,
                   outf_name="A2")
    emit_l1_ffn_prep(ctx)
    delta_fm = ctx.scratch("delta_fm", [D, T], F32)
    emit_l0_delta(ctx, A1, A2, moe_out, delta_fm)
    emit_transpose_f32(ctx, delta_fm, "delta", delta_tm, "delta_tm")


def emit_l1_attn(ctx, attnO_tm):
    pb = ctx.pb
    pool = get_pool(ctx)
    xbuf, wbuf = pool["xbuf"], pool["wbuf"]
    bias = ctx.sb("bias", [128, 128], F32)
    identb = ctx.sb("identb", [128, 128], BF16)
    pb.dma("gpsimd", identb[:, :], ctx.dram["ident"][:, :], writes=("identb",))
    qT = ctx.scratch("qT", [D, T], BF16)
    kT = ctx.scratch("kT", [D, T], BF16)
    v_tm = ctx.scratch("v_tm", [T, D], BF16)
    emit_load_fm(ctx, xbuf, "xbuf", ctx.dram["xn1_tm"], "xn1_tm")
    emit_load_bias(ctx, bias, "bias", ctx.dram["L1_attn_bq"], KT, 0)
    emit_load_bias(ctx, bias, "bias", ctx.dram["L1_attn_bk"], KT, 16)
    emit_linear_fm(ctx, "L1_attn_Wq", ctx.dram["L1_attn_Wq"], xbuf, "xbuf",
                   out_bf=qT, out_name="qT", bias_sb=bias, bias_key="bias",
                   bias_off=0)
    emit_linear_fm(ctx, "L1_attn_Wk", ctx.dram["L1_attn_Wk"], xbuf, "xbuf",
                   out_bf=kT, out_name="kT", bias_sb=bias, bias_key="bias",
                   bias_off=16)
    ps = [ctx.ps(f"ps{i}") for i in range(7)]
    qst = ctx.sb("dT", [128, KT * 128], BF16)
    onescol = qst[0:1, 0:128]
    pb.op("vector", lambda e: e.memset(onescol, 1.0), writes=("qst_0",))
    bs1 = ctx.sb("bs1", [128, T], BF16)
    bvrow = bs1[0:1, :]
    pb.dma("gpsimd", bvrow,
           ctx.dram["L1_attn_bv"].rearrange("(one d) -> one d", one=1),
           writes=FK("bs1"))
    ctx.wcache = None
    for k in range(KT):
        pb.dma("gpsimd", wbuf[:, k * D:(k + 1) * D],
               ctx.dram["L1_attn_Wv"][k * 128:(k + 1) * 128, :],
               writes=(f"wbuf{k}",))
    evb = ctx.sb("bs0", [128, T], BF16)
    kevb = FK("bs0")
    for tc in range(T // 128):
        for n in range(NT):
            ns = slice(n * NCHUNK, (n + 1) * NCHUNK)
            def vgroup(e, tc=tc, n=n, ns=ns):
                for k in range(KT):
                    e.matmul(ps[n][:, :],
                             xbuf[:, k * T + tc * 128:k * T + (tc + 1) * 128],
                             wbuf[:, k * D + n * NCHUNK:k * D + (n + 1) * NCHUNK],
                             start=(k == 0), stop=False)
                return e.matmul(ps[n][:, :], onescol, bvrow[0:1, ns],
                                start=False, stop=True)
            pb.op("tensor", vgroup,
                  reads=tuple(f"xbuf{k}" for k in range(KT)) +
                  tuple(f"wbuf{k}" for k in range(KT)) + ("qst_0",) + FK("bs1"),
                  writes=(f"ps{n}",))
            pb.op("scalar", lambda e, n=n, ns=ns: e.activation(
                evb[:, ns], ps[n][:, :], AF.Copy),
                reads=(f"ps{n}",), writes=(kevb[n],))
        pb.dma("sync", v_tm[tc * 128:(tc + 1) * 128, :], evb[:, :],
               reads=kevb, writes=(ctx.wkey("v_tm", tc),))
    pb.barrier()
    emit_load_plane(ctx, xbuf, "xbuf", kT, "kT")
    for j in range(KT):
        pb.dma("sync", wbuf[:, j * D:(j + 1) * D],
               v_tm[j * 128:(j + 1) * 128, :], reads=ctx.rkeys("v_tm"),
               writes=(f"wbuf{j}",))
    causalF = ctx.sb("tt0", [128, 128], F32)
    pb.dma("sync", causalF[:, :], ctx.dram["causal"][:, :], writes=("tt0",))
    fsq = [ctx.sb(f"fs{i}", [128, T], F32) for i in range(6)]
    pbf = bs1
    kpbf = FK("bs1")
    pTb = ctx.sb("bs0", [128, T], BF16)
    lm = ctx.sb("ln_mean", [128, 8], F32)
    lsum = ctx.sb("ln_ssq", [128, 8], F32)
    pst = ctx.ps("pst", [128, 512], BF16)
    for qc in range(T // 128):
        par = qc % 2
        sc, pex, ev = fsq[3 * par], fsq[3 * par + 1], fsq[3 * par + 2]
        ksc, kpex, kev2 = FK(f"fs{3 * par}"), FK(f"fs{3 * par + 1}"), \
            FK(f"fs{3 * par + 2}")
        lmc, lsc = 4 * par, 4 * par
        nkv = qc + 1
        W = nkv * 128
        nch = (W + NCHUNK - 1) // NCHUNK
        for k in range(KT):
            pb.dma("scalar", qst[:, k * 128:(k + 1) * 128],
                   qT[k * 128:(k + 1) * 128, qc * 128:(qc + 1) * 128],
                   reads=ctx.rkeys("qT"), writes=(f"qst_{k}",))
        for n in range(nch):
            w = min(NCHUNK, W - n * NCHUNK)
            def sgroup(e, n=n, w=w):
                last = None
                for k in range(KT):
                    last = e.matmul(
                        ps[n][:, 0:w], qst[:, k * 128:(k + 1) * 128],
                        xbuf[:, k * T + n * NCHUNK:k * T + n * NCHUNK + w],
                        start=(k == 0), stop=(k == KT - 1))
                return last
            pb.op("tensor", sgroup,
                  reads=tuple(f"qst_{k}" for k in range(KT)) +
                  tuple(f"xbuf{k}" for k in range(KT)),
                  writes=(f"ps{n}",))
            pb.op("vector", lambda e, n=n, w=w: e.tensor_scalar(
                sc[:, n * NCHUNK:n * NCHUNK + w], ps[n][:, 0:w],
                float(D) ** -0.5, None, ALU.mult),
                reads=(f"ps{n}",), writes=(ksc[n],))
        pb.op("vector", lambda e, qc=qc: e.tensor_add(
            sc[:, qc * 128:(qc + 1) * 128], sc[:, qc * 128:(qc + 1) * 128],
            causalF[:, :]),
            reads=(ksc[qc // 4], "tt0"), writes=(ksc[qc // 4],))
        pb.op("vector", lambda e, W=W, lmc=lmc: e.reduce_max(
            lm[:, lmc:lmc + 1], sc[:, 0:W], axis=AX.X),
            reads=ksc[:nch], writes=(f"l1lm{par}",))
        pb.op("vector", lambda e, lmc=lmc: e.tensor_scalar_mul(
            lm[:, lmc:lmc + 1], lm[:, lmc:lmc + 1], -1.0),
            reads=(f"l1lm{par}",), writes=(f"l1lm{par}",))
        pb.op("scalar", lambda e, W=W, lmc=lmc, lsc=lsc: e.activation(
            pex[:, 0:W], sc[:, 0:W], AF.Exp, bias=lm[:, lmc:lmc + 1],
            accum_out=lsum[:, lsc:lsc + 1]),
            reads=ksc[:nch] + (f"l1lm{par}",),
            writes=kpex[:nch] + (f"l1ls{par}",))
        pb.op("vector", lambda e, lsc=lsc: e.reciprocal(
            lsum[:, lsc + 1:lsc + 2], lsum[:, lsc:lsc + 1]),
            reads=(f"l1ls{par}",), writes=(f"l1ls{par}",))
        pb.op("vector", lambda e, W=W, lsc=lsc: e.tensor_scalar(
            pbf[:, 0:W], pex[:, 0:W], lsum[:, lsc + 1:lsc + 2], None,
            ALU.mult),
            reads=kpex[:nch] + (f"l1ls{par}",), writes=kpbf[:nch])
        for j in range(nkv):
            pb.op("tensor", lambda e, j=j: e.transpose(
                pst[:, 0:128], pbf[:, j * 128:(j + 1) * 128], identb[:, :]),
                reads=(kpbf[j // 4], "identb"), writes=("pst",))
            pb.op("vector", lambda e, j=j: e.tensor_copy(
                pTb[:, j * 128:(j + 1) * 128], pst[:, 0:128]),
                reads=("pst",), writes=(FK("bs0")[j // 4],))
        for n in range(NT):
            ns = slice(n * NCHUNK, (n + 1) * NCHUNK)
            def ogroup(e, n=n, nkv=nkv, ns=ns):
                last = None
                for j in range(nkv):
                    last = e.matmul(
                        ps[4 + n % 2][:, :], pTb[:, j * 128:(j + 1) * 128],
                        wbuf[:, j * D + n * NCHUNK:j * D + (n + 1) * NCHUNK],
                        start=(j == 0), stop=(j == nkv - 1))
                return last
            pb.op("tensor", ogroup,
                  reads=FK("bs0")[:(nkv + 3) // 4] +
                  tuple(f"wbuf{j}" for j in range(nkv)),
                  writes=(f"ps{4 + n % 2}",))
            pb.op("vector", lambda e, n=n, ns=ns: e.tensor_copy(
                ev[:, ns], ps[4 + n % 2][:, :]),
                reads=(f"ps{4 + n % 2}",), writes=(kev2[n],))
        pb.dma("sync", attnO_tm[qc * 128:(qc + 1) * 128, :], ev[:, :],
               reads=kev2, writes=(ctx.wkey("attnO", qc),))


def emit_weight_transpose(ctx, wname, W_dram, out_plane, out_name, R, C):
    pb = ctx.pb
    identb = ctx.sb("identb", [128, 128], BF16)
    pb.dma("gpsimd", identb[:, :], ctx.dram["ident"][:, :], writes=("identb",))
    band = ctx.sb("bs0", [128, T], BF16)
    stg = ctx.sb("bs1", [128, T], BF16)
    pst = ctx.ps("pst", [128, 512], BF16)
    kb, ks = FK("bs0"), FK("bs1")
    for r in range(R // 128):
        pb.dma("gpsimd", band[:, 0:C], W_dram[r * 128:(r + 1) * 128, :],
               writes=kb)
        for c in range(C // 128):
            pb.op("tensor", lambda e, c=c: e.transpose(
                pst[:, 0:128], band[:, c * 128:(c + 1) * 128], identb[:, :]),
                reads=kb + ("identb",), writes=("pst",))
            pb.op("vector", lambda e, c=c: e.tensor_copy(
                stg[:, c * 128:(c + 1) * 128], pst[:, 0:128]),
                reads=("pst",), writes=(ks[c // 4],))
        for c in range(C // 128):
            pb.dma("sync", out_plane[c * 128:(c + 1) * 128,
                                     r * 128:(r + 1) * 128],
                   stg[:, c * 128:(c + 1) * 128], reads=(ks[c // 4],),
                   writes=(ctx.wkey(out_name, f"{r}_{c}"),))


def emit_l1_ffn_prep(ctx):
    pb = ctx.pb
    Wb_in = ctx.scratch("Wb_in", [3 * D, D], BF16)
    Wb_wr = ctx.scratch("Wb_wr", [D, D], BF16)
    Wb_op = ctx.scratch("Wb_op", [D, D], BF16)
    cstg = [ctx.sb("fs4", [128, T], F32), ctx.sb("fs5", [128, T], F32)]
    bstg = [ctx.sb("bs0", [128, T], BF16), ctx.sb("bs1", [128, T], BF16)]
    for wn, wb, rr in (("L1_ffn_in_proj_w", Wb_in, 3 * D),
                       ("L1_ffn_write_w", Wb_wr, D),
                       ("L1_ffn_out_proj_w", Wb_op, D)):
        for r in range(rr // 128):
            i = r % 2
            sl = bstg[i][:, :]
            kk = FK(f"bs{i}")
            pb.dma("gpsimd", sl, ctx.dram[wn][r * 128:(r + 1) * 128, :],
                   writes=kk)
            pb.dma("sync", wb[r * 128:(r + 1) * 128, :], sl, reads=kk,
                   writes=(ctx.wkey("b" + wn, r),))


def emit_l1_ffn(ctx, ffnO_tm):
    pb = ctx.pb
    NB, S, HD, NH = 2, 16, 64, 32
    EPS = 1e-6
    pool = get_pool(ctx)
    xbuf, wbuf = pool["xbuf"], pool["wbuf"]
    ps = [ctx.ps(f"ps{i}") for i in range(7)]
    pst = ctx.ps("pst", [128, 512], BF16)
    identb = ctx.sb("identb", [128, 128], BF16)
    identf = ctx.sb("identf", [128, 128], F32)
    pb.dma("sync", identf[:, :], ctx.dram["ident"][:, :], writes=("identf",))
    bias = ctx.sb("bias", [128, 128], F32)
    fs = [ctx.sb(f"fs{i}", [128, T], F32) for i in range(6)]
    bs = [ctx.sb(f"bs{i}", [128, T], BF16) for i in range(2)]

    Wb_in = ctx.dram["Wb_in"]
    Wb_wr = ctx.dram["Wb_wr"]
    Wb_op = ctx.dram["Wb_op"]

    onesf = ctx.sb("onesf", [1, 128], F32)
    pb.op("vector", lambda e: e.memset(onesf[:, :], 1.0), writes=("onesf",))
    wcolbuf = ctx.sb("wcol", [128, 8192], BF16)
    cwT = wcolbuf[:, 0:256]
    cw = fs[4]
    wpre = fs[5]
    kcw, kwpre = FK("fs4"), FK("fs5")
    pb.dma("sync", cw[0:16, :], ctx.dram["L1_ffn_compete_w"][:, :], writes=kcw)
    pb.dma("sync", wpre[0:1, :],
           ctx.dram["L1_ffn_norm_pre_w"].rearrange("(o d) -> o d", o=1),
           writes=kwpre)
    for n in range(NT):
        ns = slice(n * NCHUNK, (n + 1) * NCHUNK)
        pb.op("tensor", lambda e, ns=ns: e.matmul(
            ps[6][0:16, :], onesf[0:1, 0:16], wpre[0:1, ns],
            start=True, stop=True), reads=("onesf",) + kwpre, writes=("ps6",))
        pb.op("vector", lambda e, n=n, ns=ns: e.tensor_mul(
            cw[0:16, ns], cw[0:16, ns], ps[6][0:16, :]),
            reads=(kcw[n], "ps6"), writes=(kcw[n],))
    cwb = bs[0]
    pb.op("vector", lambda e: e.tensor_copy(cwb[0:16, :], cw[0:16, :]),
          reads=kcw, writes=FK("bs0"))
    for k in range(KT):
        pb.op("tensor", lambda e, k=k: e.transpose(
            pst[0:128, 0:16], cwb[0:16, k * 128:(k + 1) * 128],
            identb[0:16, 0:16]), reads=FK("bs0") + ("identb",), writes=("pst",))
        pb.op("vector", lambda e, k=k: e.tensor_copy(
            cwT[:, k * 16:(k + 1) * 16], pst[0:128, 0:16]),
            reads=("pst",), writes=("cwT",))
    cbF = bias[:, 80:96]
    cbrow = cbF[0:1, 0:16]
    pb.dma("sync", cbrow[0:1, :],
           ctx.dram["L1_ffn_compete_b"].rearrange("(o s) -> o s", o=1),
           writes=("cbrow",))
    pb.op("tensor", lambda e: e.matmul(ps[6][:, 0:16], onesf[0:1, :],
                                       cbrow[0:1, :], start=True, stop=True),
          reads=("onesf", "cbrow"), writes=("ps6",))
    pb.op("vector", lambda e: e.tensor_copy(cbF[:, :], ps[6][:, 0:16]),
          reads=("ps6",), writes=("cbF",))
    emit_load_bias(ctx, bias, "bias", ctx.dram["L1_ffn_write_b"], KT, 32)
    emit_load_bias(ctx, bias, "bias", ctx.dram["L1_ffn_out_proj_b"], KT, 48)
    emit_load_bias(ctx, bias, "bias", ctx.dram["L1_ffn_in_proj_b"], KT, 64)
    bkv = wcolbuf[0:1, 1024:1024 + T]
    ipb = ctx.dram["L1_ffn_in_proj_b"].rearrange("(th d) -> th d", d=D)
    onesb = wcolbuf[0:1, 3072:3072 + 128]
    pb.op("vector", lambda e: e.memset(onesb, 1.0), writes=("onesb",))
    wsP = ctx.scratch("wsP", [S, D], F32)
    pb.dma("sync", fs[4][0:16, :], ctx.dram["L1_ffn_workspace"][:, :],
           writes=kcw)
    pb.dma("sync", wsP[:, :], fs[4][0:16, :], reads=kcw,
           writes=(ctx.wkey("wsP", 0),))
    rstdp = bias[:, 96:112]
    st8 = ctx.sb("ln_var", [128, 8], F32)
    top8 = bias[0:16, 112:120]
    wgt4 = bias[0:16, 120:128]
    wsT = wcolbuf[:, 256:512]
    kkT = wcolbuf[:, 512:768]
    gT = wcolbuf[:, 768:1024]
    pb.barrier()

    x_tm_f = "xn1f_tm"
    x_tm_b = "xn1_tm"
    xa = ctx.scratch("l1xa", [T, D], F32)
    xab = ctx.scratch("l1xab", [T, D], BF16)
    wxP = ctx.scratch("wxP", [D, T], BF16)
    q1P = ctx.scratch("q1P", [D, T], BF16)
    ctxP = ctx.scratch("ctxP", [T, D], BF16)
    xbP = ctx.scratch("xbP", [T, D], F32)

    for nb in range(NB):
        last = nb == NB - 1
        emit_load_fm(ctx, xbuf, "xbuf", ctx.dram[x_tm_b], x_tm_b)
        ctx.wcache = None
        emit_linear_fm(ctx, "WT_wr", Wb_wr, xbuf, "xbuf", out_bf=wxP,
                       out_name="wxP", bias_sb=bias, bias_key="bias",
                       bias_off=32, w_transpose=True)
        emit_linear_fm(ctx, "WT_in_q", Wb_in[0:D, :], xbuf, "xbuf",
                       out_bf=q1P, out_name="q1P", bias_sb=bias,
                       bias_key="bias", bias_off=64, w_transpose=True)
        for t in range(T // 128):
            i = t % 2
            kx = FK(f"fs{i}")
            pb.dma("sync", fs[i][:, :], ctx.dram[x_tm_f][t * 128:(t + 1) * 128, :],
                   reads=ctx.rkeys(x_tm_f), writes=kx)
            pb.op("scalar", lambda e, i=i: e.activation(
                fs[2][:, :], fs[i][:, :], AF.Square, accum_out=st8[:, 0:1]),
                reads=kx, writes=FK("fs2") + ("st8",))
            pb.op("vector", lambda e: e.tensor_scalar(
                st8[:, 1:2], st8[:, 0:1], 1.0 / D, EPS, ALU.mult, ALU.add),
                reads=("st8",), writes=("st8",))
            pb.op("scalar", lambda e: e.activation(
                st8[:, 2:3], st8[:, 1:2], AF.Sqrt), reads=("st8",),
                writes=("st8",))
            pb.op("vector", lambda e, t=t: e.reciprocal(
                rstdp[:, t:t + 1], st8[:, 2:3]), reads=("st8",),
                writes=("rstdp",))
        lsm = fs[1]
        klsm = FK("fs1")
        for tc in range(T // 128):
            def lgroup(e, tc=tc):
                last_i = None
                for k in range(KT):
                    last_i = e.matmul(
                        ps[6][:, 0:16],
                        xbuf[:, k * T + tc * 128:k * T + (tc + 1) * 128],
                        cwT[:, k * 16:(k + 1) * 16],
                        start=(k == 0), stop=(k == KT - 1))
                return last_i
            pb.op("tensor", lgroup,
                  reads=tuple(f"xbuf{k}" for k in range(KT)) + ("cwT",),
                  writes=("ps6",))
            pb.op("vector", lambda e, tc=tc: e.tensor_scalar(
                fs[0][:, 0:16], ps[6][:, 0:16], rstdp[:, tc:tc + 1], None,
                ALU.mult), reads=("ps6", "rstdp"), writes=("fs0_0",))
            pb.op("vector", lambda e: e.tensor_add(
                fs[0][:, 0:16], fs[0][:, 0:16], cbF[:, :]),
                reads=("fs0_0", "cbF"), writes=("fs0_0",))
            pb.op("tensor", lambda e, tc=tc: e.transpose(
                ps[6][0:16, 0:128], fs[0][:, 0:16], identf[:, :]),
                reads=("fs0_0", "identf"), writes=("ps6",))
            pb.op("vector", lambda e, tc=tc: e.tensor_copy(
                lsm[0:16, tc * 128:(tc + 1) * 128], ps[6][0:16, 0:128]),
                reads=("ps6",), writes=(klsm[tc // 4],))
        pb.op("vector", lambda e: e.max(top8[0:16, :], lsm[0:16, :]),
              reads=klsm, writes=("t4top",))
        pb.op("vector", lambda e: e.tensor_scalar(
            wgt4[0:16, 0:4], top8[0:16, 0:4], top8[0:16, 0:1], None,
            ALU.subtract), reads=("t4top",), writes=("t4wgt",))
        pb.op("scalar", lambda e: e.activation(
            wgt4[0:16, 0:4], wgt4[0:16, 0:4], AF.Exp,
            accum_out=wgt4[0:16, 4:5]), reads=("t4wgt",), writes=("t4wgt",))
        pb.op("vector", lambda e: e.reciprocal(wgt4[0:16, 5:6],
                                               wgt4[0:16, 4:5]),
              reads=("t4wgt",), writes=("t4wgt",))
        pb.op("vector", lambda e: e.tensor_scalar(
            wgt4[0:16, 0:4], wgt4[0:16, 0:4], wgt4[0:16, 5:6], None,
            ALU.mult), reads=("t4wgt",), writes=("t4wgt",))
        G = fs[2]
        kG = FK("fs2")
        pb.op("vector", lambda e: e.memset(G[0:16, :], 0.0), writes=kG)
        for kk4 in range(4):
            pb.op("vector", lambda e, kk4=kk4: e.tensor_scalar(
                fs[3][0:16, :], lsm[0:16, :], top8[0:16, kk4:kk4 + 1], None,
                ALU.is_equal), reads=klsm + ("t4top",), writes=FK("fs3"))
            pb.op("vector", lambda e, kk4=kk4: e.scalar_tensor_tensor(
                G[0:16, :], fs[3][0:16, :], wgt4[0:16, kk4:kk4 + 1],
                G[0:16, :], ALU.mult, ALU.add),
                reads=FK("fs3") + ("t4wgt",) + kG, writes=kG)
        pb.op("vector", lambda e: e.tensor_copy(bs[0][0:16, :], G[0:16, :]),
              reads=kG, writes=FK("bs0"))
        for k in range(KT):
            pb.op("tensor", lambda e, k=k: e.transpose(
                pst[0:128, 0:16], bs[0][0:16, k * 128:(k + 1) * 128],
                identb[0:16, 0:16]), reads=FK("bs0") + ("identb",),
                writes=("pst",))
            pb.op("vector", lambda e, k=k: e.tensor_copy(
                gT[:, k * 16:(k + 1) * 16], pst[0:128, 0:16]),
                reads=("pst",), writes=("gT",))
        pb.barrier()
        pb.barrier()
        ctx.wcache = None
        for j in range(KT):
            pb.dma("sync", wbuf[:, j * D:(j + 1) * D],
                   wxP[:, j * 128:(j + 1) * 128], transpose=True,
                   reads=ctx.rkeys("wxP"), writes=(f"wbuf{j}",))
        wr16 = fs[3]
        kwr = FK("fs3")
        for n in range(NT):
            ns = slice(n * NCHUNK, (n + 1) * NCHUNK)
            def wgroup(e, n=n, ns=ns):
                last_i = None
                for j in range(KT):
                    last_i = e.matmul(
                        ps[n][0:16, :], gT[:, j * 16:(j + 1) * 16],
                        wbuf[:, j * D + n * NCHUNK:j * D + (n + 1) * NCHUNK],
                        start=(j == 0), stop=(j == KT - 1))
                return last_i
            pb.op("tensor", wgroup,
                  reads=("gT",) + tuple(f"wbuf{j}" for j in range(KT)),
                  writes=(f"ps{n}",))
            pb.op("vector", lambda e, n=n, ns=ns: e.tensor_copy(
                wr16[0:16, ns], ps[n][0:16, :]), reads=(f"ps{n}",),
                writes=(kwr[n],))
        pb.dma("sync", fs[4][0:16, :], wsP[:, :], reads=ctx.rkeys("wsP"),
               writes=kcw)
        pb.op("vector", lambda e: e.tensor_add(
            fs[4][0:16, :], fs[4][0:16, :], wr16[0:16, :]),
            reads=kcw + kwr, writes=kcw)
        pb.op("scalar", lambda e: e.activation(
            fs[0][0:16, :], fs[4][0:16, :], AF.Square,
            accum_out=st8[0:16, 0:1]), reads=kcw,
            writes=FK("fs0") + ("st8",))
        pb.op("vector", lambda e: e.tensor_scalar(
            st8[0:16, 1:2], st8[0:16, 0:1], 1.0 / D, EPS, ALU.mult, ALU.add),
            reads=("st8",), writes=("st8",))
        pb.op("scalar", lambda e: e.activation(
            st8[0:16, 2:3], st8[0:16, 1:2], AF.Sqrt), reads=("st8",),
            writes=("st8",))
        pb.op("vector", lambda e: e.reciprocal(st8[0:16, 3:4], st8[0:16, 2:3]),
              reads=("st8",), writes=("st8",))
        pb.op("vector", lambda e: e.tensor_scalar(
            fs[4][0:16, :], fs[4][0:16, :], st8[0:16, 3:4], None, ALU.mult),
            reads=kcw + ("st8",), writes=kcw)
        pb.dma("sync", fs[5][0:1, :],
               ctx.dram["L1_ffn_norm_post_w"].rearrange("(o d) -> o d", o=1),
               writes=kwpre)
        for n in range(NT):
            ns = slice(n * NCHUNK, (n + 1) * NCHUNK)
            pb.op("tensor", lambda e, ns=ns: e.matmul(
                ps[6][0:16, :], onesf[0:1, 0:16], fs[5][0:1, ns],
                start=True, stop=True), reads=("onesf",) + kwpre,
                writes=("ps6",))
            pb.op("vector", lambda e, n=n, ns=ns: e.tensor_mul(
                fs[4][0:16, ns], fs[4][0:16, ns], ps[6][0:16, :]),
                reads=(kcw[n], "ps6"), writes=(kcw[n],))
        pb.dma("sync", wsP[:, :], fs[4][0:16, :], reads=kcw,
               writes=(ctx.wkey("wsP", nb + 1),))
        pb.barrier()
        pb.op("vector", lambda e: e.tensor_copy(bs[0][0:16, :], fs[4][0:16, :]),
              reads=kcw, writes=FK("bs0"))
        for k in range(KT):
            pb.op("tensor", lambda e, k=k: e.transpose(
                pst[0:128, 0:16], bs[0][0:16, k * 128:(k + 1) * 128],
                identb[0:16, 0:16]), reads=FK("bs0") + ("identb",),
                writes=("pst",))
            pb.op("vector", lambda e, k=k: e.tensor_copy(
                wsT[:, k * 16:(k + 1) * 16], pst[0:128, 0:16]),
                reads=("pst",), writes=("wsT",))
        vvp = bs[1]
        kvvp = FK("bs1")
        pb.op("vector", lambda e: e.memset(vvp[:, :], 0.0), writes=kvvp)
        for which, off, brow in (("kk", D, 1), ("vv", 2 * D, 2)):
            pb.dma("gpsimd", bkv[0:1, :], ipb[brow:brow + 1, :],
                   writes=("bkv0",))
            for k in range(KT):
                pb.dma("scalar", wbuf[:, k * D:(k + 1) * D],
                       Wb_in[off:off + D, k * 128:(k + 1) * 128],
                       transpose=True,
                       reads=ctx.rkeys("bL1_ffn_in_proj_w"),
                       writes=(f"wbuf{k}",))
            for n in range(NT):
                ns = slice(n * NCHUNK, (n + 1) * NCHUNK)
                def kvgroup(e, off=off, n=n, brow=brow):
                    for k in range(KT):
                        e.matmul(ps[5][0:16, :], wsT[:, k * 16:(k + 1) * 16],
                                 wbuf[:, k * D + n * NCHUNK:
                                      k * D + (n + 1) * NCHUNK],
                                 start=(k == 0), stop=False)
                    return e.matmul(ps[5][0:16, :], onesb[0:1, 0:16],
                                    bkv[0:1, n * NCHUNK:(n + 1) * NCHUNK],
                                    start=False, stop=True)
                pb.op("tensor", kvgroup,
                      reads=("wsT", "onesb", "bkv0") +
                      tuple(f"wbuf{k}" for k in range(KT)),
                      writes=("ps5",))
                if which == "kk":
                    pb.op("vector", lambda e, n=n, ns=ns: e.tensor_copy(
                        fs[0][0:16, ns], ps[5][0:16, :]), reads=("ps5",),
                        writes=(FK("fs0")[n],))
                else:
                    for j4 in (0, 64):
                        pb.op("vector", lambda e, n=n, ns=ns, j4=j4:
                              e.tensor_copy(vvp[j4:j4 + 16, ns],
                                            ps[5][0:16, :]), reads=("ps5",),
                              writes=(kvvp[n],))
        pb.op("vector", lambda e: e.tensor_copy(bs[0][0:16, :], fs[0][0:16, :]),
              reads=FK("fs0"), writes=FK("bs0"))
        for k in range(KT):
            pb.op("tensor", lambda e, k=k: e.transpose(
                pst[0:128, 0:16], bs[0][0:16, k * 128:(k + 1) * 128],
                identb[0:16, 0:16]), reads=FK("bs0") + ("identb",),
                writes=("pst",))
            pb.op("vector", lambda e, k=k: e.tensor_copy(
                kkT[:, k * 16:(k + 1) * 16], pst[0:128, 0:16]),
                reads=("pst",), writes=("kkT",))
        pb.barrier()
        ctx.wcache = None
        emit_load_plane(ctx, xbuf, "xbuf", q1P, "q1P")
        pe = fs[0]
        kpe = FK("fs0")
        am = fs[2]
        kam = FK("fs2")
        pb2 = bs[0]
        kpb2 = FK("bs0")
        qst = ctx.sb("dT", [128, KT * 128], BF16)
        for lc in range(T // 128):
            for par in (0, 1):
                def agroup(e, lc=lc, par=par):
                    last_i = None
                    for h in range(par, NH, 2):
                        po = 64 * (h % 2)
                        kt = h // 2
                        last_i = e.matmul(
                            ps[h // 8][:, 16 * (h % 8):16 * (h % 8) + 16],
                            xbuf[po:po + 64,
                                 kt * T + lc * 128:kt * T + (lc + 1) * 128],
                            kkT[po:po + 64, kt * 16:(kt + 1) * 16],
                            start=True, stop=True)
                    return last_i
                pb.op("tensor", agroup,
                      reads=tuple(f"xbuf{k}" for k in range(KT)) + ("kkT",),
                      writes=("ps0", "ps1", "ps2", "ps3"))
            for b in range(4):
                pb.op("vector", lambda e, b=b: e.tensor_scalar(
                    pe[:, b * 128:(b + 1) * 128], ps[b][:, 0:128],
                    HD ** -0.5, None, ALU.mult),
                    reads=(f"ps{b}",), writes=(kpe[0],))
            pe3 = pe[:, 0:512].rearrange("p (h s) -> p h s", s=16)
            am1 = am[:, 0:32].rearrange("p (h o) -> p h o", o=1)
            am2 = am[:, 32:64].rearrange("p (h o) -> p h o", o=1)
            pb.op("vector", lambda e: e.reduce_max(am1, pe3, axis=AX.X),
                  reads=(kpe[0],), writes=(kam[0],))
            pb.op("vector", lambda e: e.tensor_tensor(
                pe3, pe3, am1.broadcast_to((128, 32, 16)), ALU.subtract),
                reads=(kpe[0], kam[0]), writes=(kpe[0],))
            pb.op("scalar", lambda e: e.activation(
                pe[:, 0:512], pe[:, 0:512], AF.Exp),
                reads=(kpe[0],), writes=(kpe[0],))
            pb.op("vector", lambda e: e.reduce_sum(am2, pe3, axis=AX.X),
                  reads=(kpe[0],), writes=(kam[0],))
            pb.op("vector", lambda e: e.reciprocal(am[:, 32:64], am[:, 32:64]),
                  reads=(kam[0],), writes=(kam[0],))
            pb.op("vector", lambda e: e.memset(pb2[:, :], 0.0),
                  writes=kpb2)
            pb.op("vector", lambda e: e.tensor_tensor(
                pb2[:, :].rearrange("p (h s) -> p h s", s=64)[:, :, 0:16],
                pe3, am2.broadcast_to((128, 32, 16)), ALU.mult),
                reads=(kpe[0], kam[0]) + kpb2, writes=kpb2)
            for b in range(16):
                pb.op("tensor", lambda e, b=b: e.transpose(
                    pst[:, 0:128], pb2[:, b * 128:(b + 1) * 128],
                    identb[:, :]), reads=(kpb2[b // 4], "identb"),
                    writes=("pst",))
                pb.op("vector", lambda e, b=b: e.tensor_copy(
                    qst[:, b * 128:(b + 1) * 128], pst[:, 0:128]),
                    reads=("pst",), writes=(f"qst_{b // 4}",))

            for par in (0, 1):
                def cgroup(e, lc=lc, par=par):
                    last_i = None
                    for h in range(par, NH, 2):
                        blk = h // 2
                        po = 64 * (h % 2)
                        last_i = e.matmul(
                            ps[h // 8][:, 64 * (h % 8):64 * (h % 8) + 64],
                            qst[po:po + 32, blk * 128:(blk + 1) * 128],
                            vvp[po:po + 32, 64 * h:64 * h + 64],
                            start=True, stop=True)
                    return last_i
                pb.op("tensor", cgroup,
                      reads=tuple(f"qst_{b}" for b in range(4)) + kvvp,
                      writes=("ps0", "ps1", "ps2", "ps3"))
            for b in range(4):
                pb.op("vector", lambda e, b=b: e.tensor_copy(
                    bs[0][:, b * 512:(b + 1) * 512], ps[b][:, :]),
                    reads=(f"ps{b}",), writes=(kpb2[b],))
            pb.dma("sync", ctxP[lc * 128:(lc + 1) * 128, :], bs[0][:, :],
                   reads=kpb2, writes=(ctx.wkey("ctxP", lc),))
        pb.barrier()
        ctx.wcache = None
        emit_load_fm(ctx, xbuf, "xbuf", ctxP, "ctxP")
        for k in range(KT):
            pb.dma("sync", wbuf[:, k * D:(k + 1) * D],
                   Wb_op[:, k * 128:(k + 1) * 128], transpose=True,
                   reads=ctx.rkeys("bL1_ffn_out_proj_w"),
                   writes=(f"wbuf{k}",))
        oprow = bs[1][0:1, :]
        pb.dma("gpsimd", oprow,
               ctx.dram["L1_ffn_out_proj_b"].rearrange("(o d) -> o d", o=1),
               writes=(kvvp[0],))
        for tc in range(T // 128):
            for n in range(NT):
                ns = slice(n * NCHUNK, (n + 1) * NCHUNK)
                def xgroup(e, tc=tc, n=n, ns=ns):
                    for k in range(KT):
                        e.matmul(ps[n][:, :],
                                 xbuf[:, k * T + tc * 128:k * T + (tc + 1) * 128],
                                 wbuf[:, k * D + n * NCHUNK:k * D + (n + 1) * NCHUNK],
                                 start=(k == 0), stop=False)
                    return e.matmul(ps[n][:, :], onesb,
                                    oprow[0:1, ns], start=False, stop=True)
                pb.op("tensor", xgroup,
                      reads=tuple(f"xbuf{k}" for k in range(KT)) +
                      tuple(f"wbuf{k}" for k in range(KT)) +
                      ("onesb", kvvp[0]), writes=(f"ps{n}",))
                pb.op("vector", lambda e, n=n, ns=ns: e.tensor_copy(
                    fs[3][:, ns], ps[n][:, :]), reads=(f"ps{n}",),
                    writes=(FK("fs3")[n],))
            pb.dma("sync", xbP[tc * 128:(tc + 1) * 128, :], fs[3][:, :],
                   reads=FK("fs3"), writes=(ctx.wkey("xbP", tc),))
        pb.barrier()
        pb.dma("sync", fs[4][0:1, :],
               ctx.dram["L1_ffn_norm_post_w"].rearrange("(o d) -> o d", o=1),
               writes=kcw)
        for n in range(NT):
            ns = slice(n * NCHUNK, (n + 1) * NCHUNK)
            pb.op("tensor", lambda e, ns=ns: e.matmul(
                ps[6][:, :], onesf[0:1, :], fs[4][0:1, ns],
                start=True, stop=True), reads=("onesf",) + kcw,
                writes=("ps6",))
            pb.op("vector", lambda e, n=n, ns=ns: e.tensor_copy(
                fs[5][:, ns], ps[6][:, :]), reads=("ps6",),
                writes=(kwpre[n],))
        for t in range(T // 128):
            i = t % 2
            kx = FK(f"fs{i}")
            r = slice(t * 128, (t + 1) * 128)
            pb.dma("sync", fs[i][:, :], ctx.dram[x_tm_f][r, :],
                   reads=ctx.rkeys(x_tm_f), writes=kx)
            pb.dma("scalar", fs[2][:, :], xbP[r, :], reads=ctx.rkeys("xbP"),
                   writes=FK("fs2"))
            pb.op("vector", lambda e, i=i: e.tensor_add(
                fs[i][:, :], fs[i][:, :], fs[2][:, :]),
                reads=kx + FK("fs2"), writes=kx)
            pb.op("scalar", lambda e, i=i: e.activation(
                fs[3][:, :], fs[i][:, :], AF.Square, accum_out=st8[:, 0:1]),
                reads=kx, writes=FK("fs3") + ("st8",))
            pb.op("vector", lambda e: e.tensor_scalar(
                st8[:, 1:2], st8[:, 0:1], 1.0 / D, EPS, ALU.mult, ALU.add),
                reads=("st8",), writes=("st8",))
            pb.op("scalar", lambda e: e.activation(
                st8[:, 2:3], st8[:, 1:2], AF.Sqrt), reads=("st8",),
                writes=("st8",))
            pb.op("vector", lambda e: e.reciprocal(st8[:, 3:4], st8[:, 2:3]),
                  reads=("st8",), writes=("st8",))
            pb.op("vector", lambda e, i=i: e.tensor_scalar(
                fs[i][:, :], fs[i][:, :], st8[:, 3:4], None, ALU.mult),
                reads=kx + ("st8",), writes=kx)
            pb.op("vector", lambda e, i=i: e.tensor_mul(
                fs[i][:, :], fs[i][:, :], fs[5][:, :]),
                reads=kx + kwpre, writes=kx)
            if last:
                pb.dma("sync", ffnO_tm[r, :], fs[i][:, :], reads=kx,
                       writes=(ctx.wkey("ffnO", t),))
            else:
                pb.dma("sync", xa[r, :], fs[i][:, :], reads=kx,
                       writes=(ctx.wkey("l1xa", t),))
                pb.op("vector", lambda e, i=i: e.tensor_copy(
                    bs[0][:, :], fs[i][:, :]), reads=kx, writes=FK("bs0"))
                pb.dma("scalar", xab[r, :], bs[0][:, :], reads=FK("bs0"),
                       writes=(ctx.wkey("l1xab", t),))
        pb.barrier()
        x_tm_f, x_tm_b = "l1xa", "l1xab"


def emit_l1(ctx, x_tm, delta0_tm, out_dram):
    pb = ctx.pb
    x1_tm = ctx.scratch("x1_tm", [T, D], F32)
    xn1_tm = ctx.scratch("xn1_tm", [T, D], BF16)
    xn1f_tm = ctx.scratch("xn1f_tm", [T, D], F32)
    emit_ln(ctx, x_tm, xn1_tm, delta_tm=delta0_tm, x_out_tm=x1_tm,
            x_name="x", xn_name="xn1_tm", delta_name="delta_tm",
            xo_name="x1_tm", xnf_tm=xn1f_tm, xnf_name="xn1f_tm")
    pb.barrier()
    attnO_tm = ctx.scratch("attnO_tm", [T, D], F32)
    emit_l1_attn(ctx, attnO_tm)
    pb.barrier()
    ffnO_tm = ctx.scratch("ffnO_tm", [T, D], F32)
    emit_l1_ffn(ctx, ffnO_tm)
    pb.barrier()
    fs = [ctx.sb(f"fs{i}", [128, T], F32) for i in range(3)]
    for t in range(T // 128):
        r = slice(t * 128, (t + 1) * 128)
        i = t % 2
        kx = FK(f"fs{i}")
        pb.dma("sync", fs[i][:, :], x1_tm[r, :], reads=ctx.rkeys("x1_tm"),
               writes=kx)
        pb.dma("scalar", fs[2][:, :], attnO_tm[r, :], reads=ctx.rkeys("attnO"),
               writes=FK("fs2"))
        pb.op("vector", lambda e, i=i: e.tensor_add(
            fs[i][:, :], fs[i][:, :], fs[2][:, :]), reads=kx + FK("fs2"),
            writes=kx)
        pb.dma("scalar", fs[2][:, :], ffnO_tm[r, :], reads=ctx.rkeys("ffnO"),
               writes=FK("fs2"))
        pb.op("vector", lambda e, i=i: e.tensor_add(
            fs[i][:, :], fs[i][:, :], fs[2][:, :]), reads=kx + FK("fs2"),
            writes=kx)
        pb.dma("sync", out_dram[r, :], fs[i][:, :], reads=kx,
               writes=(ctx.wkey("out", t),))


def host_constants():
    C, NS = dft_constants()
    P1 = np.zeros((128, 128), np.float32)
    for r in range(1, 128):
        P1[128 - r, r] = 1.0
    P2 = np.zeros((128, 128), np.float32)
    P2[0, 0] = 1.0
    ident = np.eye(128, dtype=np.float32)
    E = np.zeros((8, 128), np.float32)
    for e in range(8):
        E[e, 16 * e:16 * e + 16] = 1.0
    cm = np.zeros((128, 128), np.float32)
    cm[np.triu_indices(128, 1)] = -1e30
    return dict(Cdft=C, NSdft=NS, ident=ident, Emoe=E, causal=cm, P1m=P1)


def build_program():
    ctx = Ctx()
    x = ctx.din("x", [T, D])
    for wn, shp in (
        ("L0_attn_Wq", [D, D]), ("L0_attn_Wk", [D, D]), ("L0_attn_Wv", [D, D]),
        ("L0_attn_Ws", [D, D]), ("L0_attn_Wt", [D, D]),
        ("L0_attn_W1", [2 * D, 4 * D]), ("L0_attn_b1", [4 * D]),
        ("L0_attn_W2", [4 * D, 2 * D]), ("L0_attn_b2", [2 * D]),
        ("L0_ffn_router_w", [8, D]), ("L0_ffn_down_w", [8, 16, D]),
        ("L0_ffn_up_w", [8, D, 16]),
        ("L1_attn_Wq", [D, D]), ("L1_attn_bq", [D]),
        ("L1_attn_Wk", [D, D]), ("L1_attn_bk", [D]),
        ("L1_attn_Wv", [D, D]), ("L1_attn_bv", [D]),
        ("L1_ffn_workspace", [16, D]), ("L1_ffn_compete_w", [16, D]),
        ("L1_ffn_compete_b", [16]), ("L1_ffn_write_w", [D, D]),
        ("L1_ffn_write_b", [D]), ("L1_ffn_in_proj_w", [3 * D, D]),
        ("L1_ffn_in_proj_b", [3 * D]), ("L1_ffn_out_proj_w", [D, D]),
        ("L1_ffn_out_proj_b", [D]),
        ("L1_ffn_norm_pre_w", [D]), ("L1_ffn_norm_post_w", [D]),
        ("Cdft", [D, D]), ("NSdft", [D, D]), ("ident", [128, 128]),
        ("Emoe", [8, 128]), ("causal", [128, 128]),
        ("P1m", [128, 128]),
    ):
        ctx.din(wn, shp)
    out = ctx.dout("out", [T, D])
    delta_tm = ctx.scratch("delta_tm", [T, D], F32)
    emit_l0(ctx, x, delta_tm)
    ctx.pb.barrier()
    emit_l1(ctx, x, delta_tm, out)
    ctx.pb.barrier()
    nc = ctx.finish()
    return ctx, nc


def kernel(**inputs):
    xs = np.ascontiguousarray(inputs["L0_attn_x"])
    weights = {k: np.ascontiguousarray(v) for k, v in inputs.items()
               if k != "L0_attn_x" and not k.startswith("L1_ffn_ig")}
    consts = host_constants()
    ctx, nc = build_program()
    in_maps = []
    for b in range(8):
        m = dict(x=xs[b], **weights, **consts)
        if hasattr(ctx, "const_values"):
            m["cst"] = const_input(ctx)
        in_maps.append(m)
    res = run_bass_kernel_spmd(nc, in_maps, core_ids=list(range(8)))
    return np.stack([r["out"] for r in res.results], axis=0)


# version 13
# speedup vs baseline: 1.1405x; 1.1405x over previous
import numpy as np
import concourse.bass as bass
import concourse.mybir as mybir
from concourse.bass_utils import run_bass_kernel_spmd

F32 = mybir.dt.float32
BF16 = mybir.dt.bfloat16
AF = mybir.ActivationFunctionType
ALU = mybir.AluOpType
AX = mybir.AxisListType

ENGS = ("sync", "scalar", "vector", "tensor", "gpsimd")

D = 2048
T = 2048
NCHUNK = 512
NT = T // NCHUNK
KT = D // 128


class PB:

    def __init__(self, nc):
        self.nc = nc
        self.items = {e: [] for e in ENGS}
        self.counts = {}
        self._sem_handles = {}
        self._ctxs = []
        self.res = {}
        self.sat = {e: {} for e in ENGS}
        self.n_ops = 0
        self._pool_i = {}

    def sem(self, name):
        if name not in self._sem_handles:
            cm = self.nc.semaphore(name)
            h = cm.__enter__()
            self._ctxs.append(cm)
            self._sem_handles[name] = h
            self.counts[name] = 0
        return self._sem_handles[name]

    def _wait(self, eng, sem_name, val):
        if val <= 0:
            return
        if self.sat[eng].get(sem_name, 0) >= val:
            return
        h = self.sem(sem_name)
        self.items[eng].append(lambda e, h=h, v=val: e.wait_ge(h, v))
        self.sat[eng][sem_name] = val

    def _deps(self, eng, reads, writes):
        qe = "q_" + eng
        for k in reads:
            r = self.res.get(k)
            if r and r["w"]:
                if r.get("wide") and r["w"][0] == qe:
                    continue
                self._wait(eng, *r["w"])
        for k in writes:
            r = self.res.get(k)
            if r:
                if r["w"] and not (r.get("wide") and r["w"][0] == qe):
                    self._wait(eng, *r["w"])
                for s, v in r["r"]:
                    if s == qe:
                        continue
                    self._wait(eng, s, v)

    def _notes(self, reads, writes, sem, val, wide=False):
        for k in reads:
            self.res.setdefault(k, {"w": None, "r": []})["r"].append((sem, val))
        for k in writes:
            self.res[k] = {"w": (sem, val), "r": [], "wide": wide}

    def op(self, eng, fn, reads=(), writes=(), wide=False):
        sem = "q_" + eng
        self._deps(eng, reads, writes)
        h = self.sem(sem)
        self.counts[sem] += 1
        val = self.counts[sem]
        self.items[eng].append(lambda e, h=h: fn(e).then_inc(h, 1))
        self._notes(reads, writes, sem, val, wide=wide)
        self.n_ops += 1
        return sem, val

    def dma(self, eng, out, in_, reads=(), writes=(), transpose=False,
            sem=None, **kw):
        if sem is None:
            i = self._pool_i.get(eng, 0)
            self._pool_i[eng] = i + 1
            sem = f"dp_{eng}_{i % 10}"
        self._deps(eng, reads, writes)
        h = self.sem(sem)
        self._wait(eng, sem, self.counts[sem])
        self.counts[sem] += 16
        val = self.counts[sem]
        if transpose:
            self.items[eng].append(
                lambda e, h=h: e.dma_start_transpose(out, in_).then_inc(h, 16))
        else:
            self.items[eng].append(
                lambda e, h=h: e.dma_start(out=out, in_=in_, **kw)
                .then_inc(h, 16))
        self._notes(reads, writes, sem, val)
        self.n_ops += 1
        return sem, val

    def barrier(self):
        snapshot = dict(self.counts)
        for e in ENGS:
            for s, v in snapshot.items():
                self._wait(e, s, v)
        self.res = {}

    def build(self):
        block_cm = self.nc.Block()
        block = block_cm.__enter__()
        for e in ENGS:
            ops = self.items[e]
            if not ops:
                continue

            def make(ops):
                def run(eng):
                    for f in ops:
                        f(eng)
                return run
            getattr(block, e)(make(ops))
        block_cm.__exit__(None, None, None)
        for cm in reversed(self._ctxs):
            cm.__exit__(None, None, None)


class Ctx:

    def __init__(self, dbg=()):
        self.nc = bass.Bass("TRN2", target_bir_lowering=False, debug=False)
        self.pb = PB(self.nc)
        self.dbg = set(dbg)
        self.dram = {}
        self.sbuf = {}
        self._sb_ctxs = []
        self.wcache = None
        self.plane_keys = {}

    def din(self, name, shape, dtype=F32):
        ap = self.nc.dram_tensor(name, list(shape), dtype,
                                 kind="ExternalInput").ap()
        self.dram[name] = ap
        return ap

    def dout(self, name, shape, dtype=F32):
        ap = self.nc.dram_tensor(name, list(shape), dtype,
                                 kind="ExternalOutput").ap()
        self.dram[name] = ap
        return ap

    def scratch(self, name, shape, dtype):
        kind = "ExternalOutput" if name in self.dbg else "Internal"
        ap = self.nc.dram_tensor(name, list(shape), dtype, kind=kind).ap()
        self.dram[name] = ap
        return ap

    def sb(self, name, shape, dtype):
        if name in self.sbuf:
            return self.sbuf[name]
        cm = self.nc.sbuf_tensor(name, list(shape), dtype)
        h = cm.__enter__()
        self._sb_ctxs.append(cm)
        self.sbuf[name] = h
        return h

    def ps(self, name, shape=(128, NCHUNK), dtype=F32):
        if name in self.sbuf:
            return self.sbuf[name]
        cm = self.nc.psum_tensor(name, list(shape), dtype)
        h = cm.__enter__()
        self._sb_ctxs.append(cm)
        self.sbuf[name] = h
        return h

    def wkey(self, plane, i):
        k = f"{plane}@{i}"
        ks = self.plane_keys.setdefault(plane, [])
        if k not in ks:
            ks.append(k)
        return k

    def rkeys(self, plane):
        return tuple(self.plane_keys.get(plane, ()))

    def finish(self):
        self.pb.build()
        for cm in reversed(self._sb_ctxs):
            cm.__exit__(None, None, None)
        return self.nc


def get_pool(ctx):
    return dict(
        wbuf=ctx.sb("wbuf", [128, KT * D], BF16),
        xbuf=ctx.sb("xbuf", [128, KT * T], BF16),
        fs0=ctx.sb("fs0", [128, T], F32),
        fs1=ctx.sb("fs1", [128, T], F32),
        fs2=ctx.sb("fs2", [128, T], F32),
        bs0=ctx.sb("bs0", [128, T], BF16),
    )


def FK(buf):
    return tuple(f"{buf}_{j}" for j in range(NT))


def emit_ln(ctx, x_tm, xn_tm, delta_tm=None, x_out_tm=None,
            x_name="x", xn_name="xn_tm", delta_name=None, xo_name=None,
            xnf_tm=None, xnf_name=None, delta_fm_bf=False):
    pb = ctx.pb
    fs = [ctx.sb(f"fs{i}", [128, T], F32) for i in range(5)]
    bs = [ctx.sb(f"bs{i}", [128, T], BF16) for i in range(2)]
    xt = [fs[0], fs[1]]
    dl = [fs[3], fs[4]]
    xnt = [bs[0], bs[1]]
    sq = fs[2]
    mean = ctx.sb("ln_mean", [128, 8], F32)
    ssq = ctx.sb("ln_ssq", [128, 8], F32)
    var = ctx.sb("ln_var", [128, 8], F32)
    rstd = ctx.sb("ln_rstd", [128, 8], F32)
    for t in range(T // 128):
        i = t % 2
        r = slice(t * 128, (t + 1) * 128)
        kx, kn = FK(f"fs{i}"), FK(f"bs{i}")
        kd = FK(f"fs{3 + i}")
        pb.dma("sync", xt[i][:, :], x_tm[r, :], writes=kx,
               reads=ctx.rkeys(x_name))
        if delta_tm is not None:
            if delta_fm_bf:
                dlb = bs[i]
                kdb = FK(f"bs{i}")
                pb.dma("scalar", dlb[:, :], delta_tm[:, r], transpose=True,
                       writes=kdb, reads=ctx.rkeys(delta_name))
                pb.op("vector", lambda e, i=i, dlb=dlb: e.tensor_add(
                    xt[i][:, :], xt[i][:, :], dlb[:, :]),
                    reads=kx + kdb, writes=kx)
            else:
                pb.dma("scalar", dl[i][:, :], delta_tm[r, :], writes=kd,
                       reads=ctx.rkeys(delta_name))
                pb.op("vector", lambda e, i=i: e.tensor_add(
                    xt[i][:, :], xt[i][:, :], dl[i][:, :]),
                    reads=kx + kd, writes=kx)
            if x_out_tm is not None:
                pb.dma("sync", x_out_tm[r, :], xt[i][:, :],
                       reads=kx, writes=(ctx.wkey(xo_name, t),))
        pb.op("vector", lambda e, i=i: e.reduce_sum(
            mean[:, 0:1], xt[i][:, :], axis=AX.X), reads=kx, writes=("ln_m",))
        pb.op("vector", lambda e: e.tensor_scalar_mul(
            mean[:, 0:1], mean[:, 0:1], 1.0 / D), reads=("ln_m",),
            writes=("ln_m",))
        pb.op("scalar", lambda e, i=i: e.activation(
            sq[:, :], xt[i][:, :], AF.Square, accum_out=ssq[:, 0:1]),
            reads=kx, writes=FK("fs2") + ("ln_ss",))
        pb.op("vector", lambda e: e.tensor_mul(
            var[:, 0:1], mean[:, 0:1], mean[:, 0:1]), reads=("ln_m",),
            writes=("ln_v",))
        pb.op("vector", lambda e: e.scalar_tensor_tensor(
            var[:, 0:1], ssq[:, 0:1], 1.0 / D, var[:, 0:1],
            ALU.mult, ALU.subtract),
            reads=("ln_ss", "ln_v"), writes=("ln_v",))
        pb.op("vector", lambda e: e.tensor_scalar_add(
            var[:, 0:1], var[:, 0:1], 1e-5), reads=("ln_v",), writes=("ln_v",))
        pb.op("scalar", lambda e: e.activation(
            rstd[:, 0:1], var[:, 0:1], AF.Sqrt),
            reads=("ln_v",), writes=("ln_r",))
        pb.op("vector", lambda e: e.reciprocal(rstd[:, 0:1], rstd[:, 0:1]),
              reads=("ln_r",), writes=("ln_r",))
        pb.op("vector", lambda e, i=i: e.tensor_scalar(
            xnt[i][:, :], xt[i][:, :], mean[:, 0:1], rstd[:, 0:1],
            ALU.subtract, ALU.mult),
            reads=kx + ("ln_m", "ln_r"), writes=kn)
        if xnf_tm is not None:
            pb.op("vector", lambda e, i=i: e.tensor_scalar(
                sq[:, :], xt[i][:, :], mean[:, 0:1], rstd[:, 0:1],
                ALU.subtract, ALU.mult),
                reads=kx + ("ln_m", "ln_r"), writes=FK("fs2"))
            pb.dma("scalar", xnf_tm[r, :], sq[:, :], reads=FK("fs2"),
                   writes=(ctx.wkey(xnf_name, t),))
        pb.dma("sync", xn_tm[r, :], xnt[i][:, :], reads=kn,
               writes=(ctx.wkey(xn_name, t),))


def emit_load_fm(ctx, dst_sb, dst_key, src_tm, src_name, n_dtiles=KT):
    pb = ctx.pb
    reads = ctx.rkeys(src_name)
    for k in range(n_dtiles):
        pb.dma("sync", dst_sb[:, k * T:(k + 1) * T],
               src_tm[:, k * 128:(k + 1) * 128], transpose=True,
               reads=reads, writes=(f"{dst_key}{k}",))


def emit_load_plane(ctx, dst_sb, dst_key, src_fm, src_name, n_dtiles=KT,
                    q="sync", key_fn=None):
    pb = ctx.pb
    for k in range(n_dtiles):
        rk = key_fn(k) if key_fn else (ctx.rkeys(src_name) if src_name else ())
        pb.dma(q, dst_sb[:, k * T:(k + 1) * T],
               src_fm[k * 128:(k + 1) * 128, :],
               reads=rk, writes=(f"{dst_key}{k}",))


def emit_linear_fm(ctx, wname, W_dram, x_sb, x_key, out_bf=None, out_f32=None,
                   Din=D, Dout=D, bias_sb=None, bias_key=None, bias_off=0,
                   act=None, scale=None, out_name=None, outf_name=None,
                   mt_range=None, w_transpose=False):
    pb = ctx.pb
    kt = Din // 128
    mt = Dout // 128
    assert kt * Dout <= KT * D
    wbuf = ctx.sb("wbuf", [128, KT * D], BF16)
    ev = ctx.sb("fs0", [128, T], F32)
    evb = ctx.sb("bs0", [128, T], BF16)
    kev, kevb = FK("fs0"), FK("bs0")
    ps = [ctx.ps(f"ps{i}") for i in range(7)]
    if ctx.wcache != wname:
        for k in range(kt):
            if w_transpose:
                pb.dma("sync", wbuf[:, k * Dout:(k + 1) * Dout],
                       W_dram[:, k * 128:(k + 1) * 128], transpose=True,
                       writes=(f"wbuf{k}",), reads=ctx.rkeys(wname))
            else:
                pb.dma("gpsimd", wbuf[:, k * Dout:(k + 1) * Dout],
                       W_dram[k * 128:(k + 1) * 128, :], writes=(f"wbuf{k}",),
                       reads=ctx.rkeys(wname))
        ctx.wcache = wname
    for m in (range(mt) if mt_range is None else mt_range):
        pgi = list(range(NT))
        for n in range(NT):
            def mmgroup(e, m=m, n=n):
                last = None
                for k in range(kt):
                    last = e.matmul(
                        ps[pgi[n]][:, :],
                        wbuf[:, k * Dout + m * 128:k * Dout + (m + 1) * 128],
                        x_sb[:, k * T + n * NCHUNK:k * T + (n + 1) * NCHUNK],
                        start=(k == 0), stop=(k == kt - 1))
                return last
            pb.op("tensor", mmgroup,
                  reads=tuple(f"wbuf{k}" for k in range(kt)) +
                  tuple(f"{x_key}{k}" for k in range(kt)),
                  writes=(f"ps{pgi[n]}",))
        dst, dst_key = (evb, "bs0") if out_f32 is None else (ev, "fs0")
        for n in range(NT):
            ns = slice(n * NCHUNK, (n + 1) * NCHUNK)
            rd = (f"ps{pgi[n]}",) + ((bias_key,) if bias_key else ())
            wkk = (kevb if dst is evb else kev)[n:n + 1]
            if act is None and bias_sb is not None:
                def ev_fn(e, m=m, n=n, ns=ns):
                    return e.tensor_scalar(
                        dst[:, ns], ps[pgi[n]][:, :],
                        bias_sb[:, bias_off + m:bias_off + m + 1], None,
                        ALU.add)
                pb.op("vector", ev_fn, reads=rd, writes=wkk)
            else:
                def ev_fn(e, n=n, ns=ns):
                    kw = {}
                    if bias_sb is not None:
                        kw["bias"] = bias_sb[:, bias_off + m:bias_off + m + 1]
                    if scale is not None:
                        kw["scale"] = scale
                    return e.activation(dst[:, ns], ps[pgi[n]][:, :],
                                        act if act is not None else AF.Copy,
                                        **kw)
                pb.op("scalar", ev_fn, reads=rd, writes=wkk)
        mr = slice(m * 128, (m + 1) * 128)
        if out_bf is not None and out_f32 is not None:
            pb.op("vector", lambda e: e.tensor_copy(evb[:, :], ev[:, :]),
                  reads=kev, writes=kevb)
        if out_bf is not None:
            pb.dma("sync", out_bf[mr, :], evb[:, :], reads=kevb,
                   writes=(ctx.wkey(out_name, m),) if out_name else ())
        if out_f32 is not None:
            pb.dma("sync", out_f32[mr, :], ev[:, :], reads=kev,
                   writes=(ctx.wkey(outf_name, m),) if outf_name else ())


def emit_load_bias(ctx, sb, sb_key, b_dram, nt, col_off=0):
    pb = ctx.pb
    b2 = b_dram.rearrange("(m p) -> m p", p=128)
    for m in range(nt):
        pb.dma("sync", sb[:, col_off + m:col_off + m + 1],
               b2[m:m + 1, :].transpose([1, 0]), writes=(sb_key,))


NCONST = 16


def const_col(ctx, value):
    cst = ctx.sb("cst", [128, NCONST], F32)
    if not hasattr(ctx, "const_values"):
        ctx.const_values = []
        cin = ctx.din("cst", [128, NCONST])
        ctx.pb.dma("sync", cst[:, :], cin[:, :], writes=("cst",))
    if value not in ctx.const_values:
        assert len(ctx.const_values) < NCONST
        ctx.const_values.append(value)
    return cst[:, ctx.const_values.index(value):
               ctx.const_values.index(value) + 1]


def const_input(ctx):
    v = getattr(ctx, "const_values", [])
    arr = np.zeros((128, NCONST), np.float32)
    for i, x in enumerate(v):
        arr[:, i] = x
    return arr


def emit_stream_elem(ctx, ins, in_names, outs_f32, outs_bf, n_tmp, body,
                     d_tiles=KT, in_keys=None):
    pb = ctx.pb
    nin, nof, nob = len(ins), len(outs_f32), len(outs_bf)
    fs = [ctx.sb(f"fs{i}", [128, T], F32) for i in range(6)]
    bs = [ctx.sb(f"bs{i}", [128, T], BF16) for i in range(2)]
    slots = [(fs[i], i, j) for i in range(6) for j in range(NT)]
    per = nin + nof + n_tmp
    assert 2 * per <= len(slots), (nin, nof, n_tmp)
    bslots = [(bs[i], i, j) for i in range(2) for j in range(NT)]
    assert 2 * nob <= len(bslots)

    def fkey(s):
        return f"fs{s[1]}_{s[2]}"

    def bkey(s):
        return f"bs{s[1]}_{s[2]}"

    def ap(s, w=NCHUNK):
        return s[0][:, s[2] * NCHUNK:s[2] * NCHUNK + w]

    ci = 0
    for d in range(d_tiles):
        rr = slice(d * 128, (d + 1) * 128)
        for c in range(NT):
            cc = slice(c * NCHUNK, (c + 1) * NCHUNK)
            par = ci % 2
            isl = slots[par * per:par * per + nin]
            osl = slots[par * per + nin:par * per + nin + nof]
            tsl = slots[par * per + nin + nof:par * per + per]
            obl = bslots[par * nob:(par + 1) * nob]
            for i, pl in enumerate(ins):
                q = "sync" if i % 2 == 0 else "scalar"
                rk = (in_keys[i](d, c, ci) if in_keys and in_keys[i]
                      else ctx.rkeys(in_names[i]))
                pb.dma(q, ap(isl[i]), pl[rr, cc],
                       reads=rk, writes=(fkey(isl[i]),))
            body(pb, [ap(s) for s in isl], [ap(s) for s in osl],
                 [ap(s) for s in tsl], [ap(s) for s in obl],
                 [fkey(s) for s in isl], [fkey(s) for s in osl],
                 [fkey(s) for s in tsl], [bkey(s) for s in obl],
                 (d, c, ci, "vector"))
            for i, (pl, nm) in enumerate(outs_f32):
                pb.dma("sync", pl[rr, cc], ap(osl[i]), reads=(fkey(osl[i]),),
                       writes=(ctx.wkey(nm, ci),))
            for i, (pl, nm) in enumerate(outs_bf):
                pb.dma("scalar", pl[rr, cc], ap(obl[i]), reads=(bkey(obl[i]),),
                       writes=(ctx.wkey(nm, ci),))
            ci += 1


def emit_l0_elemA(ctx, FQr, FQi, FKr, FKi, FVr, FVi, FXr, FXi, XHr, XHi):
    pb = ctx.pb
    carry = ctx.sb("carryRI", [128, 8], F32)

    def body(pb, ib, ob, tb, bb, ik, ok, tk, bk, info):
        d, c, ci, eng = info
        qr, qi, kr, ki, vr, vi, xr, xi = ib
        xhr, xhi = ob
        t0, t1 = tb
        kqr, kqi, kkr, kki, kvr, kvi, kxr, kxi = ik
        kxhr, kxhi = ok
        kt0, kt1 = tk
        def o(eng, fn, reads=(), writes=(), wide=True):
            pb.op(eng, fn, reads=reads, writes=writes, wide=wide)
        o(eng, lambda e: e.tensor_mul(t0, qr, kr), reads=(kqr, kkr), writes=(kt0,))
        o(eng, lambda e: e.tensor_mul(t1, qi, ki), reads=(kqi, kki), writes=(kt1,), wide=True)
        o(eng, lambda e: e.tensor_add(t0, t0, t1), reads=(kt0, kt1), writes=(kt0,))
        o(eng, lambda e: e.tensor_mul(t1, qr, ki), reads=(kqr, kki), writes=(kt1,), wide=True)
        o(eng, lambda e: e.tensor_mul(xhr, qi, kr), reads=(kqi, kkr), writes=(kxhr,))
        o(eng, lambda e: e.tensor_sub(t1, t1, xhr), reads=(kt1, kxhr), writes=(kt1,), wide=True)
        o(eng, lambda e: e.tensor_mul(xhr, t0, vr), reads=(kt0, kvr), writes=(kxhr,))
        o(eng, lambda e: e.tensor_mul(xhi, t1, vi), reads=(kt1, kvi), writes=(kxhi,), wide=True)
        o(eng, lambda e: e.tensor_sub(xhr, xhr, xhi), reads=(kxhr, kxhi), writes=(kxhr,))
        o(eng, lambda e: e.tensor_mul(t0, t0, vi), reads=(kt0, kvi), writes=(kt0,), wide=True)
        o(eng, lambda e: e.tensor_mul(xhi, t1, vr), reads=(kt1, kvr), writes=(kxhi,))
        o(eng, lambda e: e.tensor_add(t0, t0, xhi), reads=(kt0, kxhi), writes=(kt0,), wide=True)
        if c == 0:
            o("vector", lambda e: e.tensor_tensor_scan(
                xhr, xhr, xhr, 0.0, ALU.add, ALU.bypass), reads=(kxhr,), writes=(kxhr,), wide=False)
            o("vector", lambda e: e.tensor_tensor_scan(
                t0, t0, t0, 0.0, ALU.add, ALU.bypass), reads=(kt0,), writes=(kt0,), wide=False)
        else:
            o("vector", lambda e: e.tensor_tensor_scan(
                xhr, xhr, xhr, carry[:, 0:1], ALU.add, ALU.bypass),
                reads=(kxhr, "carry"), writes=(kxhr,), wide=False)
            o("vector", lambda e: e.tensor_tensor_scan(
                t0, t0, t0, carry[:, 1:2], ALU.add, ALU.bypass),
                reads=(kt0, "carry"), writes=(kt0,), wide=False)
        o("vector", lambda e: e.tensor_copy(carry[:, 0:1], xhr[:, NCHUNK - 1:NCHUNK]),
          reads=(kxhr,), writes=("carry",), wide=False)
        o("vector", lambda e: e.tensor_copy(carry[:, 1:2], t0[:, NCHUNK - 1:NCHUNK]),
          reads=(kt0,), writes=("carry",), wide=False)
        o("vector", lambda e: e.tensor_add(xhr, xr, xhr), reads=(kxr, kxhr), writes=(kxhr,))
        o("vector", lambda e: e.tensor_add(xhi, xi, t0), reads=(kxi, kt0), writes=(kxhi,), wide=True)

    nm = ["FQr", "FQi", "FKr", "FKi", "FVr", "FVi", "FXr", "FXi"]
    emit_stream_elem(ctx, [FQr, FQi, FKr, FKi, FVr, FVi, FXr, FXi], nm,
                     [(XHr, "XHr"), (XHi, "XHi")], [], 2, body, d_tiles=MH,
                     in_keys=[(lambda d, c, ci, n=n: (f"{n}@{d}",))
                              for n in nm])


def emit_l0_elemB(ctx, FSr, FSi, XHr, XHi, G, TAr, TAi):
    pb = ctx.pb

    def body(pb, ib, ob, tb, bb, ik, ok, tk, bk, info):
        eng = info[3]
        sr, si, xhr, xhi = ib
        (g,) = ob
        t0, t1 = tb
        tar, tai = bb
        ksr, ksi, kxhr, kxhi = ik
        (kg,) = ok
        kt0, kt1 = tk
        ktar, ktai = bk
        def o(eng, fn, reads=(), writes=(), wide=True):
            pb.op(eng, fn, reads=reads, writes=writes, wide=wide)
        o(eng, lambda e: e.tensor_mul(t0, sr, xhr), reads=(ksr, kxhr), writes=(kt0,))
        o(eng, lambda e: e.tensor_mul(t1, si, xhi), reads=(ksi, kxhi), writes=(kt1,), wide=True)
        o(eng, lambda e: e.tensor_add(tar, t0, t1), reads=(kt0, kt1), writes=(ktar,), wide=True)
        o(eng, lambda e: e.tensor_mul(t0, sr, xhi), reads=(ksr, kxhi), writes=(kt0,))
        o(eng, lambda e: e.tensor_mul(t1, si, xhr), reads=(ksi, kxhr), writes=(kt1,), wide=True)
        o(eng, lambda e: e.tensor_sub(tai, t0, t1), reads=(kt0, kt1), writes=(ktai,), wide=True)
        o(eng, lambda e: e.tensor_mul(t0, sr, sr), reads=(ksr,), writes=(kt0,))
        o(eng, lambda e: e.tensor_mul(t1, si, si), reads=(ksi,), writes=(kt1,), wide=True)
        o(eng, lambda e: e.tensor_add(g, t0, t1), reads=(kt0, kt1), writes=(kg,), wide=True)

    emit_stream_elem(ctx, [FSr, FSi, XHr, XHi], ["FSr", "FSi", "XHr", "XHi"],
                     [(G, "G")], [(TAr, "TAr"), (TAi, "TAi")], 2, body,
                     d_tiles=MH,
                     in_keys=[lambda d, c, ci: (f"FSr@{d}",),
                              lambda d, c, ci: (f"FSi@{d}",),
                              lambda d, c, ci: (f"XHr@{ci}",),
                              lambda d, c, ci: (f"XHi@{ci}",)])


def dft_constants():
    d = np.arange(D, dtype=np.float64)
    ang = np.outer(d, d) * (2.0 * np.pi / D)
    return np.cos(ang).astype(np.float32), (-np.sin(ang)).astype(np.float32)


MH = 9


def emit_mirror(ctx, plane, name, neg, src_bf=False):
    pb = ctx.pb
    pool = get_pool(ctx)
    xbuf = pool["xbuf"]
    p1 = ctx.sb("p1m", [128, 128], BF16)
    identb = ctx.sb("identb", [128, 128], BF16)
    if getattr(ctx, "_p1_loaded", False) is False:
        pb.dma("gpsimd", p1[:, :], ctx.dram["P1m"][:, :], writes=("p1m",))
        pb.dma("gpsimd", identb[:, :], ctx.dram["ident"][:, :],
               writes=("identb",))
        ctx._p1_loaded = True
    ps = [ctx.ps(f"ps{i}") for i in range(7)]
    if src_bf:
        ev = ctx.sb("bs0", [128, T], BF16)
        kev = FK("bs0")
    else:
        ev = ctx.sb("fs0", [128, T], F32)
        kev = FK("fs0")
    q = "sync" if src_bf else "gpsimd"
    for k in range(8):
        pb.dma(q, xbuf[:, k * T:(k + 1) * T], plane[k * 128:(k + 1) * 128, :],
               reads=ctx.rkeys(name), writes=(f"xbuf{k}",))
    for j in range(7):
        a, b = 6 - j, 7 - j
        for n in range(NT):
            ns = slice(n * NCHUNK, (n + 1) * NCHUNK)
            def mgroup(e, j=j, a=a, b=b, n=n):
                e.matmul(ps[n][:, :], p1[:, :],
                         xbuf[:, a * T + n * NCHUNK:a * T + (n + 1) * NCHUNK],
                         start=True, stop=False)
                return e.matmul(
                    ps[n][:, :], identb[0:1, :],
                    xbuf[0:1, b * T + n * NCHUNK:b * T + (n + 1) * NCHUNK],
                    start=False, stop=True)
            pb.op("tensor", mgroup,
                  reads=(f"xbuf{a}", f"xbuf{b}", "p1m", "identb"),
                  writes=(f"ps{n}",))
            pb.op("vector", lambda e, n=n, ns=ns: e.tensor_scalar(
                ev[:, ns], ps[n][:, :], -1.0 if neg else 1.0, None, ALU.mult),
                reads=(f"ps{n}",), writes=(kev[n],))
        pb.dma("sync", plane[(9 + j) * 128:(10 + j) * 128, :], ev[:, :],
               reads=kev, writes=(ctx.wkey(name, f"M{9 + j}"),))


def emit_fft_pass(ctx, wnm, pl, suffix, srcname):
    pb = ctx.pb
    pool = get_pool(ctx)
    xbuf = pool["xbuf"]
    out = ctx.scratch(pl + suffix, [D, T], F32)
    if srcname is None:
        emit_load_fm(ctx, xbuf, "xbuf", ctx.dram["xn_tm"], "xn_tm")
    else:
        emit_load_plane(ctx, xbuf, "xbuf", ctx.dram[srcname], srcname)
    emit_linear_fm(ctx, wnm, ctx.dram[wnm], xbuf, "xbuf", out_f32=out,
                   outf_name=pl + suffix, mt_range=range(MH))


def emit_l0_front(ctx, x_tm):
    pb = ctx.pb
    pool = get_pool(ctx)
    xbuf = pool["xbuf"]
    xn_tm = ctx.scratch("xn_tm", [T, D], BF16)
    emit_ln(ctx, x_tm, xn_tm)
    planes = {}
    for nm in ("q0", "k0", "v0", "s0", "t0"):
        planes[nm] = ctx.scratch(nm, [D, T], BF16)
    emit_load_fm(ctx, xbuf, "xbuf", xn_tm, "xn_tm")
    for nm, wn in (("q0", "L0_attn_Wq"), ("k0", "L0_attn_Wk"),
                   ("v0", "L0_attn_Wv"), ("s0", "L0_attn_Ws"),
                   ("t0", "L0_attn_Wt")):
        emit_linear_fm(ctx, wn, ctx.dram[wn], xbuf, "xbuf",
                       out_bf=planes[nm], out_name=nm)


def emit_load_plane_w(ctx, dst_sb, dst_key, src_fm, src_name, ktiles,
                      tok0, w, k0=0, dst_slot0=0, q="sync"):
    pb = ctx.pb
    for k in range(ktiles):
        pb.dma(q, dst_sb[:, (dst_slot0 + k) * w:(dst_slot0 + k + 1) * w],
               src_fm[(k0 + k) * 128:(k0 + k + 1) * 128, tok0:tok0 + w],
               reads=ctx.rkeys(src_name), writes=(f"{dst_key}{dst_slot0 + k}",))


def emit_linear_cols(ctx, wname, W_dram, rhs_fn, rhs_keys, Din, Dout,
                     out_specs, tok0=0, tokw=T, bias_sb=None, bias_key=None,
                     bias_off=0, act=None, scale=None):
    pb = ctx.pb
    kt = Din // 128
    mt = Dout // 128
    ntc = tokw // NCHUNK
    wcol = ctx.sb("wcol", [128, 8192], BF16)
    dbuf_w = kt <= 32
    ev = ctx.sb("fs0", [128, T], F32)
    evb = ctx.sb("bs0", [128, T], BF16)
    kev, kevb = FK("fs0"), FK("bs0")
    ps = [ctx.ps(f"ps{i}") for i in range(7)]
    for m in range(mt):
        half = (m % 2) if dbuf_w else 0
        woff = half * kt * 128
        wkeys = []
        for k0 in range(0, kt, 8):
            kn = min(8, kt - k0)
            src = W_dram[k0 * 128:(k0 + kn) * 128, m * 128:(m + 1) * 128]
            src3 = src.rearrange("(j p) m -> p j m", p=128)
            dst = wcol[:, woff + k0 * 128:woff + (k0 + kn) * 128]
            dst3 = dst.rearrange("p (j m) -> p j m", m=128)
            kk = f"wcol{half}_{k0}"
            pb.dma("gpsimd", dst3, src3, reads=ctx.rkeys(wname), writes=(kk,))
            wkeys.append(kk)
        pgi = list(range(ntc))
        for n in range(ntc):
            def mmgroup(e, m=m, n=n, woff=woff):
                last = None
                for k in range(kt):
                    last = e.matmul(
                        ps[pgi[n]][:, :],
                        wcol[:, woff + k * 128:woff + (k + 1) * 128],
                        rhs_fn(k, n),
                        start=(k == 0), stop=(k == kt - 1))
                return last
            pb.op("tensor", mmgroup,
                  reads=tuple(wkeys) + tuple(rhs_keys(n)),
                  writes=(f"ps{pgi[n]}",))
        only_bf = all(s[3] for s in out_specs)
        dst, dst_key = (evb, kevb) if only_bf else (ev, kev)
        for n in range(ntc):
            ns = slice(n * NCHUNK, (n + 1) * NCHUNK)
            rd = (f"ps{pgi[n]}",) + ((bias_key,) if bias_key else ())
            if act is None and bias_sb is not None:
                def ev_fn(e, m=m, n=n, ns=ns):
                    return e.tensor_scalar(
                        dst[:, ns], ps[pgi[n]][:, :],
                        bias_sb[:, bias_off + m:bias_off + m + 1], None,
                        ALU.add)
                pb.op("vector", ev_fn, reads=rd, writes=dst_key[n:n + 1])
            else:
                def ev_fn(e, m=m, n=n, ns=ns):
                    kw = {}
                    if bias_sb is not None:
                        kw["bias"] = bias_sb[:, bias_off + m:bias_off + m + 1]
                    if scale is not None:
                        kw["scale"] = scale
                    return e.activation(dst[:, ns], ps[pgi[n]][:, :],
                                        act if act is not None else AF.Copy,
                                        **kw)
                pb.op("scalar", ev_fn, reads=rd, writes=dst_key[n:n + 1])
        for (plane, pname, is_bf, m_lo, m_hi) in out_specs:
            if not (m_lo <= m < m_hi):
                continue
            mr = slice((m - m_lo) * 128, (m - m_lo + 1) * 128)
            src_b = dst if (is_bf == only_bf or is_bf) else ev
            if is_bf and not only_bf:
                pb.op("vector", lambda e: e.tensor_copy(
                    evb[:, :tokw], ev[:, :tokw]), reads=kev, writes=kevb)
                src_b = evb
            pb.dma("sync", plane[mr, tok0:tok0 + tokw], src_b[:, :tokw],
                   reads=(kevb if is_bf else kev),
                   writes=(ctx.wkey(pname, f"{m}_{tok0}"),))


def emit_l0_mlp(ctx):
    pb = ctx.pb
    pool = get_pool(ctx)
    xbuf, wbuf = pool["xbuf"], pool["wbuf"]
    bias = ctx.sb("bias", [128, 128], F32)
    emit_load_bias(ctx, bias, "bias", ctx.dram["L0_attn_b1"], 64, 0)
    emit_load_bias(ctx, bias, "bias", ctx.dram["L0_attn_b2"], 32, 64)
    h1 = ctx.scratch("h1", [8192, T], BF16)
    Hr = ctx.scratch("Hr", [D, T], F32)
    Hi = ctx.scratch("Hi", [D, T], F32)
    ctx.wcache = None
    emit_load_plane(ctx, xbuf, "xbuf", ctx.dram["TAr"], "TAr")
    emit_load_plane(ctx, wbuf, "wbuf", ctx.dram["TAi"], "TAi")

    def rhs1(k, n):
        buf = xbuf if k < KT else wbuf
        kk = k % KT
        return buf[:, kk * T + n * NCHUNK:kk * T + (n + 1) * NCHUNK]

    def rkeys1(n):
        return tuple(f"xbuf{k}" for k in range(KT)) + \
            tuple(f"wbuf{k}" for k in range(KT))

    emit_linear_cols(ctx, "L0_attn_W1", ctx.dram["L0_attn_W1"], rhs1, rkeys1,
                     2 * D, 4 * D, [(h1, "h1", True, 0, 64)],
                     bias_sb=bias, bias_key="bias", bias_off=0, act=AF.Relu)
    pb.barrier()
    HW = 1024
    for hi in range(2):
        t0 = hi * HW
        emit_load_plane_w(ctx, xbuf, "xbuf", h1, "h1", 32, t0, HW)
        emit_load_plane_w(ctx, wbuf, "wbuf", h1, "h1", 32, t0, HW, k0=32)

        def rhs2(k, n, t0=t0):
            buf = xbuf if k < 32 else wbuf
            kk = k % 32
            return buf[:, kk * HW + n * NCHUNK:kk * HW + (n + 1) * NCHUNK]

        def rkeys2(n):
            return tuple(f"xbuf{k}" for k in range(32)) + \
                tuple(f"wbuf{k}" for k in range(32))

        emit_linear_cols(ctx, "L0_attn_W2", ctx.dram["L0_attn_W2"], rhs2,
                         rkeys2, 4 * D, 2 * D,
                         [(Hr, "Hr", False, 0, 16), (Hi, "Hi", False, 16, 32)],
                         tok0=t0, tokw=HW,
                         bias_sb=bias, bias_key="bias", bias_off=64)


def emit_l0_combine(ctx, XHr, XHi, G, FTr, FTi, Hr, Hi, XHnR, XHnI):
    def body(pb, ib, ob, tb, bb, ik, ok, tk, bk, info):
        eng = info[3]
        xhr, xhi, g, ftr, fti, hr, hi = ib
        t0, t1, t2 = tb
        onr, oni = bb
        kxhr, kxhi, kg, kftr, kfti, khr, khi = ik
        kt0, kt1, kt2 = tk
        konr, koni = bk
        def o(eng, fn, reads=(), writes=(), wide=True):
            pb.op(eng, fn, reads=reads, writes=writes, wide=wide)
        o(eng, lambda e: e.tensor_mul(t0, ftr, hr), reads=(kftr, khr), writes=(kt0,))
        o(eng, lambda e: e.tensor_mul(t1, fti, hi), reads=(kfti, khi), writes=(kt1,), wide=True)
        o(eng, lambda e: e.tensor_sub(t0, t0, t1), reads=(kt0, kt1), writes=(kt0,))
        o(eng, lambda e: e.tensor_mul(t1, g, xhr), reads=(kg, kxhr), writes=(kt1,), wide=True)
        o(eng, lambda e: e.tensor_sub(t2, xhr, t1), reads=(kxhr, kt1), writes=(kt2,))
        o(eng, lambda e: e.tensor_add(onr, t2, t0), reads=(kt2, kt0), writes=(konr,), wide=True)
        o(eng, lambda e: e.tensor_mul(t0, ftr, hi), reads=(kftr, khi), writes=(kt0,))
        o(eng, lambda e: e.tensor_mul(t1, fti, hr), reads=(kfti, khr), writes=(kt1,), wide=True)
        o(eng, lambda e: e.tensor_add(t0, t0, t1), reads=(kt0, kt1), writes=(kt0,))
        o(eng, lambda e: e.tensor_mul(t1, g, xhi), reads=(kg, kxhi), writes=(kt1,), wide=True)
        o(eng, lambda e: e.tensor_sub(t2, xhi, t1), reads=(kxhi, kt1), writes=(kt2,))
        o(eng, lambda e: e.tensor_add(oni, t2, t0), reads=(kt2, kt0), writes=(koni,), wide=True)

    emit_stream_elem(ctx, [XHr, XHi, G, FTr, FTi, Hr, Hi],
                     ["XHr", "XHi", "G", "FTr", "FTi", "Hr", "Hi"],
                     [], [(XHnR, "XHnR"), (XHnI, "XHnI")], 3, body,
                     in_keys=[
                         lambda d, c, ci: (f"XHr@{ci}",) if d < MH
                         else (f"XHr@M{d}",),
                         lambda d, c, ci: (f"XHi@{ci}",) if d < MH
                         else (f"XHi@M{d}",),
                         lambda d, c, ci: (f"G@{ci}",) if d < MH
                         else (f"G@M{d}",),
                         lambda d, c, ci: (f"FTr@{d}",) if d < MH
                         else (f"FTr@M{d}",),
                         lambda d, c, ci: (f"FTi@{d}",) if d < MH
                         else (f"FTi@M{d}",),
                         lambda d, c, ci: (f"Hr@{d}_{0 if c < 2 else 1024}",),
                         lambda d, c, ci: (f"Hi@{d}_{0 if c < 2 else 1024}",)])


def emit_l0_moe(ctx, moe_out):
    pb = ctx.pb
    pool = get_pool(ctx)
    xbuf = pool["xbuf"]
    ps = [ctx.ps(f"ps{i}") for i in range(7)]
    pst = ctx.ps("pst", [128, 512], BF16)
    identb = ctx.sb("identb", [128, 128], BF16)
    pb.dma("gpsimd", identb[:, :], ctx.dram["ident"][:, :], writes=("identb",))
    bs0 = ctx.sb("bs0", [128, T], BF16)
    bs1 = ctx.sb("bs1", [128, T], BF16)
    rnat = bs0
    rT = ctx.sb("rT", [128, 8 * KT], BF16)
    pb.dma("gpsimd", rnat[0:8, :], ctx.dram["L0_ffn_router_w"][:, :],
           writes=FK("bs0"))
    for k in range(KT):
        pb.op("tensor", lambda e, k=k: e.transpose(
            pst[0:128, 0:8], rnat[0:8, k * 128:(k + 1) * 128],
            identb[0:8, 0:8]),
            reads=FK("bs0") + ("identb",), writes=("pst",))
        pb.op("vector", lambda e, k=k: e.tensor_copy(
            rT[:, 8 * k:8 * k + 8], pst[0:128, 0:8]),
            reads=("pst",), writes=("rT",))
    dnat = bs1
    dT = ctx.sb("dT", [128, KT * 128], BF16)
    dn_v = ctx.dram["L0_ffn_down_w"].rearrange("e i d -> (e i) d")
    pb.dma("gpsimd", dnat[:, :], dn_v, writes=FK("bs1"))
    for k in range(KT):
        pb.op("tensor", lambda e, k=k: e.transpose(
            pst[:, 0:128], dnat[:, k * 128:(k + 1) * 128], identb[:, :]),
            reads=FK("bs1") + ("identb",), writes=("pst",))
        pb.op("vector", lambda e, k=k: e.tensor_copy(
            dT[:, k * 128:(k + 1) * 128], pst[:, 0:128]),
            reads=("pst",), writes=("dT",))
    Eb = ctx.sb("Eb", [8, 128], BF16)
    pb.dma("gpsimd", Eb[:, :], ctx.dram["Emoe"][:, :], writes=("Eb",))
    g8 = bs1
    lgt = ctx.sb("lgt", [128, 8], F32)
    lm = ctx.sb("lm", [128, 8], F32)
    top8 = ctx.sb("ltop", [128, 8], F32)
    gsum = ctx.sb("gsum", [128, 8], F32)
    gw = ctx.sb("gw", [128, 8], F32)
    gb = ctx.sb("gb", [128, 8], BF16)
    for tc in range(T // 128):
        def lgroup(e, tc=tc):
            last = None
            for k in range(KT):
                last = e.matmul(ps[6][:, 0:8],
                                xbuf[:, k * T + tc * 128:k * T + (tc + 1) * 128],
                                rT[:, 8 * k:8 * k + 8],
                                start=(k == 0), stop=(k == KT - 1))
            return last
        pb.op("tensor", lgroup,
              reads=tuple(f"xbuf{k}" for k in range(KT)) + ("rT",),
              writes=("ps6",))
        pb.op("vector", lambda e: e.tensor_copy(lgt[:, :], ps[6][:, 0:8]),
              reads=("ps6",), writes=("lgt",))
        pb.op("vector", lambda e: e.reduce_max(lm[:, 0:1], lgt[:, :], axis=AX.X),
              reads=("lgt",), writes=("lm",))
        pb.op("vector", lambda e: e.tensor_scalar_mul(lm[:, 0:1], lm[:, 0:1], -1.0),
              reads=("lm",), writes=("lm",))
        pb.op("scalar", lambda e: e.activation(gw[:, :], lgt[:, :], AF.Exp,
                                               bias=lm[:, 0:1],
                                               accum_out=gsum[:, 0:1]),
              reads=("lgt", "lm"), writes=("gw", "gsum"))
        pb.op("vector", lambda e: e.reciprocal(gsum[:, 1:2], gsum[:, 0:1]),
              reads=("gsum",), writes=("gsum",))
        pb.op("vector", lambda e: e.tensor_scalar(gw[:, :], gw[:, :],
                                                  gsum[:, 1:2], None, ALU.mult),
              reads=("gw", "gsum"), writes=("gw",))
        pb.op("vector", lambda e: e.max(top8[:, :], gw[:, :]),
              reads=("gw",), writes=("ltop",))
        pb.op("vector", lambda e: e.tensor_scalar(lgt[:, :], gw[:, :],
                                                  top8[:, 1:2], None, ALU.is_ge),
              reads=("gw", "ltop"), writes=("lgt",))
        pb.op("vector", lambda e: e.tensor_mul(gb[:, :], gw[:, :], lgt[:, :]),
              reads=("gw", "lgt"), writes=("gb",))
        pb.op("tensor", lambda e, tc=tc: e.transpose(
            pst[0:8, 0:128], gb[:, :], identb[:, :]),
            reads=("gb", "identb"), writes=("pst",))
        pb.op("vector", lambda e, tc=tc: e.tensor_copy(
            g8[0:8, tc * 128:(tc + 1) * 128], pst[0:8, 0:128]),
            reads=("pst",), writes=(FK("bs1")[tc // 4],))
    gate128 = bs0
    kg128 = FK("bs0")
    for n in range(NT):
        ns = slice(n * NCHUNK, (n + 1) * NCHUNK)
        pb.op("tensor", lambda e, ns=ns: e.matmul(
            ps[5][:, :], Eb[:, :], g8[0:8, ns], start=True, stop=True),
            reads=("Eb",) + FK("bs1"), writes=("ps5",))
        pb.op("vector", lambda e, n=n, ns=ns: e.tensor_copy(
            gate128[:, ns], ps[5][:, :]), reads=("ps5",), writes=(kg128[n],))
    av = ctx.sb("fs4", [128, T], F32)
    ag = bs1
    kav, kag = FK("fs4"), FK("bs1")
    for n in range(NT):
        ns = slice(n * NCHUNK, (n + 1) * NCHUNK)
        def dgroup(e, n=n, ns=ns):
            last = None
            for k in range(KT):
                last = e.matmul(ps[n][:, :], dT[:, k * 128:(k + 1) * 128],
                                xbuf[:, k * T + n * NCHUNK:k * T + (n + 1) * NCHUNK],
                                start=(k == 0), stop=(k == KT - 1))
            return last
        pb.op("tensor", dgroup,
              reads=("dT",) + tuple(f"xbuf{k}" for k in range(KT)),
              writes=(f"ps{n}",))
        pb.op("scalar", lambda e, n=n, ns=ns: e.activation(
            av[:, ns], ps[n][:, :], AF.Silu),
            reads=(f"ps{n}",), writes=(kav[n],))
        pb.op("vector", lambda e, n=n, ns=ns: e.tensor_mul(
            ag[:, ns], av[:, ns], gate128[:, ns]),
            reads=(kav[n], kg128[n]), writes=(kag[n],))
    uT = dT
    up_v = ctx.dram["L0_ffn_up_w"].rearrange("e d i -> d e i")
    uv = bs0[:, 0:128]
    for k in range(KT):
        pb.dma("gpsimd", uv.rearrange("p (e i) -> p e i", i=16),
               up_v[k * 128:(k + 1) * 128, :, :], writes=("bs0_0",))
        pb.op("tensor", lambda e, k=k: e.transpose(
            pst[:, 0:128], uv[:, :], identb[:, :]),
            reads=("bs0_0", "identb"), writes=("pst",))
        pb.op("vector", lambda e, k=k: e.tensor_copy(
            uT[:, k * 128:(k + 1) * 128], pst[:, 0:128]),
            reads=("pst",), writes=("dT",))
    ev = ctx.sb("fs5", [128, T], F32)
    kev = FK("fs5")
    for m in range(KT):
        for n in range(NT):
            ns = slice(n * NCHUNK, (n + 1) * NCHUNK)
            pb.op("tensor", lambda e, m=m, n=n, ns=ns: e.matmul(
                ps[4 + (m % 2)][:, :], uT[:, m * 128:(m + 1) * 128],
                ag[:, ns], start=True, stop=True),
                reads=("dT",) + kag, writes=(f"ps{4 + (m % 2)}",))
            pb.op("vector", lambda e, m=m, n=n, ns=ns: e.tensor_copy(
                ev[:, ns], ps[4 + (m % 2)][:, :]),
                reads=(f"ps{4 + (m % 2)}",), writes=(kev[n],))
        pb.dma("sync", moe_out[m * 128:(m + 1) * 128, :], ev[:, :],
               reads=kev, writes=(ctx.wkey("moe_out", m),))


def emit_l0_delta(ctx, A1, A2, MOE, delta):
    def body(pb, ib, ob, tb, bb, ik, ok, tk, bk, info):
        eng = info[3]
        a1, a2, mo = ib
        (dl,) = bb
        ka1, ka2, kmo = ik
        (kdl,) = bk
        def o(eng, fn, reads=(), writes=(), wide=True):
            pb.op(eng, fn, reads=reads, writes=writes, wide=wide)
        (t0,) = tb
        (kt0,) = tk
        o(eng, lambda e: e.tensor_add(t0, a1, a2), reads=(ka1, ka2), writes=(kt0,), wide=True)
        o(eng, lambda e: e.tensor_scalar(t0, t0, 1.0 / D, None, ALU.mult),
          reads=(kt0,), writes=(kt0,))
        o(eng, lambda e: e.tensor_add(dl, t0, mo), reads=(kt0, kmo), writes=(kdl,), wide=True)

    emit_stream_elem(ctx, [A1, A2, MOE], ["A1", "A2", "moe_out"],
                     [], [(delta, "delta")], 1, body,
                     in_keys=[lambda d, c, ci: (f"A1@{d}",),
                              lambda d, c, ci: (f"A2@{d}",),
                              lambda d, c, ci: (f"moe_out@{d}",)])


def emit_transpose_f32(ctx, src_fm, src_name, dst_tm, dst_name):
    pb = ctx.pb
    identf = ctx.sb("identf", [128, 128], F32)
    pb.dma("sync", identf[:, :], ctx.dram["ident"][:, :], writes=("identf",))
    stage = ctx.sb("fs0", [128, T], F32)
    tin = [ctx.sb(f"tt{i}", [128, 128], F32) for i in range(2)]
    kst = FK("fs0")
    ps = ctx.ps("ps0")
    for tt in range(T // 128):
        for dt in range(KT):
            i = dt % 2
            pb.dma("sync", tin[i][:, :],
                   src_fm[dt * 128:(dt + 1) * 128, tt * 128:(tt + 1) * 128],
                   reads=ctx.rkeys(src_name), writes=(f"tt{i}",))
            pb.op("tensor", lambda e, i=i: e.transpose(
                ps[:, 0:128], tin[i][:, :], identf[:, :]),
                reads=(f"tt{i}", "identf"), writes=("ps0",))
            pb.op("vector", lambda e, dt=dt: e.tensor_copy(
                stage[:, dt * 128:(dt + 1) * 128], ps[:, 0:128]),
                reads=("ps0",), writes=(kst[dt // 4],))
        pb.dma("sync", dst_tm[tt * 128:(tt + 1) * 128, :], stage[:, :],
               reads=kst, writes=(ctx.wkey(dst_name, tt),))


def emit_l0(ctx, x_tm, delta_out):
    pb = ctx.pb
    pool = get_pool(ctx)
    xbuf = pool["xbuf"]
    emit_l0_front(ctx, x_tm)
    moe_out = ctx.scratch("moe_out", [D, T], F32)
    emit_l0_moe(ctx, moe_out)
    for pl, srcn in (("FX", None), ("FQ", "q0"), ("FK", "k0"),
                     ("FV", "v0"), ("FS", "s0"), ("FT", "t0")):
        emit_fft_pass(ctx, "Cdft", pl, "r", srcn)
    for pl, srcn in (("FX", None), ("FQ", "q0"), ("FK", "k0"),
                     ("FV", "v0")):
        emit_fft_pass(ctx, "NSdft", pl, "i", srcn)
    XHr = ctx.scratch("XHr", [D, T], F32)
    XHi = ctx.scratch("XHi", [D, T], F32)
    emit_l0_elemA(ctx, ctx.dram["FQr"], ctx.dram["FQi"], ctx.dram["FKr"],
                  ctx.dram["FKi"], ctx.dram["FVr"], ctx.dram["FVi"],
                  ctx.dram["FXr"], ctx.dram["FXi"], XHr, XHi)
    emit_fft_pass(ctx, "NSdft", "FS", "i", "s0")
    G = ctx.scratch("G", [D, T], F32)
    TAr = ctx.scratch("TAr", [D, T], BF16)
    TAi = ctx.scratch("TAi", [D, T], BF16)
    emit_l0_elemB(ctx, ctx.dram["FSr"], ctx.dram["FSi"], XHr, XHi,
                  G, TAr, TAi)
    emit_fft_pass(ctx, "NSdft", "FT", "i", "t0")
    pb.barrier()
    emit_mirror(ctx, TAr, "TAr", False, src_bf=True)
    emit_mirror(ctx, TAi, "TAi", True, src_bf=True)
    emit_mirror(ctx, XHr, "XHr", False)
    emit_mirror(ctx, XHi, "XHi", True)
    emit_mirror(ctx, G, "G", False)
    emit_mirror(ctx, ctx.dram["FTr"], "FTr", False)
    emit_mirror(ctx, ctx.dram["FTi"], "FTi", True)
    pb.barrier()
    emit_l0_mlp(ctx)
    pb.barrier()
    XHnR = ctx.scratch("XHnR", [D, T], BF16)
    XHnI = ctx.scratch("XHnI", [D, T], BF16)
    emit_l0_combine(ctx, ctx.dram["XHr"], ctx.dram["XHi"], ctx.dram["G"],
                    ctx.dram["FTr"], ctx.dram["FTi"], ctx.dram["Hr"],
                    ctx.dram["Hi"], XHnR, XHnI)
    ctx.wcache = None
    A1 = ctx.scratch("A1", [D, T], F32)
    A2 = ctx.scratch("A2", [D, T], F32)
    emit_load_plane(ctx, xbuf, "xbuf", XHnR, "XHnR",
                    key_fn=lambda k: tuple(f"XHnR@{4 * k + j}" for j in range(4)))
    emit_linear_fm(ctx, "Cdft", ctx.dram["Cdft"], xbuf, "xbuf", out_f32=A1,
                   outf_name="A1")
    emit_load_plane(ctx, xbuf, "xbuf", XHnI, "XHnI",
                    key_fn=lambda k: tuple(f"XHnI@{4 * k + j}" for j in range(4)))
    emit_linear_fm(ctx, "NSdft", ctx.dram["NSdft"], xbuf, "xbuf", out_f32=A2,
                   outf_name="A2")
    emit_l1_ffn_prep(ctx)
    emit_l0_delta(ctx, A1, A2, moe_out, delta_out)


def emit_l1_attn(ctx, attnO_tm):
    pb = ctx.pb
    pool = get_pool(ctx)
    xbuf, wbuf = pool["xbuf"], pool["wbuf"]
    bias = ctx.sb("bias", [128, 128], F32)
    identb = ctx.sb("identb", [128, 128], BF16)
    pb.dma("gpsimd", identb[:, :], ctx.dram["ident"][:, :], writes=("identb",))
    qT = ctx.scratch("qT", [D, T], BF16)
    kT = ctx.scratch("kT", [D, T], BF16)
    v_tm = ctx.scratch("v_tm", [T, D], BF16)
    emit_load_fm(ctx, xbuf, "xbuf", ctx.dram["xn1_tm"], "xn1_tm")
    emit_load_bias(ctx, bias, "bias", ctx.dram["L1_attn_bq"], KT, 0)
    emit_load_bias(ctx, bias, "bias", ctx.dram["L1_attn_bk"], KT, 16)
    emit_linear_fm(ctx, "L1_attn_Wq", ctx.dram["L1_attn_Wq"], xbuf, "xbuf",
                   out_bf=qT, out_name="qT", bias_sb=bias, bias_key="bias",
                   bias_off=0)
    emit_linear_fm(ctx, "L1_attn_Wk", ctx.dram["L1_attn_Wk"], xbuf, "xbuf",
                   out_bf=kT, out_name="kT", bias_sb=bias, bias_key="bias",
                   bias_off=16)
    ps = [ctx.ps(f"ps{i}") for i in range(7)]
    qst = ctx.sb("dT", [128, KT * 128], BF16)
    onescol = qst[0:1, 0:128]
    pb.op("vector", lambda e: e.memset(onescol, 1.0), writes=("qst_0",))
    bs1 = ctx.sb("bs1", [128, T], BF16)
    bvrow = bs1[0:1, :]
    pb.dma("gpsimd", bvrow,
           ctx.dram["L1_attn_bv"].rearrange("(one d) -> one d", one=1),
           writes=FK("bs1"))
    ctx.wcache = None
    for k in range(KT):
        pb.dma("gpsimd", wbuf[:, k * D:(k + 1) * D],
               ctx.dram["L1_attn_Wv"][k * 128:(k + 1) * 128, :],
               writes=(f"wbuf{k}",))
    evb = ctx.sb("bs0", [128, T], BF16)
    kevb = FK("bs0")
    for tc in range(T // 128):
        for n in range(NT):
            ns = slice(n * NCHUNK, (n + 1) * NCHUNK)
            def vgroup(e, tc=tc, n=n, ns=ns):
                for k in range(KT):
                    e.matmul(ps[n][:, :],
                             xbuf[:, k * T + tc * 128:k * T + (tc + 1) * 128],
                             wbuf[:, k * D + n * NCHUNK:k * D + (n + 1) * NCHUNK],
                             start=(k == 0), stop=False)
                return e.matmul(ps[n][:, :], onescol, bvrow[0:1, ns],
                                start=False, stop=True)
            pb.op("tensor", vgroup,
                  reads=tuple(f"xbuf{k}" for k in range(KT)) +
                  tuple(f"wbuf{k}" for k in range(KT)) + ("qst_0",) + FK("bs1"),
                  writes=(f"ps{n}",))
            pb.op("scalar", lambda e, n=n, ns=ns: e.activation(
                evb[:, ns], ps[n][:, :], AF.Copy),
                reads=(f"ps{n}",), writes=(kevb[n],))
        pb.dma("sync", v_tm[tc * 128:(tc + 1) * 128, :], evb[:, :],
               reads=kevb, writes=(ctx.wkey("v_tm", tc),))
    pb.barrier()
    emit_load_plane(ctx, xbuf, "xbuf", kT, "kT")
    for j in range(KT):
        pb.dma("sync", wbuf[:, j * D:(j + 1) * D],
               v_tm[j * 128:(j + 1) * 128, :], reads=ctx.rkeys("v_tm"),
               writes=(f"wbuf{j}",))
    causalF = ctx.sb("tt0", [128, 128], F32)
    pb.dma("sync", causalF[:, :], ctx.dram["causal"][:, :], writes=("tt0",))
    fsq = [ctx.sb(f"fs{i}", [128, T], F32) for i in range(6)]
    pbf = bs1
    kpbf = FK("bs1")
    pTb = ctx.sb("bs0", [128, T], BF16)
    lm = ctx.sb("ln_mean", [128, 8], F32)
    lsum = ctx.sb("ln_ssq", [128, 8], F32)
    pst = ctx.ps("pst", [128, 512], BF16)
    for qc in range(T // 128):
        par = qc % 2
        sc, pex, ev = fsq[3 * par], fsq[3 * par + 1], fsq[3 * par + 2]
        ksc, kpex, kev2 = FK(f"fs{3 * par}"), FK(f"fs{3 * par + 1}"), \
            FK(f"fs{3 * par + 2}")
        lmc, lsc = 4 * par, 4 * par
        nkv = qc + 1
        W = nkv * 128
        nch = (W + NCHUNK - 1) // NCHUNK
        for k in range(KT):
            pb.dma("scalar", qst[:, k * 128:(k + 1) * 128],
                   qT[k * 128:(k + 1) * 128, qc * 128:(qc + 1) * 128],
                   reads=ctx.rkeys("qT"), writes=(f"qst_{k}",))
        for n in range(nch):
            w = min(NCHUNK, W - n * NCHUNK)
            def sgroup(e, n=n, w=w):
                last = None
                for k in range(KT):
                    last = e.matmul(
                        ps[n][:, 0:w], qst[:, k * 128:(k + 1) * 128],
                        xbuf[:, k * T + n * NCHUNK:k * T + n * NCHUNK + w],
                        start=(k == 0), stop=(k == KT - 1))
                return last
            pb.op("tensor", sgroup,
                  reads=tuple(f"qst_{k}" for k in range(KT)) +
                  tuple(f"xbuf{k}" for k in range(KT)),
                  writes=(f"ps{n}",))
            pb.op("vector", lambda e, n=n, w=w: e.tensor_scalar(
                sc[:, n * NCHUNK:n * NCHUNK + w], ps[n][:, 0:w],
                float(D) ** -0.5, None, ALU.mult),
                reads=(f"ps{n}",), writes=(ksc[n],))
        pb.op("vector", lambda e, qc=qc: e.tensor_add(
            sc[:, qc * 128:(qc + 1) * 128], sc[:, qc * 128:(qc + 1) * 128],
            causalF[:, :]),
            reads=(ksc[qc // 4], "tt0"), writes=(ksc[qc // 4],))
        pb.op("vector", lambda e, W=W, lmc=lmc: e.reduce_max(
            lm[:, lmc:lmc + 1], sc[:, 0:W], axis=AX.X),
            reads=ksc[:nch], writes=(f"l1lm{par}",))
        pb.op("vector", lambda e, lmc=lmc: e.tensor_scalar_mul(
            lm[:, lmc:lmc + 1], lm[:, lmc:lmc + 1], -1.0),
            reads=(f"l1lm{par}",), writes=(f"l1lm{par}",))
        pb.op("scalar", lambda e, W=W, lmc=lmc, lsc=lsc: e.activation(
            pex[:, 0:W], sc[:, 0:W], AF.Exp, bias=lm[:, lmc:lmc + 1],
            accum_out=lsum[:, lsc:lsc + 1]),
            reads=ksc[:nch] + (f"l1lm{par}",),
            writes=kpex[:nch] + (f"l1ls{par}",))
        pb.op("vector", lambda e, lsc=lsc: e.reciprocal(
            lsum[:, lsc + 1:lsc + 2], lsum[:, lsc:lsc + 1]),
            reads=(f"l1ls{par}",), writes=(f"l1ls{par}",))
        pb.op("vector", lambda e, W=W, lsc=lsc: e.tensor_scalar(
            pbf[:, 0:W], pex[:, 0:W], lsum[:, lsc + 1:lsc + 2], None,
            ALU.mult),
            reads=kpex[:nch] + (f"l1ls{par}",), writes=kpbf[:nch])
        for j in range(nkv):
            pb.op("tensor", lambda e, j=j: e.transpose(
                pst[:, 0:128], pbf[:, j * 128:(j + 1) * 128], identb[:, :]),
                reads=(kpbf[j // 4], "identb"), writes=("pst",))
            pb.op("vector", lambda e, j=j: e.tensor_copy(
                pTb[:, j * 128:(j + 1) * 128], pst[:, 0:128]),
                reads=("pst",), writes=(FK("bs0")[j // 4],))
        for n in range(NT):
            ns = slice(n * NCHUNK, (n + 1) * NCHUNK)
            def ogroup(e, n=n, nkv=nkv, ns=ns):
                last = None
                for j in range(nkv):
                    last = e.matmul(
                        ps[4 + n % 2][:, :], pTb[:, j * 128:(j + 1) * 128],
                        wbuf[:, j * D + n * NCHUNK:j * D + (n + 1) * NCHUNK],
                        start=(j == 0), stop=(j == nkv - 1))
                return last
            pb.op("tensor", ogroup,
                  reads=FK("bs0")[:(nkv + 3) // 4] +
                  tuple(f"wbuf{j}" for j in range(nkv)),
                  writes=(f"ps{4 + n % 2}",))
            pb.op("vector", lambda e, n=n, ns=ns: e.tensor_copy(
                ev[:, ns], ps[4 + n % 2][:, :]),
                reads=(f"ps{4 + n % 2}",), writes=(kev2[n],))
        pb.dma("sync", attnO_tm[qc * 128:(qc + 1) * 128, :], ev[:, :],
               reads=kev2, writes=(ctx.wkey("attnO", qc),))


def emit_weight_transpose(ctx, wname, W_dram, out_plane, out_name, R, C):
    pb = ctx.pb
    identb = ctx.sb("identb", [128, 128], BF16)
    pb.dma("gpsimd", identb[:, :], ctx.dram["ident"][:, :], writes=("identb",))
    band = ctx.sb("bs0", [128, T], BF16)
    stg = ctx.sb("bs1", [128, T], BF16)
    pst = ctx.ps("pst", [128, 512], BF16)
    kb, ks = FK("bs0"), FK("bs1")
    for r in range(R // 128):
        pb.dma("gpsimd", band[:, 0:C], W_dram[r * 128:(r + 1) * 128, :],
               writes=kb)
        for c in range(C // 128):
            pb.op("tensor", lambda e, c=c: e.transpose(
                pst[:, 0:128], band[:, c * 128:(c + 1) * 128], identb[:, :]),
                reads=kb + ("identb",), writes=("pst",))
            pb.op("vector", lambda e, c=c: e.tensor_copy(
                stg[:, c * 128:(c + 1) * 128], pst[:, 0:128]),
                reads=("pst",), writes=(ks[c // 4],))
        for c in range(C // 128):
            pb.dma("sync", out_plane[c * 128:(c + 1) * 128,
                                     r * 128:(r + 1) * 128],
                   stg[:, c * 128:(c + 1) * 128], reads=(ks[c // 4],),
                   writes=(ctx.wkey(out_name, f"{r}_{c}"),))


def emit_l1_ffn_prep(ctx):
    pb = ctx.pb
    Wb_in = ctx.scratch("Wb_in", [3 * D, D], BF16)
    Wb_wr = ctx.scratch("Wb_wr", [D, D], BF16)
    Wb_op = ctx.scratch("Wb_op", [D, D], BF16)
    cstg = [ctx.sb("fs4", [128, T], F32), ctx.sb("fs5", [128, T], F32)]
    bstg = [ctx.sb("bs0", [128, T], BF16), ctx.sb("bs1", [128, T], BF16)]
    for wn, wb, rr in (("L1_ffn_in_proj_w", Wb_in, 3 * D),
                       ("L1_ffn_write_w", Wb_wr, D),
                       ("L1_ffn_out_proj_w", Wb_op, D)):
        for r in range(rr // 128):
            i = r % 2
            sl = bstg[i][:, :]
            kk = FK(f"bs{i}")
            pb.dma("gpsimd", sl, ctx.dram[wn][r * 128:(r + 1) * 128, :],
                   writes=kk)
            pb.dma("sync", wb[r * 128:(r + 1) * 128, :], sl, reads=kk,
                   writes=(ctx.wkey("b" + wn, r),))


def emit_l1_ffn(ctx, ffnO_tm):
    pb = ctx.pb
    NB, S, HD, NH = 2, 16, 64, 32
    EPS = 1e-6
    pool = get_pool(ctx)
    xbuf, wbuf = pool["xbuf"], pool["wbuf"]
    ps = [ctx.ps(f"ps{i}") for i in range(7)]
    pst = ctx.ps("pst", [128, 512], BF16)
    identb = ctx.sb("identb", [128, 128], BF16)
    identf = ctx.sb("identf", [128, 128], F32)
    pb.dma("sync", identf[:, :], ctx.dram["ident"][:, :], writes=("identf",))
    bias = ctx.sb("bias", [128, 128], F32)
    fs = [ctx.sb(f"fs{i}", [128, T], F32) for i in range(6)]
    bs = [ctx.sb(f"bs{i}", [128, T], BF16) for i in range(2)]

    Wb_in = ctx.dram["Wb_in"]
    Wb_wr = ctx.dram["Wb_wr"]
    Wb_op = ctx.dram["Wb_op"]

    onesf = ctx.sb("onesf", [1, 128], F32)
    pb.op("vector", lambda e: e.memset(onesf[:, :], 1.0), writes=("onesf",))
    wcolbuf = ctx.sb("wcol", [128, 8192], BF16)
    cwT = wcolbuf[:, 0:256]
    cw = fs[4]
    wpre = fs[5]
    kcw, kwpre = FK("fs4"), FK("fs5")
    pb.dma("sync", cw[0:16, :], ctx.dram["L1_ffn_compete_w"][:, :], writes=kcw)
    pb.dma("sync", wpre[0:1, :],
           ctx.dram["L1_ffn_norm_pre_w"].rearrange("(o d) -> o d", o=1),
           writes=kwpre)
    for n in range(NT):
        ns = slice(n * NCHUNK, (n + 1) * NCHUNK)
        pb.op("tensor", lambda e, ns=ns: e.matmul(
            ps[6][0:16, :], onesf[0:1, 0:16], wpre[0:1, ns],
            start=True, stop=True), reads=("onesf",) + kwpre, writes=("ps6",))
        pb.op("vector", lambda e, n=n, ns=ns: e.tensor_mul(
            cw[0:16, ns], cw[0:16, ns], ps[6][0:16, :]),
            reads=(kcw[n], "ps6"), writes=(kcw[n],))
    cwb = bs[0]
    pb.op("vector", lambda e: e.tensor_copy(cwb[0:16, :], cw[0:16, :]),
          reads=kcw, writes=FK("bs0"))
    for k in range(KT):
        pb.op("tensor", lambda e, k=k: e.transpose(
            pst[0:128, 0:16], cwb[0:16, k * 128:(k + 1) * 128],
            identb[0:16, 0:16]), reads=FK("bs0") + ("identb",), writes=("pst",))
        pb.op("vector", lambda e, k=k: e.tensor_copy(
            cwT[:, k * 16:(k + 1) * 16], pst[0:128, 0:16]),
            reads=("pst",), writes=("cwT",))
    cbF = bias[:, 80:96]
    cbrow = cbF[0:1, 0:16]
    pb.dma("sync", cbrow[0:1, :],
           ctx.dram["L1_ffn_compete_b"].rearrange("(o s) -> o s", o=1),
           writes=("cbrow",))
    pb.op("tensor", lambda e: e.matmul(ps[6][:, 0:16], onesf[0:1, :],
                                       cbrow[0:1, :], start=True, stop=True),
          reads=("onesf", "cbrow"), writes=("ps6",))
    pb.op("vector", lambda e: e.tensor_copy(cbF[:, :], ps[6][:, 0:16]),
          reads=("ps6",), writes=("cbF",))
    emit_load_bias(ctx, bias, "bias", ctx.dram["L1_ffn_write_b"], KT, 32)
    emit_load_bias(ctx, bias, "bias", ctx.dram["L1_ffn_out_proj_b"], KT, 48)
    emit_load_bias(ctx, bias, "bias", ctx.dram["L1_ffn_in_proj_b"], KT, 64)
    bkv = wcolbuf[0:1, 1024:1024 + T]
    ipb = ctx.dram["L1_ffn_in_proj_b"].rearrange("(th d) -> th d", d=D)
    onesb = wcolbuf[0:1, 3072:3072 + 128]
    pb.op("vector", lambda e: e.memset(onesb, 1.0), writes=("onesb",))
    wsP = ctx.scratch("wsP", [S, D], F32)
    pb.dma("sync", fs[4][0:16, :], ctx.dram["L1_ffn_workspace"][:, :],
           writes=kcw)
    pb.dma("sync", wsP[:, :], fs[4][0:16, :], reads=kcw,
           writes=(ctx.wkey("wsP", 0),))
    rstdp = bias[:, 96:112]
    st8 = ctx.sb("ln_var", [128, 8], F32)
    top8 = bias[0:16, 112:120]
    wgt4 = bias[0:16, 120:128]
    wsT = wcolbuf[:, 256:512]
    kkT = wcolbuf[:, 512:768]
    gT = wcolbuf[:, 768:1024]
    pb.barrier()

    x_tm_f = "xn1f_tm"
    x_tm_b = "xn1_tm"
    xa = ctx.scratch("l1xa", [T, D], F32)
    xab = ctx.scratch("l1xab", [T, D], BF16)
    wxP = ctx.scratch("wxP", [D, T], BF16)
    q1P = ctx.scratch("q1P", [D, T], BF16)
    ctxP = ctx.scratch("ctxP", [T, D], BF16)
    xbP = ctx.scratch("xbP", [T, D], F32)

    for nb in range(NB):
        last = nb == NB - 1
        emit_load_fm(ctx, xbuf, "xbuf", ctx.dram[x_tm_b], x_tm_b)
        ctx.wcache = None
        emit_linear_fm(ctx, "WT_wr", Wb_wr, xbuf, "xbuf", out_bf=wxP,
                       out_name="wxP", bias_sb=bias, bias_key="bias",
                       bias_off=32, w_transpose=True)
        emit_linear_fm(ctx, "WT_in_q", Wb_in[0:D, :], xbuf, "xbuf",
                       out_bf=q1P, out_name="q1P", bias_sb=bias,
                       bias_key="bias", bias_off=64, w_transpose=True)
        for t in range(T // 128):
            i = t % 2
            kx = FK(f"fs{i}")
            pb.dma("sync", fs[i][:, :], ctx.dram[x_tm_f][t * 128:(t + 1) * 128, :],
                   reads=ctx.rkeys(x_tm_f), writes=kx)
            pb.op("scalar", lambda e, i=i: e.activation(
                fs[2][:, :], fs[i][:, :], AF.Square, accum_out=st8[:, 0:1]),
                reads=kx, writes=FK("fs2") + ("st8",))
            pb.op("vector", lambda e: e.tensor_scalar(
                st8[:, 1:2], st8[:, 0:1], 1.0 / D, EPS, ALU.mult, ALU.add),
                reads=("st8",), writes=("st8",))
            pb.op("scalar", lambda e: e.activation(
                st8[:, 2:3], st8[:, 1:2], AF.Sqrt), reads=("st8",),
                writes=("st8",))
            pb.op("vector", lambda e, t=t: e.reciprocal(
                rstdp[:, t:t + 1], st8[:, 2:3]), reads=("st8",),
                writes=("rstdp",))
        lsm = fs[1]
        klsm = FK("fs1")
        for tc in range(T // 128):
            def lgroup(e, tc=tc):
                last_i = None
                for k in range(KT):
                    last_i = e.matmul(
                        ps[6][:, 0:16],
                        xbuf[:, k * T + tc * 128:k * T + (tc + 1) * 128],
                        cwT[:, k * 16:(k + 1) * 16],
                        start=(k == 0), stop=(k == KT - 1))
                return last_i
            pb.op("tensor", lgroup,
                  reads=tuple(f"xbuf{k}" for k in range(KT)) + ("cwT",),
                  writes=("ps6",))
            pb.op("vector", lambda e, tc=tc: e.tensor_scalar(
                fs[0][:, 0:16], ps[6][:, 0:16], rstdp[:, tc:tc + 1], None,
                ALU.mult), reads=("ps6", "rstdp"), writes=("fs0_0",))
            pb.op("vector", lambda e: e.tensor_add(
                fs[0][:, 0:16], fs[0][:, 0:16], cbF[:, :]),
                reads=("fs0_0", "cbF"), writes=("fs0_0",))
            pb.op("tensor", lambda e, tc=tc: e.transpose(
                ps[6][0:16, 0:128], fs[0][:, 0:16], identf[:, :]),
                reads=("fs0_0", "identf"), writes=("ps6",))
            pb.op("vector", lambda e, tc=tc: e.tensor_copy(
                lsm[0:16, tc * 128:(tc + 1) * 128], ps[6][0:16, 0:128]),
                reads=("ps6",), writes=(klsm[tc // 4],))
        pb.op("vector", lambda e: e.max(top8[0:16, :], lsm[0:16, :]),
              reads=klsm, writes=("t4top",))
        pb.op("vector", lambda e: e.tensor_scalar(
            wgt4[0:16, 0:4], top8[0:16, 0:4], top8[0:16, 0:1], None,
            ALU.subtract), reads=("t4top",), writes=("t4wgt",))
        pb.op("scalar", lambda e: e.activation(
            wgt4[0:16, 0:4], wgt4[0:16, 0:4], AF.Exp,
            accum_out=wgt4[0:16, 4:5]), reads=("t4wgt",), writes=("t4wgt",))
        pb.op("vector", lambda e: e.reciprocal(wgt4[0:16, 5:6],
                                               wgt4[0:16, 4:5]),
              reads=("t4wgt",), writes=("t4wgt",))
        pb.op("vector", lambda e: e.tensor_scalar(
            wgt4[0:16, 0:4], wgt4[0:16, 0:4], wgt4[0:16, 5:6], None,
            ALU.mult), reads=("t4wgt",), writes=("t4wgt",))
        G = fs[2]
        kG = FK("fs2")
        pb.op("vector", lambda e: e.memset(G[0:16, :], 0.0), writes=kG)
        for kk4 in range(4):
            pb.op("vector", lambda e, kk4=kk4: e.tensor_scalar(
                fs[3][0:16, :], lsm[0:16, :], top8[0:16, kk4:kk4 + 1], None,
                ALU.is_equal), reads=klsm + ("t4top",), writes=FK("fs3"))
            pb.op("vector", lambda e, kk4=kk4: e.scalar_tensor_tensor(
                G[0:16, :], fs[3][0:16, :], wgt4[0:16, kk4:kk4 + 1],
                G[0:16, :], ALU.mult, ALU.add),
                reads=FK("fs3") + ("t4wgt",) + kG, writes=kG)
        pb.op("vector", lambda e: e.tensor_copy(bs[0][0:16, :], G[0:16, :]),
              reads=kG, writes=FK("bs0"))
        for k in range(KT):
            pb.op("tensor", lambda e, k=k: e.transpose(
                pst[0:128, 0:16], bs[0][0:16, k * 128:(k + 1) * 128],
                identb[0:16, 0:16]), reads=FK("bs0") + ("identb",),
                writes=("pst",))
            pb.op("vector", lambda e, k=k: e.tensor_copy(
                gT[:, k * 16:(k + 1) * 16], pst[0:128, 0:16]),
                reads=("pst",), writes=("gT",))
        pb.barrier()
        pb.barrier()
        ctx.wcache = None
        for j in range(KT):
            pb.dma("sync", wbuf[:, j * D:(j + 1) * D],
                   wxP[:, j * 128:(j + 1) * 128], transpose=True,
                   reads=ctx.rkeys("wxP"), writes=(f"wbuf{j}",))
        wr16 = fs[3]
        kwr = FK("fs3")
        for n in range(NT):
            ns = slice(n * NCHUNK, (n + 1) * NCHUNK)
            def wgroup(e, n=n, ns=ns):
                last_i = None
                for j in range(KT):
                    last_i = e.matmul(
                        ps[n][0:16, :], gT[:, j * 16:(j + 1) * 16],
                        wbuf[:, j * D + n * NCHUNK:j * D + (n + 1) * NCHUNK],
                        start=(j == 0), stop=(j == KT - 1))
                return last_i
            pb.op("tensor", wgroup,
                  reads=("gT",) + tuple(f"wbuf{j}" for j in range(KT)),
                  writes=(f"ps{n}",))
            pb.op("vector", lambda e, n=n, ns=ns: e.tensor_copy(
                wr16[0:16, ns], ps[n][0:16, :]), reads=(f"ps{n}",),
                writes=(kwr[n],))
        pb.dma("sync", fs[4][0:16, :], wsP[:, :], reads=ctx.rkeys("wsP"),
               writes=kcw)
        pb.op("vector", lambda e: e.tensor_add(
            fs[4][0:16, :], fs[4][0:16, :], wr16[0:16, :]),
            reads=kcw + kwr, writes=kcw)
        pb.op("scalar", lambda e: e.activation(
            fs[0][0:16, :], fs[4][0:16, :], AF.Square,
            accum_out=st8[0:16, 0:1]), reads=kcw,
            writes=FK("fs0") + ("st8",))
        pb.op("vector", lambda e: e.tensor_scalar(
            st8[0:16, 1:2], st8[0:16, 0:1], 1.0 / D, EPS, ALU.mult, ALU.add),
            reads=("st8",), writes=("st8",))
        pb.op("scalar", lambda e: e.activation(
            st8[0:16, 2:3], st8[0:16, 1:2], AF.Sqrt), reads=("st8",),
            writes=("st8",))
        pb.op("vector", lambda e: e.reciprocal(st8[0:16, 3:4], st8[0:16, 2:3]),
              reads=("st8",), writes=("st8",))
        pb.op("vector", lambda e: e.tensor_scalar(
            fs[4][0:16, :], fs[4][0:16, :], st8[0:16, 3:4], None, ALU.mult),
            reads=kcw + ("st8",), writes=kcw)
        pb.dma("sync", fs[5][0:1, :],
               ctx.dram["L1_ffn_norm_post_w"].rearrange("(o d) -> o d", o=1),
               writes=kwpre)
        for n in range(NT):
            ns = slice(n * NCHUNK, (n + 1) * NCHUNK)
            pb.op("tensor", lambda e, ns=ns: e.matmul(
                ps[6][0:16, :], onesf[0:1, 0:16], fs[5][0:1, ns],
                start=True, stop=True), reads=("onesf",) + kwpre,
                writes=("ps6",))
            pb.op("vector", lambda e, n=n, ns=ns: e.tensor_mul(
                fs[4][0:16, ns], fs[4][0:16, ns], ps[6][0:16, :]),
                reads=(kcw[n], "ps6"), writes=(kcw[n],))
        pb.dma("sync", wsP[:, :], fs[4][0:16, :], reads=kcw,
               writes=(ctx.wkey("wsP", nb + 1),))
        pb.barrier()
        pb.op("vector", lambda e: e.tensor_copy(bs[0][0:16, :], fs[4][0:16, :]),
              reads=kcw, writes=FK("bs0"))
        for k in range(KT):
            pb.op("tensor", lambda e, k=k: e.transpose(
                pst[0:128, 0:16], bs[0][0:16, k * 128:(k + 1) * 128],
                identb[0:16, 0:16]), reads=FK("bs0") + ("identb",),
                writes=("pst",))
            pb.op("vector", lambda e, k=k: e.tensor_copy(
                wsT[:, k * 16:(k + 1) * 16], pst[0:128, 0:16]),
                reads=("pst",), writes=("wsT",))
        vvp = bs[1]
        kvvp = FK("bs1")
        pb.op("vector", lambda e: e.memset(vvp[:, :], 0.0), writes=kvvp)
        for which, off, brow in (("kk", D, 1), ("vv", 2 * D, 2)):
            pb.dma("gpsimd", bkv[0:1, :], ipb[brow:brow + 1, :],
                   writes=("bkv0",))
            for k in range(KT):
                pb.dma("scalar", wbuf[:, k * D:(k + 1) * D],
                       Wb_in[off:off + D, k * 128:(k + 1) * 128],
                       transpose=True,
                       reads=ctx.rkeys("bL1_ffn_in_proj_w"),
                       writes=(f"wbuf{k}",))
            for n in range(NT):
                ns = slice(n * NCHUNK, (n + 1) * NCHUNK)
                def kvgroup(e, off=off, n=n, brow=brow):
                    for k in range(KT):
                        e.matmul(ps[5][0:16, :], wsT[:, k * 16:(k + 1) * 16],
                                 wbuf[:, k * D + n * NCHUNK:
                                      k * D + (n + 1) * NCHUNK],
                                 start=(k == 0), stop=False)
                    return e.matmul(ps[5][0:16, :], onesb[0:1, 0:16],
                                    bkv[0:1, n * NCHUNK:(n + 1) * NCHUNK],
                                    start=False, stop=True)
                pb.op("tensor", kvgroup,
                      reads=("wsT", "onesb", "bkv0") +
                      tuple(f"wbuf{k}" for k in range(KT)),
                      writes=("ps5",))
                if which == "kk":
                    pb.op("vector", lambda e, n=n, ns=ns: e.tensor_copy(
                        fs[0][0:16, ns], ps[5][0:16, :]), reads=("ps5",),
                        writes=(FK("fs0")[n],))
                else:
                    for j4 in (0, 64):
                        pb.op("vector", lambda e, n=n, ns=ns, j4=j4:
                              e.tensor_copy(vvp[j4:j4 + 16, ns],
                                            ps[5][0:16, :]), reads=("ps5",),
                              writes=(kvvp[n],))
        pb.op("vector", lambda e: e.tensor_copy(bs[0][0:16, :], fs[0][0:16, :]),
              reads=FK("fs0"), writes=FK("bs0"))
        for k in range(KT):
            pb.op("tensor", lambda e, k=k: e.transpose(
                pst[0:128, 0:16], bs[0][0:16, k * 128:(k + 1) * 128],
                identb[0:16, 0:16]), reads=FK("bs0") + ("identb",),
                writes=("pst",))
            pb.op("vector", lambda e, k=k: e.tensor_copy(
                kkT[:, k * 16:(k + 1) * 16], pst[0:128, 0:16]),
                reads=("pst",), writes=("kkT",))
        pb.barrier()
        ctx.wcache = None
        emit_load_plane(ctx, xbuf, "xbuf", q1P, "q1P")
        pe = fs[0]
        kpe = FK("fs0")
        am = fs[2]
        kam = FK("fs2")
        pb2 = bs[0]
        kpb2 = FK("bs0")
        qst = ctx.sb("dT", [128, KT * 128], BF16)
        for lc in range(T // 128):
            for par in (0, 1):
                def agroup(e, lc=lc, par=par):
                    last_i = None
                    for h in range(par, NH, 2):
                        po = 64 * (h % 2)
                        kt = h // 2
                        last_i = e.matmul(
                            ps[h // 8][:, 16 * (h % 8):16 * (h % 8) + 16],
                            xbuf[po:po + 64,
                                 kt * T + lc * 128:kt * T + (lc + 1) * 128],
                            kkT[po:po + 64, kt * 16:(kt + 1) * 16],
                            start=True, stop=True)
                    return last_i
                pb.op("tensor", agroup,
                      reads=tuple(f"xbuf{k}" for k in range(KT)) + ("kkT",),
                      writes=("ps0", "ps1", "ps2", "ps3"))
            for b in range(4):
                pb.op("vector", lambda e, b=b: e.tensor_scalar(
                    pe[:, b * 128:(b + 1) * 128], ps[b][:, 0:128],
                    HD ** -0.5, None, ALU.mult),
                    reads=(f"ps{b}",), writes=(kpe[0],))
            pe3 = pe[:, 0:512].rearrange("p (h s) -> p h s", s=16)
            am1 = am[:, 0:32].rearrange("p (h o) -> p h o", o=1)
            am2 = am[:, 32:64].rearrange("p (h o) -> p h o", o=1)
            pb.op("vector", lambda e: e.reduce_max(am1, pe3, axis=AX.X),
                  reads=(kpe[0],), writes=(kam[0],))
            pb.op("vector", lambda e: e.tensor_tensor(
                pe3, pe3, am1.broadcast_to((128, 32, 16)), ALU.subtract),
                reads=(kpe[0], kam[0]), writes=(kpe[0],))
            pb.op("scalar", lambda e: e.activation(
                pe[:, 0:512], pe[:, 0:512], AF.Exp),
                reads=(kpe[0],), writes=(kpe[0],))
            pb.op("vector", lambda e: e.reduce_sum(am2, pe3, axis=AX.X),
                  reads=(kpe[0],), writes=(kam[0],))
            pb.op("vector", lambda e: e.reciprocal(am[:, 32:64], am[:, 32:64]),
                  reads=(kam[0],), writes=(kam[0],))
            pb.op("vector", lambda e: e.memset(pb2[:, :], 0.0),
                  writes=kpb2)
            pb.op("vector", lambda e: e.tensor_tensor(
                pb2[:, :].rearrange("p (h s) -> p h s", s=64)[:, :, 0:16],
                pe3, am2.broadcast_to((128, 32, 16)), ALU.mult),
                reads=(kpe[0], kam[0]) + kpb2, writes=kpb2)
            for b in range(16):
                pb.op("tensor", lambda e, b=b: e.transpose(
                    pst[:, 0:128], pb2[:, b * 128:(b + 1) * 128],
                    identb[:, :]), reads=(kpb2[b // 4], "identb"),
                    writes=("pst",))
                pb.op("vector", lambda e, b=b: e.tensor_copy(
                    qst[:, b * 128:(b + 1) * 128], pst[:, 0:128]),
                    reads=("pst",), writes=(f"qst_{b // 4}",))

            for par in (0, 1):
                def cgroup(e, lc=lc, par=par):
                    last_i = None
                    for h in range(par, NH, 2):
                        blk = h // 2
                        po = 64 * (h % 2)
                        last_i = e.matmul(
                            ps[h // 8][:, 64 * (h % 8):64 * (h % 8) + 64],
                            qst[po:po + 32, blk * 128:(blk + 1) * 128],
                            vvp[po:po + 32, 64 * h:64 * h + 64],
                            start=True, stop=True)
                    return last_i
                pb.op("tensor", cgroup,
                      reads=tuple(f"qst_{b}" for b in range(4)) + kvvp,
                      writes=("ps0", "ps1", "ps2", "ps3"))
            for b in range(4):
                pb.op("vector", lambda e, b=b: e.tensor_copy(
                    bs[0][:, b * 512:(b + 1) * 512], ps[b][:, :]),
                    reads=(f"ps{b}",), writes=(kpb2[b],))
            pb.dma("sync", ctxP[lc * 128:(lc + 1) * 128, :], bs[0][:, :],
                   reads=kpb2, writes=(ctx.wkey("ctxP", lc),))
        pb.barrier()
        ctx.wcache = None
        emit_load_fm(ctx, xbuf, "xbuf", ctxP, "ctxP")
        for k in range(KT):
            pb.dma("sync", wbuf[:, k * D:(k + 1) * D],
                   Wb_op[:, k * 128:(k + 1) * 128], transpose=True,
                   reads=ctx.rkeys("bL1_ffn_out_proj_w"),
                   writes=(f"wbuf{k}",))
        oprow = bs[1][0:1, :]
        pb.dma("gpsimd", oprow,
               ctx.dram["L1_ffn_out_proj_b"].rearrange("(o d) -> o d", o=1),
               writes=(kvvp[0],))
        for tc in range(T // 128):
            for n in range(NT):
                ns = slice(n * NCHUNK, (n + 1) * NCHUNK)
                def xgroup(e, tc=tc, n=n, ns=ns):
                    for k in range(KT):
                        e.matmul(ps[n][:, :],
                                 xbuf[:, k * T + tc * 128:k * T + (tc + 1) * 128],
                                 wbuf[:, k * D + n * NCHUNK:k * D + (n + 1) * NCHUNK],
                                 start=(k == 0), stop=False)
                    return e.matmul(ps[n][:, :], onesb,
                                    oprow[0:1, ns], start=False, stop=True)
                pb.op("tensor", xgroup,
                      reads=tuple(f"xbuf{k}" for k in range(KT)) +
                      tuple(f"wbuf{k}" for k in range(KT)) +
                      ("onesb", kvvp[0]), writes=(f"ps{n}",))
                pb.op("vector", lambda e, n=n, ns=ns: e.tensor_copy(
                    fs[3][:, ns], ps[n][:, :]), reads=(f"ps{n}",),
                    writes=(FK("fs3")[n],))
            pb.dma("sync", xbP[tc * 128:(tc + 1) * 128, :], fs[3][:, :],
                   reads=FK("fs3"), writes=(ctx.wkey("xbP", tc),))
        pb.barrier()
        pb.dma("sync", fs[4][0:1, :],
               ctx.dram["L1_ffn_norm_post_w"].rearrange("(o d) -> o d", o=1),
               writes=kcw)
        for n in range(NT):
            ns = slice(n * NCHUNK, (n + 1) * NCHUNK)
            pb.op("tensor", lambda e, ns=ns: e.matmul(
                ps[6][:, :], onesf[0:1, :], fs[4][0:1, ns],
                start=True, stop=True), reads=("onesf",) + kcw,
                writes=("ps6",))
            pb.op("vector", lambda e, n=n, ns=ns: e.tensor_copy(
                fs[5][:, ns], ps[6][:, :]), reads=("ps6",),
                writes=(kwpre[n],))
        for t in range(T // 128):
            i = t % 2
            kx = FK(f"fs{i}")
            r = slice(t * 128, (t + 1) * 128)
            pb.dma("sync", fs[i][:, :], ctx.dram[x_tm_f][r, :],
                   reads=ctx.rkeys(x_tm_f), writes=kx)
            pb.dma("scalar", fs[2][:, :], xbP[r, :], reads=ctx.rkeys("xbP"),
                   writes=FK("fs2"))
            pb.op("vector", lambda e, i=i: e.tensor_add(
                fs[i][:, :], fs[i][:, :], fs[2][:, :]),
                reads=kx + FK("fs2"), writes=kx)
            pb.op("scalar", lambda e, i=i: e.activation(
                fs[3][:, :], fs[i][:, :], AF.Square, accum_out=st8[:, 0:1]),
                reads=kx, writes=FK("fs3") + ("st8",))
            pb.op("vector", lambda e: e.tensor_scalar(
                st8[:, 1:2], st8[:, 0:1], 1.0 / D, EPS, ALU.mult, ALU.add),
                reads=("st8",), writes=("st8",))
            pb.op("scalar", lambda e: e.activation(
                st8[:, 2:3], st8[:, 1:2], AF.Sqrt), reads=("st8",),
                writes=("st8",))
            pb.op("vector", lambda e: e.reciprocal(st8[:, 3:4], st8[:, 2:3]),
                  reads=("st8",), writes=("st8",))
            pb.op("vector", lambda e, i=i: e.tensor_scalar(
                fs[i][:, :], fs[i][:, :], st8[:, 3:4], None, ALU.mult),
                reads=kx + ("st8",), writes=kx)
            pb.op("vector", lambda e, i=i: e.tensor_mul(
                fs[i][:, :], fs[i][:, :], fs[5][:, :]),
                reads=kx + kwpre, writes=kx)
            if last:
                pb.dma("sync", ffnO_tm[r, :], fs[i][:, :], reads=kx,
                       writes=(ctx.wkey("ffnO", t),))
            else:
                pb.dma("sync", xa[r, :], fs[i][:, :], reads=kx,
                       writes=(ctx.wkey("l1xa", t),))
                pb.op("vector", lambda e, i=i: e.tensor_copy(
                    bs[0][:, :], fs[i][:, :]), reads=kx, writes=FK("bs0"))
                pb.dma("scalar", xab[r, :], bs[0][:, :], reads=FK("bs0"),
                       writes=(ctx.wkey("l1xab", t),))
        pb.barrier()
        x_tm_f, x_tm_b = "l1xa", "l1xab"


def emit_l1(ctx, x_tm, delta0_fm, out_dram):
    pb = ctx.pb
    x1_tm = ctx.scratch("x1_tm", [T, D], F32)
    xn1_tm = ctx.scratch("xn1_tm", [T, D], BF16)
    xn1f_tm = ctx.scratch("xn1f_tm", [T, D], F32)
    emit_ln(ctx, x_tm, xn1_tm, delta_tm=delta0_fm, x_out_tm=x1_tm,
            x_name="x", xn_name="xn1_tm", delta_name="delta",
            xo_name="x1_tm", xnf_tm=xn1f_tm, xnf_name="xn1f_tm",
            delta_fm_bf=True)
    pb.barrier()
    attnO_tm = ctx.scratch("attnO_tm", [T, D], F32)
    emit_l1_attn(ctx, attnO_tm)
    pb.barrier()
    ffnO_tm = ctx.scratch("ffnO_tm", [T, D], F32)
    emit_l1_ffn(ctx, ffnO_tm)
    pb.barrier()
    fs = [ctx.sb(f"fs{i}", [128, T], F32) for i in range(3)]
    for t in range(T // 128):
        r = slice(t * 128, (t + 1) * 128)
        i = t % 2
        kx = FK(f"fs{i}")
        pb.dma("sync", fs[i][:, :], x1_tm[r, :], reads=ctx.rkeys("x1_tm"),
               writes=kx)
        pb.dma("scalar", fs[2][:, :], attnO_tm[r, :], reads=ctx.rkeys("attnO"),
               writes=FK("fs2"))
        pb.op("vector", lambda e, i=i: e.tensor_add(
            fs[i][:, :], fs[i][:, :], fs[2][:, :]), reads=kx + FK("fs2"),
            writes=kx)
        pb.dma("scalar", fs[2][:, :], ffnO_tm[r, :], reads=ctx.rkeys("ffnO"),
               writes=FK("fs2"))
        pb.op("vector", lambda e, i=i: e.tensor_add(
            fs[i][:, :], fs[i][:, :], fs[2][:, :]), reads=kx + FK("fs2"),
            writes=kx)
        pb.dma("sync", out_dram[r, :], fs[i][:, :], reads=kx,
               writes=(ctx.wkey("out", t),))


def host_constants():
    C, NS = dft_constants()
    P1 = np.zeros((128, 128), np.float32)
    for r in range(1, 128):
        P1[128 - r, r] = 1.0
    P2 = np.zeros((128, 128), np.float32)
    P2[0, 0] = 1.0
    ident = np.eye(128, dtype=np.float32)
    E = np.zeros((8, 128), np.float32)
    for e in range(8):
        E[e, 16 * e:16 * e + 16] = 1.0
    cm = np.zeros((128, 128), np.float32)
    cm[np.triu_indices(128, 1)] = -1e30
    return dict(Cdft=C, NSdft=NS, ident=ident, Emoe=E, causal=cm, P1m=P1)


def build_program():
    ctx = Ctx()
    x = ctx.din("x", [T, D])
    for wn, shp in (
        ("L0_attn_Wq", [D, D]), ("L0_attn_Wk", [D, D]), ("L0_attn_Wv", [D, D]),
        ("L0_attn_Ws", [D, D]), ("L0_attn_Wt", [D, D]),
        ("L0_attn_W1", [2 * D, 4 * D]), ("L0_attn_b1", [4 * D]),
        ("L0_attn_W2", [4 * D, 2 * D]), ("L0_attn_b2", [2 * D]),
        ("L0_ffn_router_w", [8, D]), ("L0_ffn_down_w", [8, 16, D]),
        ("L0_ffn_up_w", [8, D, 16]),
        ("L1_attn_Wq", [D, D]), ("L1_attn_bq", [D]),
        ("L1_attn_Wk", [D, D]), ("L1_attn_bk", [D]),
        ("L1_attn_Wv", [D, D]), ("L1_attn_bv", [D]),
        ("L1_ffn_workspace", [16, D]), ("L1_ffn_compete_w", [16, D]),
        ("L1_ffn_compete_b", [16]), ("L1_ffn_write_w", [D, D]),
        ("L1_ffn_write_b", [D]), ("L1_ffn_in_proj_w", [3 * D, D]),
        ("L1_ffn_in_proj_b", [3 * D]), ("L1_ffn_out_proj_w", [D, D]),
        ("L1_ffn_out_proj_b", [D]),
        ("L1_ffn_norm_pre_w", [D]), ("L1_ffn_norm_post_w", [D]),
        ("Cdft", [D, D]), ("NSdft", [D, D]), ("ident", [128, 128]),
        ("Emoe", [8, 128]), ("causal", [128, 128]),
        ("P1m", [128, 128]),
    ):
        ctx.din(wn, shp)
    out = ctx.dout("out", [T, D])
    delta_fm = ctx.scratch("delta_fm", [D, T], BF16)
    emit_l0(ctx, x, delta_fm)
    ctx.pb.barrier()
    emit_l1(ctx, x, delta_fm, out)
    ctx.pb.barrier()
    nc = ctx.finish()
    return ctx, nc


def kernel(**inputs):
    xs = np.ascontiguousarray(inputs["L0_attn_x"])
    weights = {k: np.ascontiguousarray(v) for k, v in inputs.items()
               if k != "L0_attn_x" and not k.startswith("L1_ffn_ig")}
    consts = host_constants()
    ctx, nc = build_program()
    in_maps = []
    for b in range(8):
        m = dict(x=xs[b], **weights, **consts)
        if hasattr(ctx, "const_values"):
            m["cst"] = const_input(ctx)
        in_maps.append(m)
    res = run_bass_kernel_spmd(nc, in_maps, core_ids=list(range(8)))
    return np.stack([r["out"] for r in res.results], axis=0)


# version 14
# speedup vs baseline: 1.1451x; 1.0041x over previous
import numpy as np
import concourse.bass as bass
import concourse.mybir as mybir
from concourse.bass_utils import run_bass_kernel_spmd

F32 = mybir.dt.float32
BF16 = mybir.dt.bfloat16
AF = mybir.ActivationFunctionType
ALU = mybir.AluOpType
AX = mybir.AxisListType

ENGS = ("sync", "scalar", "vector", "tensor", "gpsimd")

D = 2048
T = 2048
NCHUNK = 512
NT = T // NCHUNK
KT = D // 128


class PB:

    def __init__(self, nc):
        self.nc = nc
        self.items = {e: [] for e in ENGS}
        self.counts = {}
        self._sem_handles = {}
        self._ctxs = []
        self.res = {}
        self.sat = {e: {} for e in ENGS}
        self.n_ops = 0
        self._pool_i = {}

    def sem(self, name):
        if name not in self._sem_handles:
            cm = self.nc.semaphore(name)
            h = cm.__enter__()
            self._ctxs.append(cm)
            self._sem_handles[name] = h
            self.counts[name] = 0
        return self._sem_handles[name]

    def _wait(self, eng, sem_name, val):
        if val <= 0:
            return
        if self.sat[eng].get(sem_name, 0) >= val:
            return
        h = self.sem(sem_name)
        self.items[eng].append(lambda e, h=h, v=val: e.wait_ge(h, v))
        self.sat[eng][sem_name] = val

    def _deps(self, eng, reads, writes):
        qe = "q_" + eng
        for k in reads:
            r = self.res.get(k)
            if r and r["w"]:
                if r.get("wide") and r["w"][0] == qe:
                    continue
                self._wait(eng, *r["w"])
        for k in writes:
            r = self.res.get(k)
            if r:
                if r["w"] and not (r.get("wide") and r["w"][0] == qe):
                    self._wait(eng, *r["w"])
                for s, v in r["r"]:
                    if s == qe:
                        continue
                    self._wait(eng, s, v)

    def _notes(self, reads, writes, sem, val, wide=False):
        for k in reads:
            self.res.setdefault(k, {"w": None, "r": []})["r"].append((sem, val))
        for k in writes:
            self.res[k] = {"w": (sem, val), "r": [], "wide": wide}

    def op(self, eng, fn, reads=(), writes=(), wide=False):
        sem = "q_" + eng
        self._deps(eng, reads, writes)
        h = self.sem(sem)
        self.counts[sem] += 1
        val = self.counts[sem]
        self.items[eng].append(lambda e, h=h: fn(e).then_inc(h, 1))
        self._notes(reads, writes, sem, val, wide=wide)
        self.n_ops += 1
        return sem, val

    def dma(self, eng, out, in_, reads=(), writes=(), transpose=False,
            sem=None, **kw):
        if sem is None:
            i = self._pool_i.get(eng, 0)
            self._pool_i[eng] = i + 1
            sem = f"dp_{eng}_{i % 10}"
        self._deps(eng, reads, writes)
        h = self.sem(sem)
        self._wait(eng, sem, self.counts[sem])
        self.counts[sem] += 16
        val = self.counts[sem]
        if transpose:
            self.items[eng].append(
                lambda e, h=h: e.dma_start_transpose(out, in_).then_inc(h, 16))
        else:
            self.items[eng].append(
                lambda e, h=h: e.dma_start(out=out, in_=in_, **kw)
                .then_inc(h, 16))
        self._notes(reads, writes, sem, val)
        self.n_ops += 1
        return sem, val

    def barrier(self):
        snapshot = dict(self.counts)
        for e in ENGS:
            for s, v in snapshot.items():
                self._wait(e, s, v)
        self.res = {}

    def build(self):
        block_cm = self.nc.Block()
        block = block_cm.__enter__()
        for e in ENGS:
            ops = self.items[e]
            if not ops:
                continue

            def make(ops):
                def run(eng):
                    for f in ops:
                        f(eng)
                return run
            getattr(block, e)(make(ops))
        block_cm.__exit__(None, None, None)
        for cm in reversed(self._ctxs):
            cm.__exit__(None, None, None)


class Ctx:

    def __init__(self, dbg=()):
        self.nc = bass.Bass("TRN2", target_bir_lowering=False, debug=False)
        self.pb = PB(self.nc)
        self.dbg = set(dbg)
        self.dram = {}
        self.sbuf = {}
        self._sb_ctxs = []
        self.wcache = None
        self.plane_keys = {}

    def din(self, name, shape, dtype=F32):
        ap = self.nc.dram_tensor(name, list(shape), dtype,
                                 kind="ExternalInput").ap()
        self.dram[name] = ap
        return ap

    def dout(self, name, shape, dtype=F32):
        ap = self.nc.dram_tensor(name, list(shape), dtype,
                                 kind="ExternalOutput").ap()
        self.dram[name] = ap
        return ap

    def scratch(self, name, shape, dtype):
        kind = "ExternalOutput" if name in self.dbg else "Internal"
        ap = self.nc.dram_tensor(name, list(shape), dtype, kind=kind).ap()
        self.dram[name] = ap
        return ap

    def sb(self, name, shape, dtype):
        if name in self.sbuf:
            return self.sbuf[name]
        cm = self.nc.sbuf_tensor(name, list(shape), dtype)
        h = cm.__enter__()
        self._sb_ctxs.append(cm)
        self.sbuf[name] = h
        return h

    def ps(self, name, shape=(128, NCHUNK), dtype=F32):
        if name in self.sbuf:
            return self.sbuf[name]
        cm = self.nc.psum_tensor(name, list(shape), dtype)
        h = cm.__enter__()
        self._sb_ctxs.append(cm)
        self.sbuf[name] = h
        return h

    def wkey(self, plane, i):
        k = f"{plane}@{i}"
        ks = self.plane_keys.setdefault(plane, [])
        if k not in ks:
            ks.append(k)
        return k

    def rkeys(self, plane):
        return tuple(self.plane_keys.get(plane, ()))

    def finish(self):
        self.pb.build()
        for cm in reversed(self._sb_ctxs):
            cm.__exit__(None, None, None)
        return self.nc


def get_pool(ctx):
    return dict(
        wbuf=ctx.sb("wbuf", [128, KT * D], BF16),
        xbuf=ctx.sb("xbuf", [128, KT * T], BF16),
        fs0=ctx.sb("fs0", [128, T], F32),
        fs1=ctx.sb("fs1", [128, T], F32),
        fs2=ctx.sb("fs2", [128, T], F32),
        bs0=ctx.sb("bs0", [128, T], BF16),
    )


def FK(buf):
    return tuple(f"{buf}_{j}" for j in range(NT))


def emit_ln(ctx, x_tm, xn_tm, delta_tm=None, x_out_tm=None,
            x_name="x", xn_name="xn_tm", delta_name=None, xo_name=None,
            xnf_tm=None, xnf_name=None, delta_fm_bf=False):
    pb = ctx.pb
    fs = [ctx.sb(f"fs{i}", [128, T], F32) for i in range(5)]
    bs = [ctx.sb(f"bs{i}", [128, T], BF16) for i in range(2)]
    xt = [fs[0], fs[1]]
    dl = [fs[3], fs[4]]
    xnt = [bs[0], bs[1]]
    sq = fs[2]
    mean = ctx.sb("ln_mean", [128, 8], F32)
    ssq = ctx.sb("ln_ssq", [128, 8], F32)
    var = ctx.sb("ln_var", [128, 8], F32)
    rstd = ctx.sb("ln_rstd", [128, 8], F32)
    for t in range(T // 128):
        i = t % 2
        r = slice(t * 128, (t + 1) * 128)
        kx, kn = FK(f"fs{i}"), FK(f"bs{i}")
        kd = FK(f"fs{3 + i}")
        pb.dma("sync", xt[i][:, :], x_tm[r, :], writes=kx,
               reads=ctx.rkeys(x_name))
        if delta_tm is not None:
            if delta_fm_bf:
                dlb = bs[i]
                kdb = FK(f"bs{i}")
                pb.dma("scalar", dlb[:, :], delta_tm[:, r], transpose=True,
                       writes=kdb, reads=ctx.rkeys(delta_name))
                pb.op("vector", lambda e, i=i, dlb=dlb: e.tensor_add(
                    xt[i][:, :], xt[i][:, :], dlb[:, :]),
                    reads=kx + kdb, writes=kx)
            else:
                pb.dma("scalar", dl[i][:, :], delta_tm[r, :], writes=kd,
                       reads=ctx.rkeys(delta_name))
                pb.op("vector", lambda e, i=i: e.tensor_add(
                    xt[i][:, :], xt[i][:, :], dl[i][:, :]),
                    reads=kx + kd, writes=kx)
            if x_out_tm is not None:
                pb.dma("sync", x_out_tm[r, :], xt[i][:, :],
                       reads=kx, writes=(ctx.wkey(xo_name, t),))
        pb.op("vector", lambda e, i=i: e.reduce_sum(
            mean[:, 0:1], xt[i][:, :], axis=AX.X), reads=kx, writes=("ln_m",))
        pb.op("vector", lambda e: e.tensor_scalar_mul(
            mean[:, 0:1], mean[:, 0:1], 1.0 / D), reads=("ln_m",),
            writes=("ln_m",))
        pb.op("scalar", lambda e, i=i: e.activation(
            sq[:, :], xt[i][:, :], AF.Square, accum_out=ssq[:, 0:1]),
            reads=kx, writes=FK("fs2") + ("ln_ss",))
        pb.op("vector", lambda e: e.tensor_mul(
            var[:, 0:1], mean[:, 0:1], mean[:, 0:1]), reads=("ln_m",),
            writes=("ln_v",))
        pb.op("vector", lambda e: e.scalar_tensor_tensor(
            var[:, 0:1], ssq[:, 0:1], 1.0 / D, var[:, 0:1],
            ALU.mult, ALU.subtract),
            reads=("ln_ss", "ln_v"), writes=("ln_v",))
        pb.op("vector", lambda e: e.tensor_scalar_add(
            var[:, 0:1], var[:, 0:1], 1e-5), reads=("ln_v",), writes=("ln_v",))
        pb.op("scalar", lambda e: e.activation(
            rstd[:, 0:1], var[:, 0:1], AF.Sqrt),
            reads=("ln_v",), writes=("ln_r",))
        pb.op("vector", lambda e: e.reciprocal(rstd[:, 0:1], rstd[:, 0:1]),
              reads=("ln_r",), writes=("ln_r",))
        pb.op("vector", lambda e, i=i: e.tensor_scalar(
            xnt[i][:, :], xt[i][:, :], mean[:, 0:1], rstd[:, 0:1],
            ALU.subtract, ALU.mult),
            reads=kx + ("ln_m", "ln_r"), writes=kn)
        if xnf_tm is not None:
            pb.op("vector", lambda e, i=i: e.tensor_scalar(
                sq[:, :], xt[i][:, :], mean[:, 0:1], rstd[:, 0:1],
                ALU.subtract, ALU.mult),
                reads=kx + ("ln_m", "ln_r"), writes=FK("fs2"))
            pb.dma("scalar", xnf_tm[r, :], sq[:, :], reads=FK("fs2"),
                   writes=(ctx.wkey(xnf_name, t),))
        pb.dma("sync", xn_tm[r, :], xnt[i][:, :], reads=kn,
               writes=(ctx.wkey(xn_name, t),))


def emit_load_fm(ctx, dst_sb, dst_key, src_tm, src_name, n_dtiles=KT):
    pb = ctx.pb
    reads = ctx.rkeys(src_name)
    for k in range(n_dtiles):
        pb.dma("sync", dst_sb[:, k * T:(k + 1) * T],
               src_tm[:, k * 128:(k + 1) * 128], transpose=True,
               reads=reads, writes=(f"{dst_key}{k}",))


def emit_load_plane(ctx, dst_sb, dst_key, src_fm, src_name, n_dtiles=KT,
                    q="sync", key_fn=None):
    pb = ctx.pb
    for k in range(n_dtiles):
        rk = key_fn(k) if key_fn else (ctx.rkeys(src_name) if src_name else ())
        pb.dma(q, dst_sb[:, k * T:(k + 1) * T],
               src_fm[k * 128:(k + 1) * 128, :],
               reads=rk, writes=(f"{dst_key}{k}",))


def emit_linear_fm(ctx, wname, W_dram, x_sb, x_key, out_bf=None, out_f32=None,
                   Din=D, Dout=D, bias_sb=None, bias_key=None, bias_off=0,
                   act=None, scale=None, out_name=None, outf_name=None,
                   mt_range=None, w_transpose=False):
    pb = ctx.pb
    kt = Din // 128
    mt = Dout // 128
    assert kt * Dout <= KT * D
    wbuf = ctx.sb("wbuf", [128, KT * D], BF16)
    ev = ctx.sb("fs0", [128, T], F32)
    evb = ctx.sb("bs0", [128, T], BF16)
    kev, kevb = FK("fs0"), FK("bs0")
    ps = [ctx.ps(f"ps{i}") for i in range(7)]
    if ctx.wcache != wname:
        for k in range(kt):
            if w_transpose:
                pb.dma("sync", wbuf[:, k * Dout:(k + 1) * Dout],
                       W_dram[:, k * 128:(k + 1) * 128], transpose=True,
                       writes=(f"wbuf{k}",), reads=ctx.rkeys(wname))
            else:
                pb.dma("gpsimd", wbuf[:, k * Dout:(k + 1) * Dout],
                       W_dram[k * 128:(k + 1) * 128, :], writes=(f"wbuf{k}",),
                       reads=ctx.rkeys(wname))
        ctx.wcache = wname
    for m in (range(mt) if mt_range is None else mt_range):
        pgi = list(range(NT))
        for n in range(NT):
            def mmgroup(e, m=m, n=n):
                last = None
                for k in range(kt):
                    last = e.matmul(
                        ps[pgi[n]][:, :],
                        wbuf[:, k * Dout + m * 128:k * Dout + (m + 1) * 128],
                        x_sb[:, k * T + n * NCHUNK:k * T + (n + 1) * NCHUNK],
                        start=(k == 0), stop=(k == kt - 1))
                return last
            pb.op("tensor", mmgroup,
                  reads=tuple(f"wbuf{k}" for k in range(kt)) +
                  tuple(f"{x_key}{k}" for k in range(kt)),
                  writes=(f"ps{pgi[n]}",))
        dst, dst_key = (evb, "bs0") if out_f32 is None else (ev, "fs0")
        for n in range(NT):
            ns = slice(n * NCHUNK, (n + 1) * NCHUNK)
            rd = (f"ps{pgi[n]}",) + ((bias_key,) if bias_key else ())
            wkk = (kevb if dst is evb else kev)[n:n + 1]
            if act is None and bias_sb is not None:
                def ev_fn(e, m=m, n=n, ns=ns):
                    return e.tensor_scalar(
                        dst[:, ns], ps[pgi[n]][:, :],
                        bias_sb[:, bias_off + m:bias_off + m + 1], None,
                        ALU.add)
                pb.op("vector", ev_fn, reads=rd, writes=wkk)
            else:
                def ev_fn(e, n=n, ns=ns):
                    kw = {}
                    if bias_sb is not None:
                        kw["bias"] = bias_sb[:, bias_off + m:bias_off + m + 1]
                    if scale is not None:
                        kw["scale"] = scale
                    return e.activation(dst[:, ns], ps[pgi[n]][:, :],
                                        act if act is not None else AF.Copy,
                                        **kw)
                pb.op("scalar", ev_fn, reads=rd, writes=wkk)
        mr = slice(m * 128, (m + 1) * 128)
        if out_bf is not None and out_f32 is not None:
            pb.op("vector", lambda e: e.tensor_copy(evb[:, :], ev[:, :]),
                  reads=kev, writes=kevb)
        if out_bf is not None:
            pb.dma("sync", out_bf[mr, :], evb[:, :], reads=kevb,
                   writes=(ctx.wkey(out_name, m),) if out_name else ())
        if out_f32 is not None:
            pb.dma("sync", out_f32[mr, :], ev[:, :], reads=kev,
                   writes=(ctx.wkey(outf_name, m),) if outf_name else ())


def emit_load_bias(ctx, sb, sb_key, b_dram, nt, col_off=0):
    pb = ctx.pb
    b2 = b_dram.rearrange("(m p) -> m p", p=128)
    for m in range(nt):
        pb.dma("sync", sb[:, col_off + m:col_off + m + 1],
               b2[m:m + 1, :].transpose([1, 0]), writes=(sb_key,))


NCONST = 16


def const_col(ctx, value):
    cst = ctx.sb("cst", [128, NCONST], F32)
    if not hasattr(ctx, "const_values"):
        ctx.const_values = []
        cin = ctx.din("cst", [128, NCONST])
        ctx.pb.dma("sync", cst[:, :], cin[:, :], writes=("cst",))
    if value not in ctx.const_values:
        assert len(ctx.const_values) < NCONST
        ctx.const_values.append(value)
    return cst[:, ctx.const_values.index(value):
               ctx.const_values.index(value) + 1]


def const_input(ctx):
    v = getattr(ctx, "const_values", [])
    arr = np.zeros((128, NCONST), np.float32)
    for i, x in enumerate(v):
        arr[:, i] = x
    return arr


def emit_stream_elem(ctx, ins, in_names, outs_f32, outs_bf, n_tmp, body,
                     d_tiles=KT, in_keys=None):
    pb = ctx.pb
    nin, nof, nob = len(ins), len(outs_f32), len(outs_bf)
    fs = [ctx.sb(f"fs{i}", [128, T], F32) for i in range(6)]
    bs = [ctx.sb(f"bs{i}", [128, T], BF16) for i in range(2)]
    slots = [(fs[i], i, j) for i in range(6) for j in range(NT)]
    per = nin + nof + n_tmp
    assert 2 * per <= len(slots), (nin, nof, n_tmp)
    bslots = [(bs[i], i, j) for i in range(2) for j in range(NT)]
    assert 2 * nob <= len(bslots)

    def fkey(s):
        return f"fs{s[1]}_{s[2]}"

    def bkey(s):
        return f"bs{s[1]}_{s[2]}"

    def ap(s, w=NCHUNK):
        return s[0][:, s[2] * NCHUNK:s[2] * NCHUNK + w]

    ci = 0
    for d in range(d_tiles):
        rr = slice(d * 128, (d + 1) * 128)
        for c in range(NT):
            cc = slice(c * NCHUNK, (c + 1) * NCHUNK)
            par = ci % 2
            isl = slots[par * per:par * per + nin]
            osl = slots[par * per + nin:par * per + nin + nof]
            tsl = slots[par * per + nin + nof:par * per + per]
            obl = bslots[par * nob:(par + 1) * nob]
            for i, pl in enumerate(ins):
                q = "sync" if i % 2 == 0 else "scalar"
                rk = (in_keys[i](d, c, ci) if in_keys and in_keys[i]
                      else ctx.rkeys(in_names[i]))
                pb.dma(q, ap(isl[i]), pl[rr, cc],
                       reads=rk, writes=(fkey(isl[i]),))
            body(pb, [ap(s) for s in isl], [ap(s) for s in osl],
                 [ap(s) for s in tsl], [ap(s) for s in obl],
                 [fkey(s) for s in isl], [fkey(s) for s in osl],
                 [fkey(s) for s in tsl], [bkey(s) for s in obl],
                 (d, c, ci, "vector"))
            for i, (pl, nm) in enumerate(outs_f32):
                pb.dma("sync", pl[rr, cc], ap(osl[i]), reads=(fkey(osl[i]),),
                       writes=(ctx.wkey(nm, ci),))
            for i, (pl, nm) in enumerate(outs_bf):
                pb.dma("scalar", pl[rr, cc], ap(obl[i]), reads=(bkey(obl[i]),),
                       writes=(ctx.wkey(nm, ci),))
            ci += 1


def emit_l0_elemA(ctx, FQr, FQi, FKr, FKi, FVr, FVi, FXr, FXi, XHr, XHi):
    pb = ctx.pb
    carry = ctx.sb("carryRI", [128, 8], F32)

    def body(pb, ib, ob, tb, bb, ik, ok, tk, bk, info):
        d, c, ci, eng = info
        qr, qi, kr, ki, vr, vi, xr, xi = ib
        xhr, xhi = ob
        t0, t1 = tb
        kqr, kqi, kkr, kki, kvr, kvi, kxr, kxi = ik
        kxhr, kxhi = ok
        kt0, kt1 = tk
        def o(eng, fn, reads=(), writes=(), wide=True):
            pb.op(eng, fn, reads=reads, writes=writes, wide=wide)
        o(eng, lambda e: e.tensor_mul(t0, qr, kr), reads=(kqr, kkr), writes=(kt0,))
        o(eng, lambda e: e.tensor_mul(t1, qi, ki), reads=(kqi, kki), writes=(kt1,), wide=True)
        o(eng, lambda e: e.tensor_add(t0, t0, t1), reads=(kt0, kt1), writes=(kt0,))
        o(eng, lambda e: e.tensor_mul(t1, qr, ki), reads=(kqr, kki), writes=(kt1,), wide=True)
        o(eng, lambda e: e.tensor_mul(xhr, qi, kr), reads=(kqi, kkr), writes=(kxhr,))
        o(eng, lambda e: e.tensor_sub(t1, t1, xhr), reads=(kt1, kxhr), writes=(kt1,), wide=True)
        o(eng, lambda e: e.tensor_mul(xhr, t0, vr), reads=(kt0, kvr), writes=(kxhr,))
        o(eng, lambda e: e.tensor_mul(xhi, t1, vi), reads=(kt1, kvi), writes=(kxhi,), wide=True)
        o(eng, lambda e: e.tensor_sub(xhr, xhr, xhi), reads=(kxhr, kxhi), writes=(kxhr,))
        o(eng, lambda e: e.tensor_mul(t0, t0, vi), reads=(kt0, kvi), writes=(kt0,), wide=True)
        o(eng, lambda e: e.tensor_mul(xhi, t1, vr), reads=(kt1, kvr), writes=(kxhi,))
        o(eng, lambda e: e.tensor_add(t0, t0, xhi), reads=(kt0, kxhi), writes=(kt0,), wide=True)
        if c == 0:
            o("vector", lambda e: e.tensor_tensor_scan(
                xhr, xhr, xhr, 0.0, ALU.add, ALU.bypass), reads=(kxhr,), writes=(kxhr,), wide=False)
            o("vector", lambda e: e.tensor_tensor_scan(
                t0, t0, t0, 0.0, ALU.add, ALU.bypass), reads=(kt0,), writes=(kt0,), wide=False)
        else:
            o("vector", lambda e: e.tensor_tensor_scan(
                xhr, xhr, xhr, carry[:, 0:1], ALU.add, ALU.bypass),
                reads=(kxhr, "carry"), writes=(kxhr,), wide=False)
            o("vector", lambda e: e.tensor_tensor_scan(
                t0, t0, t0, carry[:, 1:2], ALU.add, ALU.bypass),
                reads=(kt0, "carry"), writes=(kt0,), wide=False)
        o("vector", lambda e: e.tensor_copy(carry[:, 0:1], xhr[:, NCHUNK - 1:NCHUNK]),
          reads=(kxhr,), writes=("carry",), wide=False)
        o("vector", lambda e: e.tensor_copy(carry[:, 1:2], t0[:, NCHUNK - 1:NCHUNK]),
          reads=(kt0,), writes=("carry",), wide=False)
        o("vector", lambda e: e.tensor_add(xhr, xr, xhr), reads=(kxr, kxhr), writes=(kxhr,))
        o("vector", lambda e: e.tensor_add(xhi, xi, t0), reads=(kxi, kt0), writes=(kxhi,), wide=True)

    nm = ["FQr", "FQi", "FKr", "FKi", "FVr", "FVi", "FXr", "FXi"]
    emit_stream_elem(ctx, [FQr, FQi, FKr, FKi, FVr, FVi, FXr, FXi], nm,
                     [(XHr, "XHr"), (XHi, "XHi")], [], 2, body, d_tiles=MH,
                     in_keys=[(lambda d, c, ci, n=n: (f"{n}@{d}",))
                              for n in nm])


def emit_l0_elemB(ctx, FSr, FSi, XHr, XHi, G, TAr, TAi):
    pb = ctx.pb

    def body(pb, ib, ob, tb, bb, ik, ok, tk, bk, info):
        eng = info[3]
        sr, si, xhr, xhi = ib
        (g,) = ob
        t0, t1 = tb
        tar, tai = bb
        ksr, ksi, kxhr, kxhi = ik
        (kg,) = ok
        kt0, kt1 = tk
        ktar, ktai = bk
        def o(eng, fn, reads=(), writes=(), wide=True):
            pb.op(eng, fn, reads=reads, writes=writes, wide=wide)
        o(eng, lambda e: e.tensor_mul(t0, sr, xhr), reads=(ksr, kxhr), writes=(kt0,))
        o(eng, lambda e: e.tensor_mul(t1, si, xhi), reads=(ksi, kxhi), writes=(kt1,), wide=True)
        o(eng, lambda e: e.tensor_add(tar, t0, t1), reads=(kt0, kt1), writes=(ktar,), wide=True)
        o(eng, lambda e: e.tensor_mul(t0, sr, xhi), reads=(ksr, kxhi), writes=(kt0,))
        o(eng, lambda e: e.tensor_mul(t1, si, xhr), reads=(ksi, kxhr), writes=(kt1,), wide=True)
        o(eng, lambda e: e.tensor_sub(tai, t0, t1), reads=(kt0, kt1), writes=(ktai,), wide=True)
        o(eng, lambda e: e.tensor_mul(t0, sr, sr), reads=(ksr,), writes=(kt0,))
        o(eng, lambda e: e.tensor_mul(t1, si, si), reads=(ksi,), writes=(kt1,), wide=True)
        o(eng, lambda e: e.tensor_add(g, t0, t1), reads=(kt0, kt1), writes=(kg,), wide=True)

    emit_stream_elem(ctx, [FSr, FSi, XHr, XHi], ["FSr", "FSi", "XHr", "XHi"],
                     [(G, "G")], [(TAr, "TAr"), (TAi, "TAi")], 2, body,
                     d_tiles=MH,
                     in_keys=[lambda d, c, ci: (f"FSr@{d}",),
                              lambda d, c, ci: (f"FSi@{d}",),
                              lambda d, c, ci: (f"XHr@{ci}",),
                              lambda d, c, ci: (f"XHi@{ci}",)])


def dft_constants():
    d = np.arange(D, dtype=np.float64)
    ang = np.outer(d, d) * (2.0 * np.pi / D)
    return np.cos(ang).astype(np.float32), (-np.sin(ang)).astype(np.float32)


MH = 9


def emit_mirror(ctx, plane, name, neg, src_bf=False):
    pb = ctx.pb
    pool = get_pool(ctx)
    xbuf = pool["xbuf"]
    p1 = ctx.sb("p1m", [128, 128], BF16)
    identb = ctx.sb("identb", [128, 128], BF16)
    if getattr(ctx, "_p1_loaded", False) is False:
        pb.dma("gpsimd", p1[:, :], ctx.dram["P1m"][:, :], writes=("p1m",))
        pb.dma("gpsimd", identb[:, :], ctx.dram["ident"][:, :],
               writes=("identb",))
        ctx._p1_loaded = True
    ps = [ctx.ps(f"ps{i}") for i in range(7)]
    if src_bf:
        ev = ctx.sb("bs0", [128, T], BF16)
        kev = FK("bs0")
    else:
        ev = ctx.sb("fs0", [128, T], F32)
        kev = FK("fs0")
    q = "sync" if src_bf else "gpsimd"
    for k in range(8):
        pb.dma(q, xbuf[:, k * T:(k + 1) * T], plane[k * 128:(k + 1) * 128, :],
               reads=ctx.rkeys(name), writes=(f"xbuf{k}",))
    for j in range(7):
        a, b = 6 - j, 7 - j
        for n in range(NT):
            ns = slice(n * NCHUNK, (n + 1) * NCHUNK)
            def mgroup(e, j=j, a=a, b=b, n=n):
                e.matmul(ps[n][:, :], p1[:, :],
                         xbuf[:, a * T + n * NCHUNK:a * T + (n + 1) * NCHUNK],
                         start=True, stop=False)
                return e.matmul(
                    ps[n][:, :], identb[0:1, :],
                    xbuf[0:1, b * T + n * NCHUNK:b * T + (n + 1) * NCHUNK],
                    start=False, stop=True)
            pb.op("tensor", mgroup,
                  reads=(f"xbuf{a}", f"xbuf{b}", "p1m", "identb"),
                  writes=(f"ps{n}",))
            pb.op("vector", lambda e, n=n, ns=ns: e.tensor_scalar(
                ev[:, ns], ps[n][:, :], -1.0 if neg else 1.0, None, ALU.mult),
                reads=(f"ps{n}",), writes=(kev[n],))
        pb.dma("sync", plane[(9 + j) * 128:(10 + j) * 128, :], ev[:, :],
               reads=kev, writes=(ctx.wkey(name, f"M{9 + j}"),))


def emit_fft_pass(ctx, wnm, pl, suffix, srcname):
    pb = ctx.pb
    pool = get_pool(ctx)
    xbuf = pool["xbuf"]
    out = ctx.scratch(pl + suffix, [D, T], F32)
    if srcname is None:
        emit_load_fm(ctx, xbuf, "xbuf", ctx.dram["xn_tm"], "xn_tm")
    else:
        emit_load_plane(ctx, xbuf, "xbuf", ctx.dram[srcname], srcname)
    emit_linear_fm(ctx, wnm, ctx.dram[wnm], xbuf, "xbuf", out_f32=out,
                   outf_name=pl + suffix, mt_range=range(MH))


def emit_l0_front(ctx, x_tm):
    pb = ctx.pb
    pool = get_pool(ctx)
    xbuf = pool["xbuf"]
    xn_tm = ctx.scratch("xn_tm", [T, D], BF16)
    emit_ln(ctx, x_tm, xn_tm)
    planes = {}
    for nm in ("q0", "k0", "v0", "s0", "t0"):
        planes[nm] = ctx.scratch(nm, [D, T], BF16)
    emit_load_fm(ctx, xbuf, "xbuf", xn_tm, "xn_tm")
    for nm, wn in (("q0", "L0_attn_Wq"), ("k0", "L0_attn_Wk"),
                   ("v0", "L0_attn_Wv"), ("s0", "L0_attn_Ws"),
                   ("t0", "L0_attn_Wt")):
        emit_linear_fm(ctx, wn, ctx.dram[wn], xbuf, "xbuf",
                       out_bf=planes[nm], out_name=nm)


def emit_load_plane_w(ctx, dst_sb, dst_key, src_fm, src_name, ktiles,
                      tok0, w, k0=0, dst_slot0=0, q="sync"):
    pb = ctx.pb
    for k in range(ktiles):
        pb.dma(q, dst_sb[:, (dst_slot0 + k) * w:(dst_slot0 + k + 1) * w],
               src_fm[(k0 + k) * 128:(k0 + k + 1) * 128, tok0:tok0 + w],
               reads=ctx.rkeys(src_name), writes=(f"{dst_key}{dst_slot0 + k}",))


def emit_linear_cols(ctx, wname, W_dram, rhs_fn, rhs_keys, Din, Dout,
                     out_specs, tok0=0, tokw=T, bias_sb=None, bias_key=None,
                     bias_off=0, act=None, scale=None):
    pb = ctx.pb
    kt = Din // 128
    mt = Dout // 128
    ntc = tokw // NCHUNK
    wcol = ctx.sb("wcol", [128, 8192], BF16)
    dbuf_w = kt <= 32
    ev = ctx.sb("fs0", [128, T], F32)
    evb = ctx.sb("bs0", [128, T], BF16)
    kev, kevb = FK("fs0"), FK("bs0")
    ps = [ctx.ps(f"ps{i}") for i in range(7)]
    for m in range(mt):
        half = (m % 2) if dbuf_w else 0
        woff = half * kt * 128
        wkeys = []
        for k0 in range(0, kt, 8):
            kn = min(8, kt - k0)
            src = W_dram[k0 * 128:(k0 + kn) * 128, m * 128:(m + 1) * 128]
            src3 = src.rearrange("(j p) m -> p j m", p=128)
            dst = wcol[:, woff + k0 * 128:woff + (k0 + kn) * 128]
            dst3 = dst.rearrange("p (j m) -> p j m", m=128)
            kk = f"wcol{half}_{k0}"
            pb.dma("gpsimd", dst3, src3, reads=ctx.rkeys(wname), writes=(kk,))
            wkeys.append(kk)
        pgi = list(range(ntc))
        for n in range(ntc):
            def mmgroup(e, m=m, n=n, woff=woff):
                last = None
                for k in range(kt):
                    last = e.matmul(
                        ps[pgi[n]][:, :],
                        wcol[:, woff + k * 128:woff + (k + 1) * 128],
                        rhs_fn(k, n),
                        start=(k == 0), stop=(k == kt - 1))
                return last
            pb.op("tensor", mmgroup,
                  reads=tuple(wkeys) + tuple(rhs_keys(n)),
                  writes=(f"ps{pgi[n]}",))
        only_bf = all(s[3] for s in out_specs)
        dst, dst_key = (evb, kevb) if only_bf else (ev, kev)
        for n in range(ntc):
            ns = slice(n * NCHUNK, (n + 1) * NCHUNK)
            rd = (f"ps{pgi[n]}",) + ((bias_key,) if bias_key else ())
            if act is None and bias_sb is not None:
                def ev_fn(e, m=m, n=n, ns=ns):
                    return e.tensor_scalar(
                        dst[:, ns], ps[pgi[n]][:, :],
                        bias_sb[:, bias_off + m:bias_off + m + 1], None,
                        ALU.add)
                pb.op("vector", ev_fn, reads=rd, writes=dst_key[n:n + 1])
            else:
                def ev_fn(e, m=m, n=n, ns=ns):
                    kw = {}
                    if bias_sb is not None:
                        kw["bias"] = bias_sb[:, bias_off + m:bias_off + m + 1]
                    if scale is not None:
                        kw["scale"] = scale
                    return e.activation(dst[:, ns], ps[pgi[n]][:, :],
                                        act if act is not None else AF.Copy,
                                        **kw)
                pb.op("scalar", ev_fn, reads=rd, writes=dst_key[n:n + 1])
        for (plane, pname, is_bf, m_lo, m_hi) in out_specs:
            if not (m_lo <= m < m_hi):
                continue
            mr = slice((m - m_lo) * 128, (m - m_lo + 1) * 128)
            src_b = dst if (is_bf == only_bf or is_bf) else ev
            if is_bf and not only_bf:
                pb.op("vector", lambda e: e.tensor_copy(
                    evb[:, :tokw], ev[:, :tokw]), reads=kev, writes=kevb)
                src_b = evb
            pb.dma("sync", plane[mr, tok0:tok0 + tokw], src_b[:, :tokw],
                   reads=(kevb if is_bf else kev),
                   writes=(ctx.wkey(pname, f"{m}_{tok0}"),))


def emit_l0_mlp(ctx):
    pb = ctx.pb
    pool = get_pool(ctx)
    xbuf, wbuf = pool["xbuf"], pool["wbuf"]
    bias = ctx.sb("bias", [128, 128], F32)
    emit_load_bias(ctx, bias, "bias", ctx.dram["L0_attn_b1"], 64, 0)
    emit_load_bias(ctx, bias, "bias", ctx.dram["L0_attn_b2"], 32, 64)
    h1 = ctx.scratch("h1", [8192, T], BF16)
    Hr = ctx.scratch("Hr", [D, T], F32)
    Hi = ctx.scratch("Hi", [D, T], F32)
    ctx.wcache = None
    emit_load_plane(ctx, xbuf, "xbuf", ctx.dram["TAr"], "TAr")
    emit_load_plane(ctx, wbuf, "wbuf", ctx.dram["TAi"], "TAi")

    def rhs1(k, n):
        buf = xbuf if k < KT else wbuf
        kk = k % KT
        return buf[:, kk * T + n * NCHUNK:kk * T + (n + 1) * NCHUNK]

    def rkeys1(n):
        return tuple(f"xbuf{k}" for k in range(KT)) + \
            tuple(f"wbuf{k}" for k in range(KT))

    emit_linear_cols(ctx, "L0_attn_W1", ctx.dram["L0_attn_W1"], rhs1, rkeys1,
                     2 * D, 4 * D, [(h1, "h1", True, 0, 64)],
                     bias_sb=bias, bias_key="bias", bias_off=0, act=AF.Relu)
    pb.barrier()
    HW = 1024
    for hi in range(2):
        t0 = hi * HW
        emit_load_plane_w(ctx, xbuf, "xbuf", h1, "h1", 32, t0, HW)
        emit_load_plane_w(ctx, wbuf, "wbuf", h1, "h1", 32, t0, HW, k0=32)

        def rhs2(k, n, t0=t0):
            buf = xbuf if k < 32 else wbuf
            kk = k % 32
            return buf[:, kk * HW + n * NCHUNK:kk * HW + (n + 1) * NCHUNK]

        def rkeys2(n):
            return tuple(f"xbuf{k}" for k in range(32)) + \
                tuple(f"wbuf{k}" for k in range(32))

        emit_linear_cols(ctx, "L0_attn_W2", ctx.dram["L0_attn_W2"], rhs2,
                         rkeys2, 4 * D, 2 * D,
                         [(Hr, "Hr", False, 0, 16), (Hi, "Hi", False, 16, 32)],
                         tok0=t0, tokw=HW,
                         bias_sb=bias, bias_key="bias", bias_off=64)


def emit_l0_combine(ctx, XHr, XHi, G, FTr, FTi, Hr, Hi, XHnR, XHnI):
    def body(pb, ib, ob, tb, bb, ik, ok, tk, bk, info):
        eng = info[3]
        xhr, xhi, g, ftr, fti, hr, hi = ib
        t0, t1, t2 = tb
        onr, oni = bb
        kxhr, kxhi, kg, kftr, kfti, khr, khi = ik
        kt0, kt1, kt2 = tk
        konr, koni = bk
        def o(eng, fn, reads=(), writes=(), wide=True):
            pb.op(eng, fn, reads=reads, writes=writes, wide=wide)
        o(eng, lambda e: e.tensor_mul(t0, ftr, hr), reads=(kftr, khr), writes=(kt0,))
        o(eng, lambda e: e.tensor_mul(t1, fti, hi), reads=(kfti, khi), writes=(kt1,), wide=True)
        o(eng, lambda e: e.tensor_sub(t0, t0, t1), reads=(kt0, kt1), writes=(kt0,))
        o(eng, lambda e: e.tensor_mul(t1, g, xhr), reads=(kg, kxhr), writes=(kt1,), wide=True)
        o(eng, lambda e: e.tensor_sub(t2, xhr, t1), reads=(kxhr, kt1), writes=(kt2,))
        o(eng, lambda e: e.tensor_add(onr, t2, t0), reads=(kt2, kt0), writes=(konr,), wide=True)
        o(eng, lambda e: e.tensor_mul(t0, ftr, hi), reads=(kftr, khi), writes=(kt0,))
        o(eng, lambda e: e.tensor_mul(t1, fti, hr), reads=(kfti, khr), writes=(kt1,), wide=True)
        o(eng, lambda e: e.tensor_add(t0, t0, t1), reads=(kt0, kt1), writes=(kt0,))
        o(eng, lambda e: e.tensor_mul(t1, g, xhi), reads=(kg, kxhi), writes=(kt1,), wide=True)
        o(eng, lambda e: e.tensor_sub(t2, xhi, t1), reads=(kxhi, kt1), writes=(kt2,))
        o(eng, lambda e: e.tensor_add(oni, t2, t0), reads=(kt2, kt0), writes=(koni,), wide=True)

    emit_stream_elem(ctx, [XHr, XHi, G, FTr, FTi, Hr, Hi],
                     ["XHr", "XHi", "G", "FTr", "FTi", "Hr", "Hi"],
                     [], [(XHnR, "XHnR"), (XHnI, "XHnI")], 3, body,
                     in_keys=[
                         lambda d, c, ci: (f"XHr@{ci}",) if d < MH
                         else (f"XHr@M{d}",),
                         lambda d, c, ci: (f"XHi@{ci}",) if d < MH
                         else (f"XHi@M{d}",),
                         lambda d, c, ci: (f"G@{ci}",) if d < MH
                         else (f"G@M{d}",),
                         lambda d, c, ci: (f"FTr@{d}",) if d < MH
                         else (f"FTr@M{d}",),
                         lambda d, c, ci: (f"FTi@{d}",) if d < MH
                         else (f"FTi@M{d}",),
                         lambda d, c, ci: (f"Hr@{d}_{0 if c < 2 else 1024}",),
                         lambda d, c, ci: (f"Hi@{d}_{0 if c < 2 else 1024}",)])


def emit_l0_moe(ctx, moe_out):
    pb = ctx.pb
    pool = get_pool(ctx)
    xbuf = pool["xbuf"]
    ps = [ctx.ps(f"ps{i}") for i in range(7)]
    pst = ctx.ps("pst", [128, 512], BF16)
    identb = ctx.sb("identb", [128, 128], BF16)
    pb.dma("gpsimd", identb[:, :], ctx.dram["ident"][:, :], writes=("identb",))
    bs0 = ctx.sb("bs0", [128, T], BF16)
    bs1 = ctx.sb("bs1", [128, T], BF16)
    rnat = bs0
    rT = ctx.sb("rT", [128, 8 * KT], BF16)
    pb.dma("gpsimd", rnat[0:8, :], ctx.dram["L0_ffn_router_w"][:, :],
           writes=FK("bs0"))
    for k in range(KT):
        pb.op("tensor", lambda e, k=k: e.transpose(
            pst[0:128, 0:8], rnat[0:8, k * 128:(k + 1) * 128],
            identb[0:8, 0:8]),
            reads=FK("bs0") + ("identb",), writes=("pst",))
        pb.op("vector", lambda e, k=k: e.tensor_copy(
            rT[:, 8 * k:8 * k + 8], pst[0:128, 0:8]),
            reads=("pst",), writes=("rT",))
    dnat = bs1
    dT = ctx.sb("dT", [128, KT * 128], BF16)
    dn_v = ctx.dram["L0_ffn_down_w"].rearrange("e i d -> (e i) d")
    pb.dma("gpsimd", dnat[:, :], dn_v, writes=FK("bs1"))
    for k in range(KT):
        pb.op("tensor", lambda e, k=k: e.transpose(
            pst[:, 0:128], dnat[:, k * 128:(k + 1) * 128], identb[:, :]),
            reads=FK("bs1") + ("identb",), writes=("pst",))
        pb.op("vector", lambda e, k=k: e.tensor_copy(
            dT[:, k * 128:(k + 1) * 128], pst[:, 0:128]),
            reads=("pst",), writes=("dT",))
    Eb = ctx.sb("Eb", [8, 128], BF16)
    pb.dma("gpsimd", Eb[:, :], ctx.dram["Emoe"][:, :], writes=("Eb",))
    g8 = bs1
    lgt = ctx.sb("lgt", [128, 8], F32)
    lm = ctx.sb("lm", [128, 8], F32)
    top8 = ctx.sb("ltop", [128, 8], F32)
    gsum = ctx.sb("gsum", [128, 8], F32)
    gw = ctx.sb("gw", [128, 8], F32)
    gb = ctx.sb("gb", [128, 8], BF16)
    for tc in range(T // 128):
        def lgroup(e, tc=tc):
            last = None
            for k in range(KT):
                last = e.matmul(ps[6][:, 0:8],
                                xbuf[:, k * T + tc * 128:k * T + (tc + 1) * 128],
                                rT[:, 8 * k:8 * k + 8],
                                start=(k == 0), stop=(k == KT - 1))
            return last
        pb.op("tensor", lgroup,
              reads=tuple(f"xbuf{k}" for k in range(KT)) + ("rT",),
              writes=("ps6",))
        pb.op("vector", lambda e: e.tensor_copy(lgt[:, :], ps[6][:, 0:8]),
              reads=("ps6",), writes=("lgt",))
        pb.op("vector", lambda e: e.reduce_max(lm[:, 0:1], lgt[:, :], axis=AX.X),
              reads=("lgt",), writes=("lm",))
        pb.op("vector", lambda e: e.tensor_scalar_mul(lm[:, 0:1], lm[:, 0:1], -1.0),
              reads=("lm",), writes=("lm",))
        pb.op("scalar", lambda e: e.activation(gw[:, :], lgt[:, :], AF.Exp,
                                               bias=lm[:, 0:1],
                                               accum_out=gsum[:, 0:1]),
              reads=("lgt", "lm"), writes=("gw", "gsum"))
        pb.op("vector", lambda e: e.reciprocal(gsum[:, 1:2], gsum[:, 0:1]),
              reads=("gsum",), writes=("gsum",))
        pb.op("vector", lambda e: e.tensor_scalar(gw[:, :], gw[:, :],
                                                  gsum[:, 1:2], None, ALU.mult),
              reads=("gw", "gsum"), writes=("gw",))
        pb.op("vector", lambda e: e.max(top8[:, :], gw[:, :]),
              reads=("gw",), writes=("ltop",))
        pb.op("vector", lambda e: e.tensor_scalar(lgt[:, :], gw[:, :],
                                                  top8[:, 1:2], None, ALU.is_ge),
              reads=("gw", "ltop"), writes=("lgt",))
        pb.op("vector", lambda e: e.tensor_mul(gb[:, :], gw[:, :], lgt[:, :]),
              reads=("gw", "lgt"), writes=("gb",))
        pb.op("tensor", lambda e, tc=tc: e.transpose(
            pst[0:8, 0:128], gb[:, :], identb[:, :]),
            reads=("gb", "identb"), writes=("pst",))
        pb.op("vector", lambda e, tc=tc: e.tensor_copy(
            g8[0:8, tc * 128:(tc + 1) * 128], pst[0:8, 0:128]),
            reads=("pst",), writes=(FK("bs1")[tc // 4],))
    gate128 = bs0
    kg128 = FK("bs0")
    for n in range(NT):
        ns = slice(n * NCHUNK, (n + 1) * NCHUNK)
        pb.op("tensor", lambda e, ns=ns: e.matmul(
            ps[5][:, :], Eb[:, :], g8[0:8, ns], start=True, stop=True),
            reads=("Eb",) + FK("bs1"), writes=("ps5",))
        pb.op("vector", lambda e, n=n, ns=ns: e.tensor_copy(
            gate128[:, ns], ps[5][:, :]), reads=("ps5",), writes=(kg128[n],))
    av = ctx.sb("fs4", [128, T], F32)
    ag = bs1
    kav, kag = FK("fs4"), FK("bs1")
    for n in range(NT):
        ns = slice(n * NCHUNK, (n + 1) * NCHUNK)
        def dgroup(e, n=n, ns=ns):
            last = None
            for k in range(KT):
                last = e.matmul(ps[n][:, :], dT[:, k * 128:(k + 1) * 128],
                                xbuf[:, k * T + n * NCHUNK:k * T + (n + 1) * NCHUNK],
                                start=(k == 0), stop=(k == KT - 1))
            return last
        pb.op("tensor", dgroup,
              reads=("dT",) + tuple(f"xbuf{k}" for k in range(KT)),
              writes=(f"ps{n}",))
        pb.op("scalar", lambda e, n=n, ns=ns: e.activation(
            av[:, ns], ps[n][:, :], AF.Silu),
            reads=(f"ps{n}",), writes=(kav[n],))
        pb.op("vector", lambda e, n=n, ns=ns: e.tensor_mul(
            ag[:, ns], av[:, ns], gate128[:, ns]),
            reads=(kav[n], kg128[n]), writes=(kag[n],))
    uT = dT
    up_v = ctx.dram["L0_ffn_up_w"].rearrange("e d i -> d e i")
    uv = bs0[:, 0:128]
    for k in range(KT):
        pb.dma("gpsimd", uv.rearrange("p (e i) -> p e i", i=16),
               up_v[k * 128:(k + 1) * 128, :, :], writes=("bs0_0",))
        pb.op("tensor", lambda e, k=k: e.transpose(
            pst[:, 0:128], uv[:, :], identb[:, :]),
            reads=("bs0_0", "identb"), writes=("pst",))
        pb.op("vector", lambda e, k=k: e.tensor_copy(
            uT[:, k * 128:(k + 1) * 128], pst[:, 0:128]),
            reads=("pst",), writes=("dT",))
    ev = ctx.sb("fs5", [128, T], F32)
    kev = FK("fs5")
    for m in range(KT):
        for n in range(NT):
            ns = slice(n * NCHUNK, (n + 1) * NCHUNK)
            pb.op("tensor", lambda e, m=m, n=n, ns=ns: e.matmul(
                ps[4 + (m % 2)][:, :], uT[:, m * 128:(m + 1) * 128],
                ag[:, ns], start=True, stop=True),
                reads=("dT",) + kag, writes=(f"ps{4 + (m % 2)}",))
            pb.op("vector", lambda e, m=m, n=n, ns=ns: e.tensor_copy(
                ev[:, ns], ps[4 + (m % 2)][:, :]),
                reads=(f"ps{4 + (m % 2)}",), writes=(kev[n],))
        pb.dma("sync", moe_out[m * 128:(m + 1) * 128, :], ev[:, :],
               reads=kev, writes=(ctx.wkey("moe_out", m),))


def emit_l0_delta(ctx, A1, A2, MOE, delta):
    def body(pb, ib, ob, tb, bb, ik, ok, tk, bk, info):
        eng = info[3]
        a1, a2, mo = ib
        (dl,) = bb
        ka1, ka2, kmo = ik
        (kdl,) = bk
        def o(eng, fn, reads=(), writes=(), wide=True):
            pb.op(eng, fn, reads=reads, writes=writes, wide=wide)
        (t0,) = tb
        (kt0,) = tk
        o(eng, lambda e: e.tensor_add(t0, a1, a2), reads=(ka1, ka2), writes=(kt0,), wide=True)
        o(eng, lambda e: e.tensor_scalar(t0, t0, 1.0 / D, None, ALU.mult),
          reads=(kt0,), writes=(kt0,))
        o(eng, lambda e: e.tensor_add(dl, t0, mo), reads=(kt0, kmo), writes=(kdl,), wide=True)

    emit_stream_elem(ctx, [A1, A2, MOE], ["A1", "A2", "moe_out"],
                     [], [(delta, "delta")], 1, body,
                     in_keys=[lambda d, c, ci: (f"A1@{d}",),
                              lambda d, c, ci: (f"A2@{d}",),
                              lambda d, c, ci: (f"moe_out@{d}",)])


def emit_transpose_f32(ctx, src_fm, src_name, dst_tm, dst_name):
    pb = ctx.pb
    identf = ctx.sb("identf", [128, 128], F32)
    pb.dma("sync", identf[:, :], ctx.dram["ident"][:, :], writes=("identf",))
    stage = ctx.sb("fs0", [128, T], F32)
    tin = [ctx.sb(f"tt{i}", [128, 128], F32) for i in range(2)]
    kst = FK("fs0")
    ps = ctx.ps("ps0")
    for tt in range(T // 128):
        for dt in range(KT):
            i = dt % 2
            pb.dma("sync", tin[i][:, :],
                   src_fm[dt * 128:(dt + 1) * 128, tt * 128:(tt + 1) * 128],
                   reads=ctx.rkeys(src_name), writes=(f"tt{i}",))
            pb.op("tensor", lambda e, i=i: e.transpose(
                ps[:, 0:128], tin[i][:, :], identf[:, :]),
                reads=(f"tt{i}", "identf"), writes=("ps0",))
            pb.op("vector", lambda e, dt=dt: e.tensor_copy(
                stage[:, dt * 128:(dt + 1) * 128], ps[:, 0:128]),
                reads=("ps0",), writes=(kst[dt // 4],))
        pb.dma("sync", dst_tm[tt * 128:(tt + 1) * 128, :], stage[:, :],
               reads=kst, writes=(ctx.wkey(dst_name, tt),))


def emit_l0(ctx, x_tm, delta_out):
    pb = ctx.pb
    pool = get_pool(ctx)
    xbuf = pool["xbuf"]
    emit_l0_front(ctx, x_tm)
    moe_out = ctx.scratch("moe_out", [D, T], F32)
    emit_l0_moe(ctx, moe_out)
    for pl, srcn in (("FX", None), ("FQ", "q0"), ("FK", "k0"),
                     ("FV", "v0"), ("FS", "s0"), ("FT", "t0")):
        emit_fft_pass(ctx, "Cdft", pl, "r", srcn)
    for pl, srcn in (("FX", None), ("FQ", "q0"), ("FK", "k0"),
                     ("FV", "v0"), ("FS", "s0"), ("FT", "t0")):
        emit_fft_pass(ctx, "NSdft", pl, "i", srcn)
    XHr = ctx.scratch("XHr", [D, T], F32)
    XHi = ctx.scratch("XHi", [D, T], F32)
    emit_l0_elemA(ctx, ctx.dram["FQr"], ctx.dram["FQi"], ctx.dram["FKr"],
                  ctx.dram["FKi"], ctx.dram["FVr"], ctx.dram["FVi"],
                  ctx.dram["FXr"], ctx.dram["FXi"], XHr, XHi)
    G = ctx.scratch("G", [D, T], F32)
    TAr = ctx.scratch("TAr", [D, T], BF16)
    TAi = ctx.scratch("TAi", [D, T], BF16)
    emit_l0_elemB(ctx, ctx.dram["FSr"], ctx.dram["FSi"], XHr, XHi,
                  G, TAr, TAi)
    pb.barrier()
    emit_mirror(ctx, TAr, "TAr", False, src_bf=True)
    emit_mirror(ctx, TAi, "TAi", True, src_bf=True)
    emit_mirror(ctx, XHr, "XHr", False)
    emit_mirror(ctx, XHi, "XHi", True)
    emit_mirror(ctx, G, "G", False)
    emit_mirror(ctx, ctx.dram["FTr"], "FTr", False)
    emit_mirror(ctx, ctx.dram["FTi"], "FTi", True)
    pb.barrier()
    emit_l0_mlp(ctx)
    pb.barrier()
    XHnR = ctx.scratch("XHnR", [D, T], BF16)
    XHnI = ctx.scratch("XHnI", [D, T], BF16)
    emit_l0_combine(ctx, ctx.dram["XHr"], ctx.dram["XHi"], ctx.dram["G"],
                    ctx.dram["FTr"], ctx.dram["FTi"], ctx.dram["Hr"],
                    ctx.dram["Hi"], XHnR, XHnI)
    ctx.wcache = None
    A1 = ctx.scratch("A1", [D, T], F32)
    A2 = ctx.scratch("A2", [D, T], F32)
    emit_load_plane(ctx, xbuf, "xbuf", XHnR, "XHnR",
                    key_fn=lambda k: tuple(f"XHnR@{4 * k + j}" for j in range(4)))
    emit_linear_fm(ctx, "Cdft", ctx.dram["Cdft"], xbuf, "xbuf", out_f32=A1,
                   outf_name="A1")
    emit_load_plane(ctx, xbuf, "xbuf", XHnI, "XHnI",
                    key_fn=lambda k: tuple(f"XHnI@{4 * k + j}" for j in range(4)))
    emit_linear_fm(ctx, "NSdft", ctx.dram["NSdft"], xbuf, "xbuf", out_f32=A2,
                   outf_name="A2")
    emit_l1_ffn_prep(ctx)
    emit_l0_delta(ctx, A1, A2, moe_out, delta_out)


def emit_l1_attn(ctx, attnO_tm):
    pb = ctx.pb
    pool = get_pool(ctx)
    xbuf, wbuf = pool["xbuf"], pool["wbuf"]
    bias = ctx.sb("bias", [128, 128], F32)
    identb = ctx.sb("identb", [128, 128], BF16)
    pb.dma("gpsimd", identb[:, :], ctx.dram["ident"][:, :], writes=("identb",))
    qT = ctx.scratch("qT", [D, T], BF16)
    kT = ctx.scratch("kT", [D, T], BF16)
    v_tm = ctx.scratch("v_tm", [T, D], BF16)
    emit_load_fm(ctx, xbuf, "xbuf", ctx.dram["xn1_tm"], "xn1_tm")
    emit_load_bias(ctx, bias, "bias", ctx.dram["L1_attn_bq"], KT, 0)
    emit_load_bias(ctx, bias, "bias", ctx.dram["L1_attn_bk"], KT, 16)
    emit_linear_fm(ctx, "L1_attn_Wq", ctx.dram["L1_attn_Wq"], xbuf, "xbuf",
                   out_bf=qT, out_name="qT", bias_sb=bias, bias_key="bias",
                   bias_off=0)
    emit_linear_fm(ctx, "L1_attn_Wk", ctx.dram["L1_attn_Wk"], xbuf, "xbuf",
                   out_bf=kT, out_name="kT", bias_sb=bias, bias_key="bias",
                   bias_off=16)
    ps = [ctx.ps(f"ps{i}") for i in range(7)]
    qst = ctx.sb("dT", [128, KT * 128], BF16)
    onescol = qst[0:1, 0:128]
    pb.op("vector", lambda e: e.memset(onescol, 1.0), writes=("qst_0",))
    bs1 = ctx.sb("bs1", [128, T], BF16)
    bvrow = bs1[0:1, :]
    pb.dma("gpsimd", bvrow,
           ctx.dram["L1_attn_bv"].rearrange("(one d) -> one d", one=1),
           writes=FK("bs1"))
    ctx.wcache = None
    for k in range(KT):
        pb.dma("gpsimd", wbuf[:, k * D:(k + 1) * D],
               ctx.dram["L1_attn_Wv"][k * 128:(k + 1) * 128, :],
               writes=(f"wbuf{k}",))
    evb = ctx.sb("bs0", [128, T], BF16)
    kevb = FK("bs0")
    for tc in range(T // 128):
        for n in range(NT):
            ns = slice(n * NCHUNK, (n + 1) * NCHUNK)
            def vgroup(e, tc=tc, n=n, ns=ns):
                for k in range(KT):
                    e.matmul(ps[n][:, :],
                             xbuf[:, k * T + tc * 128:k * T + (tc + 1) * 128],
                             wbuf[:, k * D + n * NCHUNK:k * D + (n + 1) * NCHUNK],
                             start=(k == 0), stop=False)
                return e.matmul(ps[n][:, :], onescol, bvrow[0:1, ns],
                                start=False, stop=True)
            pb.op("tensor", vgroup,
                  reads=tuple(f"xbuf{k}" for k in range(KT)) +
                  tuple(f"wbuf{k}" for k in range(KT)) + ("qst_0",) + FK("bs1"),
                  writes=(f"ps{n}",))
            pb.op("scalar", lambda e, n=n, ns=ns: e.activation(
                evb[:, ns], ps[n][:, :], AF.Copy),
                reads=(f"ps{n}",), writes=(kevb[n],))
        pb.dma("sync", v_tm[tc * 128:(tc + 1) * 128, :], evb[:, :],
               reads=kevb, writes=(ctx.wkey("v_tm", tc),))
    pb.barrier()
    emit_load_plane(ctx, xbuf, "xbuf", kT, "kT")
    for j in range(KT):
        pb.dma("sync", wbuf[:, j * D:(j + 1) * D],
               v_tm[j * 128:(j + 1) * 128, :], reads=ctx.rkeys("v_tm"),
               writes=(f"wbuf{j}",))
    causalF = ctx.sb("tt0", [128, 128], F32)
    pb.dma("sync", causalF[:, :], ctx.dram["causal"][:, :], writes=("tt0",))
    fsq = [ctx.sb(f"fs{i}", [128, T], F32) for i in range(6)]
    pbf = bs1
    kpbf = FK("bs1")
    pTb = ctx.sb("bs0", [128, T], BF16)
    lm = ctx.sb("ln_mean", [128, 8], F32)
    lsum = ctx.sb("ln_ssq", [128, 8], F32)
    pst = ctx.ps("pst", [128, 512], BF16)
    for qc in range(T // 128):
        par = qc % 2
        sc, pex, ev = fsq[3 * par], fsq[3 * par + 1], fsq[3 * par + 2]
        ksc, kpex, kev2 = FK(f"fs{3 * par}"), FK(f"fs{3 * par + 1}"), \
            FK(f"fs{3 * par + 2}")
        lmc, lsc = 4 * par, 4 * par
        nkv = qc + 1
        W = nkv * 128
        nch = (W + NCHUNK - 1) // NCHUNK
        for k in range(KT):
            pb.dma("scalar", qst[:, k * 128:(k + 1) * 128],
                   qT[k * 128:(k + 1) * 128, qc * 128:(qc + 1) * 128],
                   reads=ctx.rkeys("qT"), writes=(f"qst_{k}",))
        for n in range(nch):
            w = min(NCHUNK, W - n * NCHUNK)
            def sgroup(e, n=n, w=w):
                last = None
                for k in range(KT):
                    last = e.matmul(
                        ps[n][:, 0:w], qst[:, k * 128:(k + 1) * 128],
                        xbuf[:, k * T + n * NCHUNK:k * T + n * NCHUNK + w],
                        start=(k == 0), stop=(k == KT - 1))
                return last
            pb.op("tensor", sgroup,
                  reads=tuple(f"qst_{k}" for k in range(KT)) +
                  tuple(f"xbuf{k}" for k in range(KT)),
                  writes=(f"ps{n}",))
            pb.op("vector", lambda e, n=n, w=w: e.tensor_scalar(
                sc[:, n * NCHUNK:n * NCHUNK + w], ps[n][:, 0:w],
                float(D) ** -0.5, None, ALU.mult),
                reads=(f"ps{n}",), writes=(ksc[n],))
        pb.op("vector", lambda e, qc=qc: e.tensor_add(
            sc[:, qc * 128:(qc + 1) * 128], sc[:, qc * 128:(qc + 1) * 128],
            causalF[:, :]),
            reads=(ksc[qc // 4], "tt0"), writes=(ksc[qc // 4],))
        pb.op("vector", lambda e, W=W, lmc=lmc: e.reduce_max(
            lm[:, lmc:lmc + 1], sc[:, 0:W], axis=AX.X),
            reads=ksc[:nch], writes=(f"l1lm{par}",))
        pb.op("vector", lambda e, lmc=lmc: e.tensor_scalar_mul(
            lm[:, lmc:lmc + 1], lm[:, lmc:lmc + 1], -1.0),
            reads=(f"l1lm{par}",), writes=(f"l1lm{par}",))
        pb.op("scalar", lambda e, W=W, lmc=lmc, lsc=lsc: e.activation(
            pex[:, 0:W], sc[:, 0:W], AF.Exp, bias=lm[:, lmc:lmc + 1],
            accum_out=lsum[:, lsc:lsc + 1]),
            reads=ksc[:nch] + (f"l1lm{par}",),
            writes=kpex[:nch] + (f"l1ls{par}",))
        pb.op("vector", lambda e, lsc=lsc: e.reciprocal(
            lsum[:, lsc + 1:lsc + 2], lsum[:, lsc:lsc + 1]),
            reads=(f"l1ls{par}",), writes=(f"l1ls{par}",))
        pb.op("vector", lambda e, W=W, lsc=lsc: e.tensor_scalar(
            pbf[:, 0:W], pex[:, 0:W], lsum[:, lsc + 1:lsc + 2], None,
            ALU.mult),
            reads=kpex[:nch] + (f"l1ls{par}",), writes=kpbf[:nch])
        for j in range(nkv):
            pb.op("tensor", lambda e, j=j: e.transpose(
                pst[:, 0:128], pbf[:, j * 128:(j + 1) * 128], identb[:, :]),
                reads=(kpbf[j // 4], "identb"), writes=("pst",))
            pb.op("vector", lambda e, j=j: e.tensor_copy(
                pTb[:, j * 128:(j + 1) * 128], pst[:, 0:128]),
                reads=("pst",), writes=(FK("bs0")[j // 4],))
        for n in range(NT):
            ns = slice(n * NCHUNK, (n + 1) * NCHUNK)
            def ogroup(e, n=n, nkv=nkv, ns=ns):
                last = None
                for j in range(nkv):
                    last = e.matmul(
                        ps[4 + n % 2][:, :], pTb[:, j * 128:(j + 1) * 128],
                        wbuf[:, j * D + n * NCHUNK:j * D + (n + 1) * NCHUNK],
                        start=(j == 0), stop=(j == nkv - 1))
                return last
            pb.op("tensor", ogroup,
                  reads=FK("bs0")[:(nkv + 3) // 4] +
                  tuple(f"wbuf{j}" for j in range(nkv)),
                  writes=(f"ps{4 + n % 2}",))
            pb.op("vector", lambda e, n=n, ns=ns: e.tensor_copy(
                ev[:, ns], ps[4 + n % 2][:, :]),
                reads=(f"ps{4 + n % 2}",), writes=(kev2[n],))
        pb.dma("sync", attnO_tm[qc * 128:(qc + 1) * 128, :], ev[:, :],
               reads=kev2, writes=(ctx.wkey("attnO", qc),))


def emit_weight_transpose(ctx, wname, W_dram, out_plane, out_name, R, C):
    pb = ctx.pb
    identb = ctx.sb("identb", [128, 128], BF16)
    pb.dma("gpsimd", identb[:, :], ctx.dram["ident"][:, :], writes=("identb",))
    band = ctx.sb("bs0", [128, T], BF16)
    stg = ctx.sb("bs1", [128, T], BF16)
    pst = ctx.ps("pst", [128, 512], BF16)
    kb, ks = FK("bs0"), FK("bs1")
    for r in range(R // 128):
        pb.dma("gpsimd", band[:, 0:C], W_dram[r * 128:(r + 1) * 128, :],
               writes=kb)
        for c in range(C // 128):
            pb.op("tensor", lambda e, c=c: e.transpose(
                pst[:, 0:128], band[:, c * 128:(c + 1) * 128], identb[:, :]),
                reads=kb + ("identb",), writes=("pst",))
            pb.op("vector", lambda e, c=c: e.tensor_copy(
                stg[:, c * 128:(c + 1) * 128], pst[:, 0:128]),
                reads=("pst",), writes=(ks[c // 4],))
        for c in range(C // 128):
            pb.dma("sync", out_plane[c * 128:(c + 1) * 128,
                                     r * 128:(r + 1) * 128],
                   stg[:, c * 128:(c + 1) * 128], reads=(ks[c // 4],),
                   writes=(ctx.wkey(out_name, f"{r}_{c}"),))


def emit_l1_ffn_prep(ctx):
    pb = ctx.pb
    Wb_in = ctx.scratch("Wb_in", [3 * D, D], BF16)
    Wb_wr = ctx.scratch("Wb_wr", [D, D], BF16)
    Wb_op = ctx.scratch("Wb_op", [D, D], BF16)
    cstg = [ctx.sb("fs4", [128, T], F32), ctx.sb("fs5", [128, T], F32)]
    bstg = [ctx.sb("bs0", [128, T], BF16), ctx.sb("bs1", [128, T], BF16)]
    for wn, wb, rr in (("L1_ffn_in_proj_w", Wb_in, 3 * D),
                       ("L1_ffn_write_w", Wb_wr, D),
                       ("L1_ffn_out_proj_w", Wb_op, D)):
        for r in range(rr // 128):
            i = r % 2
            sl = bstg[i][:, :]
            kk = FK(f"bs{i}")
            pb.dma("gpsimd", sl, ctx.dram[wn][r * 128:(r + 1) * 128, :],
                   writes=kk)
            pb.dma("sync", wb[r * 128:(r + 1) * 128, :], sl, reads=kk,
                   writes=(ctx.wkey("b" + wn, r),))


def emit_l1_ffn(ctx, ffnO_tm):
    pb = ctx.pb
    NB, S, HD, NH = 2, 16, 64, 32
    EPS = 1e-6
    pool = get_pool(ctx)
    xbuf, wbuf = pool["xbuf"], pool["wbuf"]
    ps = [ctx.ps(f"ps{i}") for i in range(7)]
    pst = ctx.ps("pst", [128, 512], BF16)
    identb = ctx.sb("identb", [128, 128], BF16)
    identf = ctx.sb("identf", [128, 128], F32)
    pb.dma("sync", identf[:, :], ctx.dram["ident"][:, :], writes=("identf",))
    bias = ctx.sb("bias", [128, 128], F32)
    fs = [ctx.sb(f"fs{i}", [128, T], F32) for i in range(6)]
    bs = [ctx.sb(f"bs{i}", [128, T], BF16) for i in range(2)]

    Wb_in = ctx.dram["Wb_in"]
    Wb_wr = ctx.dram["Wb_wr"]
    Wb_op = ctx.dram["Wb_op"]

    onesf = ctx.sb("onesf", [1, 128], F32)
    pb.op("vector", lambda e: e.memset(onesf[:, :], 1.0), writes=("onesf",))
    wcolbuf = ctx.sb("wcol", [128, 8192], BF16)
    cwT = wcolbuf[:, 0:256]
    cw = fs[4]
    wpre = fs[5]
    kcw, kwpre = FK("fs4"), FK("fs5")
    pb.dma("sync", cw[0:16, :], ctx.dram["L1_ffn_compete_w"][:, :], writes=kcw)
    pb.dma("sync", wpre[0:1, :],
           ctx.dram["L1_ffn_norm_pre_w"].rearrange("(o d) -> o d", o=1),
           writes=kwpre)
    for n in range(NT):
        ns = slice(n * NCHUNK, (n + 1) * NCHUNK)
        pb.op("tensor", lambda e, ns=ns: e.matmul(
            ps[6][0:16, :], onesf[0:1, 0:16], wpre[0:1, ns],
            start=True, stop=True), reads=("onesf",) + kwpre, writes=("ps6",))
        pb.op("vector", lambda e, n=n, ns=ns: e.tensor_mul(
            cw[0:16, ns], cw[0:16, ns], ps[6][0:16, :]),
            reads=(kcw[n], "ps6"), writes=(kcw[n],))
    cwb = bs[0]
    pb.op("vector", lambda e: e.tensor_copy(cwb[0:16, :], cw[0:16, :]),
          reads=kcw, writes=FK("bs0"))
    for k in range(KT):
        pb.op("tensor", lambda e, k=k: e.transpose(
            pst[0:128, 0:16], cwb[0:16, k * 128:(k + 1) * 128],
            identb[0:16, 0:16]), reads=FK("bs0") + ("identb",), writes=("pst",))
        pb.op("vector", lambda e, k=k: e.tensor_copy(
            cwT[:, k * 16:(k + 1) * 16], pst[0:128, 0:16]),
            reads=("pst",), writes=("cwT",))
    cbF = bias[:, 80:96]
    cbrow = cbF[0:1, 0:16]
    pb.dma("sync", cbrow[0:1, :],
           ctx.dram["L1_ffn_compete_b"].rearrange("(o s) -> o s", o=1),
           writes=("cbrow",))
    pb.op("tensor", lambda e: e.matmul(ps[6][:, 0:16], onesf[0:1, :],
                                       cbrow[0:1, :], start=True, stop=True),
          reads=("onesf", "cbrow"), writes=("ps6",))
    pb.op("vector", lambda e: e.tensor_copy(cbF[:, :], ps[6][:, 0:16]),
          reads=("ps6",), writes=("cbF",))
    emit_load_bias(ctx, bias, "bias", ctx.dram["L1_ffn_write_b"], KT, 32)
    emit_load_bias(ctx, bias, "bias", ctx.dram["L1_ffn_out_proj_b"], KT, 48)
    emit_load_bias(ctx, bias, "bias", ctx.dram["L1_ffn_in_proj_b"], KT, 64)
    bkv = wcolbuf[0:1, 1024:1024 + T]
    ipb = ctx.dram["L1_ffn_in_proj_b"].rearrange("(th d) -> th d", d=D)
    onesb = wcolbuf[0:1, 3072:3072 + 128]
    pb.op("vector", lambda e: e.memset(onesb, 1.0), writes=("onesb",))
    wsP = ctx.scratch("wsP", [S, D], F32)
    pb.dma("sync", fs[4][0:16, :], ctx.dram["L1_ffn_workspace"][:, :],
           writes=kcw)
    pb.dma("sync", wsP[:, :], fs[4][0:16, :], reads=kcw,
           writes=(ctx.wkey("wsP", 0),))
    rstdp = bias[:, 96:112]
    st8 = ctx.sb("ln_var", [128, 8], F32)
    top8 = bias[0:16, 112:120]
    wgt4 = bias[0:16, 120:128]
    wsT = wcolbuf[:, 256:512]
    kkT = wcolbuf[:, 512:768]
    gT = wcolbuf[:, 768:1024]
    pb.barrier()

    x_tm_f = "xn1f_tm"
    x_tm_b = "xn1_tm"
    xa = ctx.scratch("l1xa", [T, D], F32)
    xab = ctx.scratch("l1xab", [T, D], BF16)
    wxP = ctx.scratch("wxP", [D, T], BF16)
    q1P = ctx.scratch("q1P", [D, T], BF16)
    ctxP = ctx.scratch("ctxP", [T, D], BF16)
    xbP = ctx.scratch("xbP", [T, D], F32)

    for nb in range(NB):
        last = nb == NB - 1
        emit_load_fm(ctx, xbuf, "xbuf", ctx.dram[x_tm_b], x_tm_b)
        ctx.wcache = None
        emit_linear_fm(ctx, "WT_wr", Wb_wr, xbuf, "xbuf", out_bf=wxP,
                       out_name="wxP", bias_sb=bias, bias_key="bias",
                       bias_off=32, w_transpose=True)
        emit_linear_fm(ctx, "WT_in_q", Wb_in[0:D, :], xbuf, "xbuf",
                       out_bf=q1P, out_name="q1P", bias_sb=bias,
                       bias_key="bias", bias_off=64, w_transpose=True)
        for t in range(T // 128):
            i = t % 2
            kx = FK(f"fs{i}")
            pb.dma("sync", fs[i][:, :], ctx.dram[x_tm_f][t * 128:(t + 1) * 128, :],
                   reads=ctx.rkeys(x_tm_f), writes=kx)
            pb.op("scalar", lambda e, i=i: e.activation(
                fs[2][:, :], fs[i][:, :], AF.Square, accum_out=st8[:, 0:1]),
                reads=kx, writes=FK("fs2") + ("st8",))
            pb.op("vector", lambda e: e.tensor_scalar(
                st8[:, 1:2], st8[:, 0:1], 1.0 / D, EPS, ALU.mult, ALU.add),
                reads=("st8",), writes=("st8",))
            pb.op("scalar", lambda e: e.activation(
                st8[:, 2:3], st8[:, 1:2], AF.Sqrt), reads=("st8",),
                writes=("st8",))
            pb.op("vector", lambda e, t=t: e.reciprocal(
                rstdp[:, t:t + 1], st8[:, 2:3]), reads=("st8",),
                writes=("rstdp",))
        lsm = fs[1]
        klsm = FK("fs1")
        for tc in range(T // 128):
            def lgroup(e, tc=tc):
                last_i = None
                for k in range(KT):
                    last_i = e.matmul(
                        ps[6][:, 0:16],
                        xbuf[:, k * T + tc * 128:k * T + (tc + 1) * 128],
                        cwT[:, k * 16:(k + 1) * 16],
                        start=(k == 0), stop=(k == KT - 1))
                return last_i
            pb.op("tensor", lgroup,
                  reads=tuple(f"xbuf{k}" for k in range(KT)) + ("cwT",),
                  writes=("ps6",))
            pb.op("vector", lambda e, tc=tc: e.tensor_scalar(
                fs[0][:, 0:16], ps[6][:, 0:16], rstdp[:, tc:tc + 1], None,
                ALU.mult), reads=("ps6", "rstdp"), writes=("fs0_0",))
            pb.op("vector", lambda e: e.tensor_add(
                fs[0][:, 0:16], fs[0][:, 0:16], cbF[:, :]),
                reads=("fs0_0", "cbF"), writes=("fs0_0",))
            pb.op("tensor", lambda e, tc=tc: e.transpose(
                ps[6][0:16, 0:128], fs[0][:, 0:16], identf[:, :]),
                reads=("fs0_0", "identf"), writes=("ps6",))
            pb.op("vector", lambda e, tc=tc: e.tensor_copy(
                lsm[0:16, tc * 128:(tc + 1) * 128], ps[6][0:16, 0:128]),
                reads=("ps6",), writes=(klsm[tc // 4],))
        pb.op("vector", lambda e: e.max(top8[0:16, :], lsm[0:16, :]),
              reads=klsm, writes=("t4top",))
        pb.op("vector", lambda e: e.tensor_scalar(
            wgt4[0:16, 0:4], top8[0:16, 0:4], top8[0:16, 0:1], None,
            ALU.subtract), reads=("t4top",), writes=("t4wgt",))
        pb.op("scalar", lambda e: e.activation(
            wgt4[0:16, 0:4], wgt4[0:16, 0:4], AF.Exp,
            accum_out=wgt4[0:16, 4:5]), reads=("t4wgt",), writes=("t4wgt",))
        pb.op("vector", lambda e: e.reciprocal(wgt4[0:16, 5:6],
                                               wgt4[0:16, 4:5]),
              reads=("t4wgt",), writes=("t4wgt",))
        pb.op("vector", lambda e: e.tensor_scalar(
            wgt4[0:16, 0:4], wgt4[0:16, 0:4], wgt4[0:16, 5:6], None,
            ALU.mult), reads=("t4wgt",), writes=("t4wgt",))
        G = fs[2]
        kG = FK("fs2")
        pb.op("vector", lambda e: e.memset(G[0:16, :], 0.0), writes=kG)
        for kk4 in range(4):
            pb.op("vector", lambda e, kk4=kk4: e.tensor_scalar(
                fs[3][0:16, :], lsm[0:16, :], top8[0:16, kk4:kk4 + 1], None,
                ALU.is_equal), reads=klsm + ("t4top",), writes=FK("fs3"))
            pb.op("vector", lambda e, kk4=kk4: e.scalar_tensor_tensor(
                G[0:16, :], fs[3][0:16, :], wgt4[0:16, kk4:kk4 + 1],
                G[0:16, :], ALU.mult, ALU.add),
                reads=FK("fs3") + ("t4wgt",) + kG, writes=kG)
        pb.op("vector", lambda e: e.tensor_copy(bs[0][0:16, :], G[0:16, :]),
              reads=kG, writes=FK("bs0"))
        for k in range(KT):
            pb.op("tensor", lambda e, k=k: e.transpose(
                pst[0:128, 0:16], bs[0][0:16, k * 128:(k + 1) * 128],
                identb[0:16, 0:16]), reads=FK("bs0") + ("identb",),
                writes=("pst",))
            pb.op("vector", lambda e, k=k: e.tensor_copy(
                gT[:, k * 16:(k + 1) * 16], pst[0:128, 0:16]),
                reads=("pst",), writes=("gT",))
        pb.barrier()
        pb.barrier()
        ctx.wcache = None
        for j in range(KT):
            pb.dma("sync", wbuf[:, j * D:(j + 1) * D],
                   wxP[:, j * 128:(j + 1) * 128], transpose=True,
                   reads=ctx.rkeys("wxP"), writes=(f"wbuf{j}",))
        wr16 = fs[3]
        kwr = FK("fs3")
        for n in range(NT):
            ns = slice(n * NCHUNK, (n + 1) * NCHUNK)
            def wgroup(e, n=n, ns=ns):
                last_i = None
                for j in range(KT):
                    last_i = e.matmul(
                        ps[n][0:16, :], gT[:, j * 16:(j + 1) * 16],
                        wbuf[:, j * D + n * NCHUNK:j * D + (n + 1) * NCHUNK],
                        start=(j == 0), stop=(j == KT - 1))
                return last_i
            pb.op("tensor", wgroup,
                  reads=("gT",) + tuple(f"wbuf{j}" for j in range(KT)),
                  writes=(f"ps{n}",))
            pb.op("vector", lambda e, n=n, ns=ns: e.tensor_copy(
                wr16[0:16, ns], ps[n][0:16, :]), reads=(f"ps{n}",),
                writes=(kwr[n],))
        pb.dma("sync", fs[4][0:16, :], wsP[:, :], reads=ctx.rkeys("wsP"),
               writes=kcw)
        pb.op("vector", lambda e: e.tensor_add(
            fs[4][0:16, :], fs[4][0:16, :], wr16[0:16, :]),
            reads=kcw + kwr, writes=kcw)
        pb.op("scalar", lambda e: e.activation(
            fs[0][0:16, :], fs[4][0:16, :], AF.Square,
            accum_out=st8[0:16, 0:1]), reads=kcw,
            writes=FK("fs0") + ("st8",))
        pb.op("vector", lambda e: e.tensor_scalar(
            st8[0:16, 1:2], st8[0:16, 0:1], 1.0 / D, EPS, ALU.mult, ALU.add),
            reads=("st8",), writes=("st8",))
        pb.op("scalar", lambda e: e.activation(
            st8[0:16, 2:3], st8[0:16, 1:2], AF.Sqrt), reads=("st8",),
            writes=("st8",))
        pb.op("vector", lambda e: e.reciprocal(st8[0:16, 3:4], st8[0:16, 2:3]),
              reads=("st8",), writes=("st8",))
        pb.op("vector", lambda e: e.tensor_scalar(
            fs[4][0:16, :], fs[4][0:16, :], st8[0:16, 3:4], None, ALU.mult),
            reads=kcw + ("st8",), writes=kcw)
        pb.dma("sync", fs[5][0:1, :],
               ctx.dram["L1_ffn_norm_post_w"].rearrange("(o d) -> o d", o=1),
               writes=kwpre)
        for n in range(NT):
            ns = slice(n * NCHUNK, (n + 1) * NCHUNK)
            pb.op("tensor", lambda e, ns=ns: e.matmul(
                ps[6][0:16, :], onesf[0:1, 0:16], fs[5][0:1, ns],
                start=True, stop=True), reads=("onesf",) + kwpre,
                writes=("ps6",))
            pb.op("vector", lambda e, n=n, ns=ns: e.tensor_mul(
                fs[4][0:16, ns], fs[4][0:16, ns], ps[6][0:16, :]),
                reads=(kcw[n], "ps6"), writes=(kcw[n],))
        pb.dma("sync", wsP[:, :], fs[4][0:16, :], reads=kcw,
               writes=(ctx.wkey("wsP", nb + 1),))
        pb.barrier()
        pb.op("vector", lambda e: e.tensor_copy(bs[0][0:16, :], fs[4][0:16, :]),
              reads=kcw, writes=FK("bs0"))
        for k in range(KT):
            pb.op("tensor", lambda e, k=k: e.transpose(
                pst[0:128, 0:16], bs[0][0:16, k * 128:(k + 1) * 128],
                identb[0:16, 0:16]), reads=FK("bs0") + ("identb",),
                writes=("pst",))
            pb.op("vector", lambda e, k=k: e.tensor_copy(
                wsT[:, k * 16:(k + 1) * 16], pst[0:128, 0:16]),
                reads=("pst",), writes=("wsT",))
        vvp = bs[1]
        kvvp = FK("bs1")
        pb.op("vector", lambda e: e.memset(vvp[:, :], 0.0), writes=kvvp)
        for which, off, brow in (("kk", D, 1), ("vv", 2 * D, 2)):
            pb.dma("gpsimd", bkv[0:1, :], ipb[brow:brow + 1, :],
                   writes=("bkv0",))
            for k in range(KT):
                pb.dma("scalar", wbuf[:, k * D:(k + 1) * D],
                       Wb_in[off:off + D, k * 128:(k + 1) * 128],
                       transpose=True,
                       reads=ctx.rkeys("bL1_ffn_in_proj_w"),
                       writes=(f"wbuf{k}",))
            for n in range(NT):
                ns = slice(n * NCHUNK, (n + 1) * NCHUNK)
                def kvgroup(e, off=off, n=n, brow=brow):
                    for k in range(KT):
                        e.matmul(ps[5][0:16, :], wsT[:, k * 16:(k + 1) * 16],
                                 wbuf[:, k * D + n * NCHUNK:
                                      k * D + (n + 1) * NCHUNK],
                                 start=(k == 0), stop=False)
                    return e.matmul(ps[5][0:16, :], onesb[0:1, 0:16],
                                    bkv[0:1, n * NCHUNK:(n + 1) * NCHUNK],
                                    start=False, stop=True)
                pb.op("tensor", kvgroup,
                      reads=("wsT", "onesb", "bkv0") +
                      tuple(f"wbuf{k}" for k in range(KT)),
                      writes=("ps5",))
                if which == "kk":
                    pb.op("vector", lambda e, n=n, ns=ns: e.tensor_copy(
                        fs[0][0:16, ns], ps[5][0:16, :]), reads=("ps5",),
                        writes=(FK("fs0")[n],))
                else:
                    for j4 in (0, 64):
                        pb.op("vector", lambda e, n=n, ns=ns, j4=j4:
                              e.tensor_copy(vvp[j4:j4 + 16, ns],
                                            ps[5][0:16, :]), reads=("ps5",),
                              writes=(kvvp[n],))
        pb.op("vector", lambda e: e.tensor_copy(bs[0][0:16, :], fs[0][0:16, :]),
              reads=FK("fs0"), writes=FK("bs0"))
        for k in range(KT):
            pb.op("tensor", lambda e, k=k: e.transpose(
                pst[0:128, 0:16], bs[0][0:16, k * 128:(k + 1) * 128],
                identb[0:16, 0:16]), reads=FK("bs0") + ("identb",),
                writes=("pst",))
            pb.op("vector", lambda e, k=k: e.tensor_copy(
                kkT[:, k * 16:(k + 1) * 16], pst[0:128, 0:16]),
                reads=("pst",), writes=("kkT",))
        pb.barrier()
        ctx.wcache = None
        emit_load_plane(ctx, xbuf, "xbuf", q1P, "q1P")
        pe = fs[0]
        kpe = FK("fs0")
        am = fs[2]
        kam = FK("fs2")
        pb2 = bs[0]
        kpb2 = FK("bs0")
        qst = ctx.sb("dT", [128, KT * 128], BF16)
        for lc in range(T // 128):
            for par in (0, 1):
                def agroup(e, lc=lc, par=par):
                    last_i = None
                    for h in range(par, NH, 2):
                        po = 64 * (h % 2)
                        kt = h // 2
                        last_i = e.matmul(
                            ps[h // 8][:, 16 * (h % 8):16 * (h % 8) + 16],
                            xbuf[po:po + 64,
                                 kt * T + lc * 128:kt * T + (lc + 1) * 128],
                            kkT[po:po + 64, kt * 16:(kt + 1) * 16],
                            start=True, stop=True)
                    return last_i
                pb.op("tensor", agroup,
                      reads=tuple(f"xbuf{k}" for k in range(KT)) + ("kkT",),
                      writes=("ps0", "ps1", "ps2", "ps3"))
            for b in range(4):
                pb.op("vector", lambda e, b=b: e.tensor_scalar(
                    pe[:, b * 128:(b + 1) * 128], ps[b][:, 0:128],
                    HD ** -0.5, None, ALU.mult),
                    reads=(f"ps{b}",), writes=(kpe[0],))
            pe3 = pe[:, 0:512].rearrange("p (h s) -> p h s", s=16)
            am1 = am[:, 0:32].rearrange("p (h o) -> p h o", o=1)
            am2 = am[:, 32:64].rearrange("p (h o) -> p h o", o=1)
            pb.op("vector", lambda e: e.reduce_max(am1, pe3, axis=AX.X),
                  reads=(kpe[0],), writes=(kam[0],))
            pb.op("vector", lambda e: e.tensor_tensor(
                pe3, pe3, am1.broadcast_to((128, 32, 16)), ALU.subtract),
                reads=(kpe[0], kam[0]), writes=(kpe[0],))
            pb.op("scalar", lambda e: e.activation(
                pe[:, 0:512], pe[:, 0:512], AF.Exp),
                reads=(kpe[0],), writes=(kpe[0],))
            pb.op("vector", lambda e: e.reduce_sum(am2, pe3, axis=AX.X),
                  reads=(kpe[0],), writes=(kam[0],))
            pb.op("vector", lambda e: e.reciprocal(am[:, 32:64], am[:, 32:64]),
                  reads=(kam[0],), writes=(kam[0],))
            pb.op("vector", lambda e: e.memset(pb2[:, :], 0.0),
                  writes=kpb2)
            pb.op("vector", lambda e: e.tensor_tensor(
                pb2[:, :].rearrange("p (h s) -> p h s", s=64)[:, :, 0:16],
                pe3, am2.broadcast_to((128, 32, 16)), ALU.mult),
                reads=(kpe[0], kam[0]) + kpb2, writes=kpb2)
            for b in range(16):
                pb.op("tensor", lambda e, b=b: e.transpose(
                    pst[:, 0:128], pb2[:, b * 128:(b + 1) * 128],
                    identb[:, :]), reads=(kpb2[b // 4], "identb"),
                    writes=("pst",))
                pb.op("vector", lambda e, b=b: e.tensor_copy(
                    qst[:, b * 128:(b + 1) * 128], pst[:, 0:128]),
                    reads=("pst",), writes=(f"qst_{b // 4}",))

            for par in (0, 1):
                def cgroup(e, lc=lc, par=par):
                    last_i = None
                    for h in range(par, NH, 2):
                        blk = h // 2
                        po = 64 * (h % 2)
                        last_i = e.matmul(
                            ps[h // 8][:, 64 * (h % 8):64 * (h % 8) + 64],
                            qst[po:po + 32, blk * 128:(blk + 1) * 128],
                            vvp[po:po + 32, 64 * h:64 * h + 64],
                            start=True, stop=True)
                    return last_i
                pb.op("tensor", cgroup,
                      reads=tuple(f"qst_{b}" for b in range(4)) + kvvp,
                      writes=("ps0", "ps1", "ps2", "ps3"))
            for b in range(4):
                pb.op("vector", lambda e, b=b: e.tensor_copy(
                    bs[0][:, b * 512:(b + 1) * 512], ps[b][:, :]),
                    reads=(f"ps{b}",), writes=(kpb2[b],))
            pb.dma("sync", ctxP[lc * 128:(lc + 1) * 128, :], bs[0][:, :],
                   reads=kpb2, writes=(ctx.wkey("ctxP", lc),))
        pb.barrier()
        ctx.wcache = None
        emit_load_fm(ctx, xbuf, "xbuf", ctxP, "ctxP")
        for k in range(KT):
            pb.dma("sync", wbuf[:, k * D:(k + 1) * D],
                   Wb_op[:, k * 128:(k + 1) * 128], transpose=True,
                   reads=ctx.rkeys("bL1_ffn_out_proj_w"),
                   writes=(f"wbuf{k}",))
        oprow = bs[1][0:1, :]
        pb.dma("gpsimd", oprow,
               ctx.dram["L1_ffn_out_proj_b"].rearrange("(o d) -> o d", o=1),
               writes=(kvvp[0],))
        for tc in range(T // 128):
            for n in range(NT):
                ns = slice(n * NCHUNK, (n + 1) * NCHUNK)
                def xgroup(e, tc=tc, n=n, ns=ns):
                    for k in range(KT):
                        e.matmul(ps[n][:, :],
                                 xbuf[:, k * T + tc * 128:k * T + (tc + 1) * 128],
                                 wbuf[:, k * D + n * NCHUNK:k * D + (n + 1) * NCHUNK],
                                 start=(k == 0), stop=False)
                    return e.matmul(ps[n][:, :], onesb,
                                    oprow[0:1, ns], start=False, stop=True)
                pb.op("tensor", xgroup,
                      reads=tuple(f"xbuf{k}" for k in range(KT)) +
                      tuple(f"wbuf{k}" for k in range(KT)) +
                      ("onesb", kvvp[0]), writes=(f"ps{n}",))
                pb.op("vector", lambda e, n=n, ns=ns: e.tensor_copy(
                    fs[3][:, ns], ps[n][:, :]), reads=(f"ps{n}",),
                    writes=(FK("fs3")[n],))
            pb.dma("sync", xbP[tc * 128:(tc + 1) * 128, :], fs[3][:, :],
                   reads=FK("fs3"), writes=(ctx.wkey("xbP", tc),))
        pb.barrier()
        pb.dma("sync", fs[4][0:1, :],
               ctx.dram["L1_ffn_norm_post_w"].rearrange("(o d) -> o d", o=1),
               writes=kcw)
        for n in range(NT):
            ns = slice(n * NCHUNK, (n + 1) * NCHUNK)
            pb.op("tensor", lambda e, ns=ns: e.matmul(
                ps[6][:, :], onesf[0:1, :], fs[4][0:1, ns],
                start=True, stop=True), reads=("onesf",) + kcw,
                writes=("ps6",))
            pb.op("vector", lambda e, n=n, ns=ns: e.tensor_copy(
                fs[5][:, ns], ps[6][:, :]), reads=("ps6",),
                writes=(kwpre[n],))
        for t in range(T // 128):
            i = t % 2
            kx = FK(f"fs{i}")
            r = slice(t * 128, (t + 1) * 128)
            pb.dma("sync", fs[i][:, :], ctx.dram[x_tm_f][r, :],
                   reads=ctx.rkeys(x_tm_f), writes=kx)
            pb.dma("scalar", fs[2][:, :], xbP[r, :], reads=ctx.rkeys("xbP"),
                   writes=FK("fs2"))
            pb.op("vector", lambda e, i=i: e.tensor_add(
                fs[i][:, :], fs[i][:, :], fs[2][:, :]),
                reads=kx + FK("fs2"), writes=kx)
            pb.op("scalar", lambda e, i=i: e.activation(
                fs[3][:, :], fs[i][:, :], AF.Square, accum_out=st8[:, 0:1]),
                reads=kx, writes=FK("fs3") + ("st8",))
            pb.op("vector", lambda e: e.tensor_scalar(
                st8[:, 1:2], st8[:, 0:1], 1.0 / D, EPS, ALU.mult, ALU.add),
                reads=("st8",), writes=("st8",))
            pb.op("scalar", lambda e: e.activation(
                st8[:, 2:3], st8[:, 1:2], AF.Sqrt), reads=("st8",),
                writes=("st8",))
            pb.op("vector", lambda e: e.reciprocal(st8[:, 3:4], st8[:, 2:3]),
                  reads=("st8",), writes=("st8",))
            pb.op("vector", lambda e, i=i: e.tensor_scalar(
                fs[i][:, :], fs[i][:, :], st8[:, 3:4], None, ALU.mult),
                reads=kx + ("st8",), writes=kx)
            pb.op("vector", lambda e, i=i: e.tensor_mul(
                fs[i][:, :], fs[i][:, :], fs[5][:, :]),
                reads=kx + kwpre, writes=kx)
            if last:
                pb.dma("sync", ffnO_tm[r, :], fs[i][:, :], reads=kx,
                       writes=(ctx.wkey("ffnO", t),))
            else:
                pb.dma("sync", xa[r, :], fs[i][:, :], reads=kx,
                       writes=(ctx.wkey("l1xa", t),))
                pb.op("vector", lambda e, i=i: e.tensor_copy(
                    bs[0][:, :], fs[i][:, :]), reads=kx, writes=FK("bs0"))
                pb.dma("scalar", xab[r, :], bs[0][:, :], reads=FK("bs0"),
                       writes=(ctx.wkey("l1xab", t),))
        pb.barrier()
        x_tm_f, x_tm_b = "l1xa", "l1xab"


def emit_l1(ctx, x_tm, delta0_fm, out_dram):
    pb = ctx.pb
    x1_tm = ctx.scratch("x1_tm", [T, D], F32)
    xn1_tm = ctx.scratch("xn1_tm", [T, D], BF16)
    xn1f_tm = ctx.scratch("xn1f_tm", [T, D], F32)
    emit_ln(ctx, x_tm, xn1_tm, delta_tm=delta0_fm, x_out_tm=x1_tm,
            x_name="x", xn_name="xn1_tm", delta_name="delta",
            xo_name="x1_tm", xnf_tm=xn1f_tm, xnf_name="xn1f_tm",
            delta_fm_bf=True)
    pb.barrier()
    attnO_tm = ctx.scratch("attnO_tm", [T, D], F32)
    emit_l1_attn(ctx, attnO_tm)
    pb.barrier()
    ffnO_tm = ctx.scratch("ffnO_tm", [T, D], F32)
    emit_l1_ffn(ctx, ffnO_tm)
    pb.barrier()
    fs = [ctx.sb(f"fs{i}", [128, T], F32) for i in range(3)]
    for t in range(T // 128):
        r = slice(t * 128, (t + 1) * 128)
        i = t % 2
        kx = FK(f"fs{i}")
        pb.dma("sync", fs[i][:, :], x1_tm[r, :], reads=ctx.rkeys("x1_tm"),
               writes=kx)
        pb.dma("scalar", fs[2][:, :], attnO_tm[r, :], reads=ctx.rkeys("attnO"),
               writes=FK("fs2"))
        pb.op("vector", lambda e, i=i: e.tensor_add(
            fs[i][:, :], fs[i][:, :], fs[2][:, :]), reads=kx + FK("fs2"),
            writes=kx)
        pb.dma("scalar", fs[2][:, :], ffnO_tm[r, :], reads=ctx.rkeys("ffnO"),
               writes=FK("fs2"))
        pb.op("vector", lambda e, i=i: e.tensor_add(
            fs[i][:, :], fs[i][:, :], fs[2][:, :]), reads=kx + FK("fs2"),
            writes=kx)
        pb.dma("sync", out_dram[r, :], fs[i][:, :], reads=kx,
               writes=(ctx.wkey("out", t),))


def host_constants():
    C, NS = dft_constants()
    P1 = np.zeros((128, 128), np.float32)
    for r in range(1, 128):
        P1[128 - r, r] = 1.0
    P2 = np.zeros((128, 128), np.float32)
    P2[0, 0] = 1.0
    ident = np.eye(128, dtype=np.float32)
    E = np.zeros((8, 128), np.float32)
    for e in range(8):
        E[e, 16 * e:16 * e + 16] = 1.0
    cm = np.zeros((128, 128), np.float32)
    cm[np.triu_indices(128, 1)] = -1e30
    return dict(Cdft=C, NSdft=NS, ident=ident, Emoe=E, causal=cm, P1m=P1)


def build_program():
    ctx = Ctx()
    x = ctx.din("x", [T, D])
    for wn, shp in (
        ("L0_attn_Wq", [D, D]), ("L0_attn_Wk", [D, D]), ("L0_attn_Wv", [D, D]),
        ("L0_attn_Ws", [D, D]), ("L0_attn_Wt", [D, D]),
        ("L0_attn_W1", [2 * D, 4 * D]), ("L0_attn_b1", [4 * D]),
        ("L0_attn_W2", [4 * D, 2 * D]), ("L0_attn_b2", [2 * D]),
        ("L0_ffn_router_w", [8, D]), ("L0_ffn_down_w", [8, 16, D]),
        ("L0_ffn_up_w", [8, D, 16]),
        ("L1_attn_Wq", [D, D]), ("L1_attn_bq", [D]),
        ("L1_attn_Wk", [D, D]), ("L1_attn_bk", [D]),
        ("L1_attn_Wv", [D, D]), ("L1_attn_bv", [D]),
        ("L1_ffn_workspace", [16, D]), ("L1_ffn_compete_w", [16, D]),
        ("L1_ffn_compete_b", [16]), ("L1_ffn_write_w", [D, D]),
        ("L1_ffn_write_b", [D]), ("L1_ffn_in_proj_w", [3 * D, D]),
        ("L1_ffn_in_proj_b", [3 * D]), ("L1_ffn_out_proj_w", [D, D]),
        ("L1_ffn_out_proj_b", [D]),
        ("L1_ffn_norm_pre_w", [D]), ("L1_ffn_norm_post_w", [D]),
        ("Cdft", [D, D]), ("NSdft", [D, D]), ("ident", [128, 128]),
        ("Emoe", [8, 128]), ("causal", [128, 128]),
        ("P1m", [128, 128]),
    ):
        ctx.din(wn, shp)
    out = ctx.dout("out", [T, D])
    delta_fm = ctx.scratch("delta_fm", [D, T], BF16)
    emit_l0(ctx, x, delta_fm)
    ctx.pb.barrier()
    emit_l1(ctx, x, delta_fm, out)
    ctx.pb.barrier()
    nc = ctx.finish()
    return ctx, nc


def kernel(**inputs):
    xs = np.ascontiguousarray(inputs["L0_attn_x"])
    weights = {k: np.ascontiguousarray(v) for k, v in inputs.items()
               if k != "L0_attn_x" and not k.startswith("L1_ffn_ig")}
    consts = host_constants()
    ctx, nc = build_program()
    in_maps = []
    for b in range(8):
        m = dict(x=xs[b], **weights, **consts)
        if hasattr(ctx, "const_values"):
            m["cst"] = const_input(ctx)
        in_maps.append(m)
    res = run_bass_kernel_spmd(nc, in_maps, core_ids=list(range(8)))
    return np.stack([r["out"] for r in res.results], axis=0)
